# version 05
# baseline (speedup 1.0000x reference)
import contextlib
import numpy as np

import concourse.bass as bass
import concourse.mybir as mybir
from concourse.bass_utils import run_bass_kernel_spmd

F32 = mybir.dt.float32
BF16 = mybir.dt.bfloat16
AF = mybir.ActivationFunctionType
ALU = mybir.AluOpType
AXX = mybir.AxisListType.X

T = 16384
D = 1024
E = 2048
N = 64
KC = 4
EQ = 512
NEC = 4
TCA = 1024
NTCA = T // TCA
TCB = 1024
NTCB = T // TCB
TSL = T // 4
EPS = 1e-5

DEBUG_TAPS = False
PHASES = "ABC"


class Ctr:
    def __init__(self, handle):
        self.h = handle
        self.n = 0

    def inc(self, k=1):
        self.n += k
        return self.n


class Plan:
    def __init__(self):
        self.q = {k: [] for k in ("sync", "scalar", "vector", "tensor", "gpsimd")}

    def wait(self, eng, ctr, tgt=None):
        t = ctr.n if tgt is None else tgt
        if t > 0:
            self.q[eng].append(lambda e, h=ctr.h, t=t: e.wait_ge(h, t))

    def dma(self, eng, ctr, out_ap, in_ap, accum=False, slow_ok=False):
        tgt = ctr.inc(16)
        if accum:
            self.q[eng].append(lambda e, o=out_ap, i=in_ap, h=ctr.h: e.dma_start(
                out=o, in_=i, accum_op=ALU.add).then_inc(h, 16))
        elif slow_ok:
            self.q[eng].append(lambda e, o=out_ap, i=in_ap, h=ctr.h: e.dma_start(
                out=o, in_=i, allow_slow_non_contiguous=True).then_inc(h, 16))
        else:
            self.q[eng].append(lambda e, o=out_ap, i=in_ap, h=ctr.h: e.dma_start(
                out=o, in_=i).then_inc(h, 16))
        return tgt

    def op(self, eng, fn, ctr=None):
        if ctr is None:
            self.q[eng].append(fn)
            return None
        tgt = ctr.inc(1)
        self.q[eng].append(lambda e, f=fn, h=ctr.h: f(e).then_inc(h, 1))
        return tgt


def build_nc():
    nc = bass.Bass("TRN2", target_bir_lowering=False, debug=False)

    def din(name, shape, dt=F32):
        return nc.dram_tensor(name, shape, dt, kind="ExternalInput").ap()

    def dint(name, shape, dt=F32, notap=False):
        kind = "ExternalOutput" if (DEBUG_TAPS and not notap) else "Internal"
        return nc.dram_tensor(name, shape, dt, kind=kind).ap()

    x_in = din("x", [T, D])
    x_sl = din("x_sl", [TSL, D])
    w_inT = [din(f"w_inT{d}", [D, E + EQ]) for d in range(2)]
    conv_wi = [din(f"conv_w{d}", [16, 128, KC]) for d in range(2)]
    conv_bi = [din(f"conv_b{d}", [16, 128]) for d in range(2)]
    w_xpTi = [din(f"w_xpT{d}", [E, 192]) for d in range(2)]
    w_dtTi = [din(f"w_dtT{d}", [N, EQ]) for d in range(2)]
    dt_b4 = [din(f"dt_b{d}", [NEC, 128]) for d in range(2)]
    a_log4 = [din(f"a_log{d}", [NEC, 128, N]) for d in range(2)]
    d_col4 = [din(f"d_col{d}", [NEC, 128]) for d in range(2)]
    w_outTi = [din(f"w_outT{d}", [EQ, D]) for d in range(2)]
    ln1_w8 = din("ln1_w8", [8, 128])
    ln1_b8 = din("ln1_b8", [8, 128])
    ln1_w_row = din("ln1_w_row", [1, D])
    ln1_b_row = din("ln1_b_row", [1, D])
    ln2_w_row = din("ln2_w_row", [1, D])
    ln2_b_row = din("ln2_b_row", [1, D])
    w_rail1 = din("w_rail1", [32, 128, 128])
    w_rail2 = din("w_rail2", [128, 128, 128])
    cprojTi = din("cprojT", [T, D])
    ident_in = din("ident", [128, 128])

    out_sl = nc.dram_tensor("out_sl", [TSL, D], F32, kind="ExternalOutput").ap()

    xnfm_sp = dint("xnfm_sp", [D, T], BF16)
    xnpl_sp = dint("xnpl_sp", [4, D, TSL], BF16, notap=True)
    xr2_sp = dint("xr2_sp", [4, TSL, D], F32, notap=True)
    xnpl_rx = nc.dram_tensor("xnpl_rx", [D, TSL], BF16, kind="Internal").ap()
    xr2_rx = nc.dram_tensor("xr2_rx", [TSL, D], F32, kind="Internal").ap()
    dt_sp = [dint(f"dt_sp{d}", [EQ, T], F32) for d in range(2)]
    w_sp = [dint(f"w_sp{d}", [EQ, T], BF16) for d in range(2)]
    xc_sp = [dint(f"xc_sp{d}", [EQ, T], BF16) for d in range(2)]
    sz_sp = [dint(f"sz_sp{d}", [EQ, T], BF16) for d in range(2)]
    b_sp = [dint(f"b_sp{d}", [N, T], BF16) for d in range(2)]
    c_sp = [dint(f"c_sp{d}", [N, T], BF16) for d in range(2)]
    xbuf = dint("xbuf", [T, D], F32, notap=True)
    mbuf = nc.dram_tensor("mbuf", [TSL, D], F32, kind="Internal").ap()
    cprojbf = nc.dram_tensor("cprojbf", [T, D], BF16, kind="Internal").ap()
    bdummy = nc.dram_tensor("bdummy", [1, 4], F32, kind="Internal").ap()
    xp_tap = dint("xp_tap", [2, EQ, T], F32) if DEBUG_TAPS else None

    pl = Plan()
    G2 = [[0, 1, 2, 3], [4, 5, 6, 7]]

    with contextlib.ExitStack() as ctx:
        S = lambda name: Ctr(ctx.enter_context(nc.semaphore(name)))
        sA = S("sA")
        sM = S("sM")
        sX = [S("sX0"), S("sX1")]
        sNF = S("sNF")
        sDW = [S("sDW0"), S("sDW1")]
        sCR = [S("sCR0"), S("sCR1")]
        sG = [S("sG0"), S("sG1")]
        sCP = [S("sCP0"), S("sCP1"), S("sCP2"), S("sCP3")]
        sXS = S("sXS")
        sF = [S("sF0"), S("sF1")]
        sWB = S("sWB")
        pXR = S("pXR")
        pNF = S("pNF")
        pSZ = S("pSZ")
        pXC = S("pXC")
        pBC = S("pBC")
        pDT = S("pDT")
        pW = S("pW")
        pCPK = S("pCPK")
        pOUT = [S("pOUT0"), S("pOUT1")]
        pTAP = S("pTAP")
        gBR = [S("gBR0"), S("gBR1")]
        gP = [S("gP0"), S("gP1")]
        va = S("va")
        aa = S("aa")
        ta = S("ta")
        gg = S("gg")
        c1 = S("c1")
        c2 = S("c2")
        c3 = S("c3")
        bs = S("bs")

        SYNCQ = [sA, sM] + sX + [sNF] + sDW + sCR + sG + sCP + [sXS] + sF + [sWB]
        SCALQ = [pXR, pNF, pSZ, pXC, pBC, pDT, pW, pCPK, pTAP] + pOUT
        GPQ = gBR + gP

        sb = lambda name, shape, dt: ctx.enter_context(nc.sbuf_tensor(name, shape, dt))

        psA = ctx.enter_context(nc.psum_tensor("psA", [128, 2048], F32))
        psB = ctx.enter_context(nc.psum_tensor("psB", [128, 2048], F32))
        ident_bf = sb("ident_bf", [128, 128], BF16)
        w_xpT_bf = [sb(f"w_xpT_bf{d}", [128, 16 * 192], BF16) for d in range(2)]
        w_dtT_bf = [sb(f"w_dtT_bf{d}", [64, EQ], BF16) for d in range(2)]
        w_outT_bf = [sb(f"w_outT_bf{d}", [128, NEC * D], BF16) for d in range(2)]
        a_dev = [sb(f"a_dev{d}", [128, NEC * N], F32) for d in range(2)]
        convw_c = [sb(f"convw_c{d}", [128, 16 * KC], F32) for d in range(2)]
        convb_c = [sb(f"convb_c{d}", [128, 16], F32) for d in range(2)]
        dtb_c = [sb(f"dtb_c{d}", [128, NEC], F32) for d in range(2)]
        d_c = [sb(f"d_c{d}", [128, NEC], F32) for d in range(2)]
        w1_c = sb("w1_c", [128, 8], F32)
        b1_cbf = sb("b1_cbf", [128, 8], BF16)
        wb1_c = [sb(f"wb1_c{d}", [128, 20], F32) for d in range(2)]
        carry = [[sb(f"carry{d}_{ec}", [128, N], F32) for ec in range(NEC)]
                 for d in range(2)]
        bscr = sb("bscr", [1, 8], F32)

        bank_ap = [psA[:, i * 512:(i + 1) * 512] for i in range(4)] + \
                  [psB[:, i * 512:(i + 1) * 512] for i in range(4)]
        bank_evac = [0] * 8
        bank_stop = [0] * 8
        bank_rr = [0]

        def grab_bank():
            i = bank_rr[0] % 8
            bank_rr[0] += 1
            pl.wait("tensor", aa, bank_evac[i])
            return i

        def barrier():
            for c in SYNCQ:
                pl.wait("sync", c)
            pl.dma("sync", bs, bscr[0:1, 4:8], bdummy[:])
            for c in SCALQ:
                pl.wait("scalar", c)
            pl.op("scalar", lambda e: e.copy(out=bscr[:, 0:1], in_=bscr[:, 0:1]), aa)
            for c in GPQ:
                pl.wait("gpsimd", c)
            pl.op("gpsimd", lambda e: e.tensor_copy(out=bscr[:, 2:3], in_=bscr[:, 2:3]), gg)
            pl.op("vector", lambda e: e.tensor_copy(out=bscr[:, 1:2], in_=bscr[:, 1:2]), va)
            marks = [(aa, aa.n), (va, va.n), (ta, ta.n), (gg, gg.n), (bs, bs.n)]
            for eng in ("sync", "scalar", "vector", "tensor", "gpsimd"):
                for (c, t) in marks:
                    pl.wait(eng, c, t)

        with contextlib.ExitStack() as ctx0:
            stage0 = ctx0.enter_context(nc.sbuf_tensor("stage0", [128, 2560], F32))
            t1 = pl.dma("sync", sA, stage0[:, 0:128], ident_in[:])
            pl.wait("vector", sA, t1)
            pl.op("vector", lambda e: e.tensor_copy(out=ident_bf[:], in_=stage0[:, 0:128]), va)
            for d in range(2):
                for mt in range(16):
                    pl.dma("sync", sM, convw_c[d][:, mt * KC:(mt + 1) * KC], conv_wi[d][mt])
                pl.dma("sync", sM, convb_c[d][:], conv_bi[d].transpose([1, 0]), slow_ok=True)
                pl.dma("sync", sM, dtb_c[d][:], dt_b4[d].transpose([1, 0]), slow_ok=True)
                pl.dma("sync", sM, d_c[d][:], d_col4[d].transpose([1, 0]), slow_ok=True)
                pl.wait("sync", aa)
                pl.wait("sync", va)
                for ec in range(NEC):
                    t1 = pl.dma("sync", sA, stage0[:, 1024 + ec * N:1024 + (ec + 1) * N],
                                a_log4[d][ec])
                pl.wait("scalar", sA, t1)
                pl.op("scalar", lambda e, d=d: e.activation(
                    out=a_dev[d][:], in_=stage0[:, 1024:1024 + NEC * N], func=AF.Exp), aa)
                pl.wait("vector", aa)
                pl.op("vector", lambda e, d=d: e.tensor_scalar(
                    out=a_dev[d][:], in0=a_dev[d][:], scalar1=-1.0, scalar2=None,
                    op0=ALU.mult), va)
                for k in range(16):
                    pl.wait("sync", va)
                    t1 = pl.dma("sync", sA, stage0[:, 0:192],
                                w_xpTi[d][k * 128:(k + 1) * 128, :])
                    pl.wait("vector", sA, t1)
                    pl.op("vector", lambda e, d=d, k=k: e.tensor_copy(
                        out=w_xpT_bf[d][:, k * 192:(k + 1) * 192], in_=stage0[:, 0:192]), va)
                pl.wait("sync", va)
                t1 = pl.dma("sync", sA, stage0[0:64, 0:EQ], w_dtTi[d][:])
                pl.wait("vector", sA, t1)
                pl.op("vector", lambda e, d=d: e.tensor_copy(
                    out=w_dtT_bf[d][:], in_=stage0[0:64, 0:EQ]), va)
                for ec in range(NEC):
                    pl.wait("sync", va)
                    t1 = pl.dma("sync", sA, stage0[:, 0:D],
                                w_outTi[d][ec * 128:(ec + 1) * 128, :])
                    pl.wait("vector", sA, t1)
                    pl.op("vector", lambda e, d=d, ec=ec: e.tensor_copy(
                        out=w_outT_bf[d][:, ec * D:(ec + 1) * D], in_=stage0[:, 0:D]), va)
            pl.dma("sync", sM, w1_c[:], ln1_w8.transpose([1, 0]), slow_ok=True)
            pl.wait("sync", va)
            t1 = pl.dma("sync", sA, stage0[:, 0:8], ln1_b8.transpose([1, 0]), slow_ok=True)
            pl.wait("vector", sA, t1)
            pl.op("vector", lambda e: e.tensor_copy(out=b1_cbf[:], in_=stage0[:, 0:8]), va)
            for d in range(2):
                for ec in range(NEC):
                    pl.op("vector", lambda e, d=d, ec=ec: e.memset(carry[d][ec][:], 0.0), va)
            barrier()

        with contextlib.ExitStack() as ctxA:
            sbA = lambda nm, shp, dt: ctxA.enter_context(nc.sbuf_tensor(nm, shp, dt))
            stage = sbA("stageA", [128, 2560], F32)
            w_in_buf = sbA("w_in_buf", [128, 8 * 2560], BF16)
            xt = [sbA(f"xt{i}", [128, D], F32) for i in range(2)]
            ln_nm = ("s1", "mu", "s2", "ex2", "mu2", "var", "av", "ra", "v2",
                     "bv", "la", "lb", "lab", "rr", "rt")
            ln = {nm: sbA(f"ln_{nm}", [128, 4], F32) for nm in ln_nm}
            xr2_bf = [sbA(f"xr2bf{i}", [128, D], BF16) for i in range(2)]
            xnp_bf = [sbA(f"xnpbf{i}", [128, D], BF16) for i in range(2)]
            xr2_f = sbA("xr2f", [128, D], F32)
            xnfm = [sbA(f"xnfm{dc}", [128, TCA], BF16) for dc in range(8)]
            xnfm_p = [sbA(f"xnfmp{dc}", [128, TCA], BF16) for dc in range(8)]
            xp_buf = [sbA(f"xp{i}", [128, TCA + 3], F32) for i in range(3)]
            cacc = [sbA(f"cacc{i}", [128, TCA], F32) for i in range(2)]
            stash = [sbA(f"stash{mt}", [128, 3], F32) for mt in range(16)]
            xc_all = [sbA(f"xc{mt}", [128, TCA], BF16) for mt in range(16)]
            dtlo_bf = sbA("dtlo_bf", [64, TCA], BF16)
            bc_bf = sbA("bc_bf", [64, TCA], BF16)
            dt_t = [sbA(f"dt_t{i}", [128, TCA], F32) for i in range(2)]
            w_t = sbA("w_tile", [128, TCA], BF16)
            sz_t = sbA("sz_tile", [128, TCA], BF16)
            sqd = stage[:, 0:1024]
            e1_t = stage[:, 1024:1536]

            lg = {"xt_v": [0, 0], "xt_a": [0, 0], "slot_a": [0] * 4, "slot_v": [0] * 4,
                  "xr2f_sp": 0, "xbf_ta": [0, 0], "xnfm_sp": 0, "xnfm_ta": 0,
                  "xp_v": [0, 0, 0], "xp_tap_sp": 0, "cacc_a": 0,
                  "xc_sp": [0] * 4, "xc_ta": 0, "dtlo_ta": 0, "bc_sp": 0,
                  "dt_sp": [0, 0], "dt_v": [0, 0], "w_sp": 0, "sz_sp": 0}

            if "A" in PHASES:
                for d in range(2):
                    pl.wait("sync", aa)
                    pl.wait("sync", va)
                    pl.wait("sync", ta)
                    pl.wait("vector", ta)
                    pl.wait("vector", aa)
                    for k in range(8):
                        pl.wait("sync", va)
                        t1 = pl.dma("sync", sA, stage[:, 0:2560],
                                    w_inT[d][k * 128:(k + 1) * 128, :])
                        pl.wait("vector", sA, t1)
                        pl.op("vector", lambda e, k=k: e.tensor_copy(
                            out=w_in_buf[:, k * 2560:(k + 1) * 2560],
                            in_=stage[:, 0:2560]), va)
                    pl.wait("tensor", va)
                    for mt in range(20):
                        bi = grab_bank()
                        for k in range(8):
                            pl.op("tensor", lambda e, mt=mt, k=k, bi=bi: e.matmul(
                                bank_ap[bi][:, 0:1],
                                w_in_buf[:, k * 2560 + mt * 128:k * 2560 + (mt + 1) * 128],
                                b1_cbf[:, k:k + 1], start=(k == 0), stop=(k == 7)),
                                ta if k == 7 else None)
                        bank_stop[bi] = ta.n
                        pl.wait("scalar", ta, bank_stop[bi])
                        pl.op("scalar", lambda e, mt=mt, bi=bi, d=d: e.copy(
                            out=wb1_c[d][:, mt:mt + 1], in_=bank_ap[bi][:, 0:1]), aa)
                        bank_evac[bi] = aa.n
                    wb1_ta = ta.n
                    for k in range(8):
                        pl.wait("sync", va)
                        t1 = pl.dma("sync", sA, stage[:, 0:2560],
                                    w_inT[d][k * 128:(k + 1) * 128, :])
                        pl.wait("vector", sA, t1)
                        if k == 0:
                            pl.wait("vector", ta, wb1_ta)
                        pl.op("vector", lambda e, k=k: e.tensor_scalar(
                            out=w_in_buf[:, k * 2560:(k + 1) * 2560], in0=stage[:, 0:2560],
                            scalar1=w1_c[:, k:k + 1], scalar2=None, op0=ALU.mult), va)
                    w_in_v = va.n
                    pl.wait("tensor", va, w_in_v)
                    for mt in range(16):
                        pl.op("vector", lambda e, mt=mt: e.memset(stash[mt][:], 0.0), va)

                    for tci in range(NTCA):
                        tc = tci if d == 0 else NTCA - 1 - tci
                        t0 = tc * TCA
                        qq, lt = t0 // TSL, t0 % TSL
                        a_fm = aa.n
                        if d == 0:
                            for j in range(8):
                                jj = j % 2
                                slt = j % 4
                                pl.wait("sync", va, lg["xt_v"][jj])
                                pl.wait("sync", aa, lg["xt_a"][jj])
                                tgt_x = pl.dma("sync", sX[jj], xt[jj][:],
                                               x_in[t0 + j * 128:t0 + (j + 1) * 128, :])
                                pl.wait("vector", sX[jj], tgt_x)
                                pl.wait("vector", aa, lg["slot_a"][slt])
                                pl.op("vector", lambda e, jj=jj, slt=slt: e.reduce_sum(
                                    out=ln["s1"][:, slt:slt + 1], in_=xt[jj][:], axis=AXX), va)
                                pl.wait("vector", va)
                                pl.op("vector", lambda e, slt=slt: e.tensor_scalar(
                                    out=ln["mu"][:, slt:slt + 1], in0=ln["s1"][:, slt:slt + 1],
                                    scalar1=1.0 / D, scalar2=None, op0=ALU.mult), va)
                                pl.wait("scalar", sX[jj], tgt_x)
                                pl.wait("scalar", va, lg["slot_v"][slt])
                                pl.op("scalar", lambda e, jj=jj, slt=slt: e.activation(
                                    out=sqd, in_=xt[jj][:], func=AF.Square,
                                    accum_out=ln["s2"][:, slt:slt + 1]), aa)
                                a_s2 = aa.n
                                lg["xt_a"][jj] = a_s2
                                pl.wait("vector", aa, a_s2)
                                pl.op("vector", lambda e, slt=slt: e.tensor_scalar(
                                    out=ln["ex2"][:, slt:slt + 1], in0=ln["s2"][:, slt:slt + 1],
                                    scalar1=1.0 / D, scalar2=None, op0=ALU.mult), None)
                                pl.op("vector", lambda e, slt=slt: e.tensor_mul(
                                    out=ln["mu2"][:, slt:slt + 1], in0=ln["mu"][:, slt:slt + 1],
                                    in1=ln["mu"][:, slt:slt + 1]), va)
                                pl.wait("vector", va)
                                pl.op("vector", lambda e, slt=slt: e.tensor_sub(
                                    out=ln["var"][:, slt:slt + 1], in0=ln["ex2"][:, slt:slt + 1],
                                    in1=ln["mu2"][:, slt:slt + 1]), va)
                                pl.wait("vector", va)
                                pl.op("vector", lambda e, slt=slt: e.tensor_scalar(
                                    out=ln["av"][:, slt:slt + 1], in0=ln["var"][:, slt:slt + 1],
                                    scalar1=EPS, scalar2=None, op0=ALU.add), va)
                                pl.wait("vector", va)
                                pl.op("vector", lambda e, slt=slt: e.reciprocal(
                                    out=ln["ra"][:, slt:slt + 1], in_=ln["av"][:, slt:slt + 1]), va)
                                pl.wait("vector", va)
                                pl.op("vector", lambda e, slt=slt: e.tensor_mul(
                                    out=ln["v2"][:, slt:slt + 1], in0=ln["var"][:, slt:slt + 1],
                                    in1=ln["ra"][:, slt:slt + 1]), va)
                                pl.wait("vector", va)
                                pl.op("vector", lambda e, slt=slt: e.tensor_scalar(
                                    out=ln["bv"][:, slt:slt + 1], in0=ln["v2"][:, slt:slt + 1],
                                    scalar1=EPS, scalar2=None, op0=ALU.add), va)
                                v_bv = va.n
                                pl.wait("scalar", va, v_bv)
                                pl.op("scalar", lambda e, slt=slt: e.activation(
                                    out=ln["la"][:, slt:slt + 1], in_=ln["av"][:, slt:slt + 1],
                                    func=AF.Ln), aa)
                                pl.op("scalar", lambda e, slt=slt: e.activation(
                                    out=ln["lb"][:, slt:slt + 1], in_=ln["bv"][:, slt:slt + 1],
                                    func=AF.Ln), aa)
                                pl.wait("scalar", aa)
                                pl.op("scalar", lambda e, slt=slt: e.activation(
                                    out=ln["rr"][:, slt:slt + 1], in_=ln["la"][:, slt:slt + 1],
                                    func=AF.Exp, scale=-0.5), aa)
                                a_lb = aa.n
                                pl.wait("vector", aa, a_lb)
                                pl.op("vector", lambda e, slt=slt: e.tensor_add(
                                    out=ln["lab"][:, slt:slt + 1], in0=ln["la"][:, slt:slt + 1],
                                    in1=ln["lb"][:, slt:slt + 1]), va)
                                v_lab = va.n
                                pl.wait("scalar", va, v_lab)
                                pl.op("scalar", lambda e, slt=slt: e.activation(
                                    out=ln["rt"][:, slt:slt + 1], in_=ln["lab"][:, slt:slt + 1],
                                    func=AF.Exp, scale=-0.5), aa)
                                a_rt = aa.n
                                lg["slot_a"][slt] = a_rt
                                pl.wait("vector", aa, a_rt)
                                pl.wait("vector", ta, lg["xbf_ta"][jj])
                                pl.wait("vector", pXR, lg["xr2f_sp"])
                                pl.op("vector", lambda e, jj=jj, slt=slt: e.tensor_scalar(
                                    out=xr2_bf[jj][:], in0=xt[jj][:],
                                    scalar1=ln["mu"][:, slt:slt + 1],
                                    scalar2=ln["rt"][:, slt:slt + 1],
                                    op0=ALU.subtract, op1=ALU.mult), None)
                                pl.op("vector", lambda e, jj=jj, slt=slt: e.tensor_scalar(
                                    out=xnp_bf[jj][:], in0=xt[jj][:],
                                    scalar1=ln["mu"][:, slt:slt + 1],
                                    scalar2=ln["rr"][:, slt:slt + 1],
                                    op0=ALU.subtract, op1=ALU.mult), None)
                                pl.op("vector", lambda e, jj=jj, slt=slt: e.tensor_scalar(
                                    out=xr2_f[:], in0=xt[jj][:],
                                    scalar1=ln["mu"][:, slt:slt + 1],
                                    scalar2=ln["rt"][:, slt:slt + 1],
                                    op0=ALU.subtract, op1=ALU.mult), va)
                                v_xn = va.n
                                lg["slot_v"][slt] = v_xn
                                lg["xt_v"][jj] = v_xn
                                pl.wait("scalar", va, v_xn)
                                lg["xr2f_sp"] = pl.dma(
                                    "scalar", pXR,
                                    xr2_sp[qq, lt + j * 128:lt + (j + 1) * 128, :], xr2_f[:])
                                pl.wait("tensor", va, v_xn)
                                for dc in range(8):
                                    bi = grab_bank()
                                    pl.op("tensor", lambda e, jj=jj, dc=dc, bi=bi: e.transpose(
                                        out=bank_ap[bi].bitcast(BF16)[:, 0:128],
                                        in_=xr2_bf[jj][:, dc * 128:(dc + 1) * 128],
                                        identity=ident_bf[:]), None)
                                    pl.op("tensor", lambda e, jj=jj, dc=dc, bi=bi: e.transpose(
                                        out=bank_ap[bi].bitcast(BF16)[:, 128:256],
                                        in_=xnp_bf[jj][:, dc * 128:(dc + 1) * 128],
                                        identity=ident_bf[:]), ta)
                                    bank_stop[bi] = ta.n
                                    if dc == 7:
                                        lg["xbf_ta"][jj] = ta.n
                                    pl.wait("scalar", ta, bank_stop[bi])
                                    if j == 0 and dc == 0:
                                        pl.wait("scalar", ta, lg["xnfm_ta"])
                                        pl.wait("scalar", pNF, lg["xnfm_sp"])
                                    pl.op("scalar", lambda e, dc=dc, j=j, bi=bi: e.copy(
                                        out=xnfm[dc][:, j * 128:(j + 1) * 128],
                                        in_=bank_ap[bi].bitcast(BF16)[:, 0:128]), None)
                                    pl.op("scalar", lambda e, dc=dc, j=j, bi=bi: e.copy(
                                        out=xnfm_p[dc][:, j * 128:(j + 1) * 128],
                                        in_=bank_ap[bi].bitcast(BF16)[:, 128:256]), aa)
                                    bank_evac[bi] = aa.n
                            a_fm = aa.n
                            pl.wait("scalar", aa, a_fm)
                            for dc in range(8):
                                pl.dma("scalar", pNF, xnfm_sp[dc * 128:(dc + 1) * 128,
                                                              t0:t0 + TCA], xnfm[dc][:])
                                s_t = pl.dma("scalar", pNF,
                                             xnpl_sp[qq, dc * 128:(dc + 1) * 128, lt:lt + TCA],
                                             xnfm_p[dc][:])
                            lg["xnfm_sp"] = s_t
                        else:
                            if tci == 0:
                                pl.wait("sync", pNF)
                            pl.wait("sync", ta, lg["xnfm_ta"])
                            nf_t = 0
                            for dc in range(8):
                                nf_t = pl.dma("sync", sNF, xnfm[dc][:],
                                              xnfm_sp[dc * 128:(dc + 1) * 128, t0:t0 + TCA])
                            pl.wait("tensor", sNF, nf_t)

                        if d == 0:
                            pl.wait("tensor", aa, a_fm)
                        for mt in range(20):
                            for f in range(2):
                                bi = grab_bank()
                                for k in range(8):
                                    pl.op("tensor", lambda e, mt=mt, f=f, k=k, bi=bi: e.matmul(
                                        bank_ap[bi],
                                        w_in_buf[:, k * 2560 + mt * 128:k * 2560 + (mt + 1) * 128],
                                        xnfm[k][:, f * 512:(f + 1) * 512],
                                        start=(k == 0), stop=(k == 7)),
                                        ta if k == 7 else None)
                                bank_stop[bi] = ta.n
                                if mt == 19 and f == 1:
                                    lg["xnfm_ta"] = ta.n
                                pl.wait("scalar", ta, bank_stop[bi])
                                if mt < 16:
                                    off = (3 if d == 0 else 0) + f * 512
                                    xb = mt % 3
                                    if f == 0:
                                        pl.wait("scalar", va, lg["xp_v"][xb])
                                        pl.wait("scalar", pTAP, lg["xp_tap_sp"])
                                    pl.op("scalar", lambda e, bi=bi, mt=mt, off=off, xb=xb, d=d: e.add(
                                        out=xp_buf[xb][:, off:off + 512], in_=bank_ap[bi],
                                        add=wb1_c[d][:, mt:mt + 1]), aa)
                                else:
                                    if f == 0:
                                        pl.wait("scalar", pSZ, lg["sz_sp"])
                                    pl.op("scalar", lambda e, bi=bi, mt=mt, f=f, d=d: e.activation(
                                        out=sz_t[:, f * 512:(f + 1) * 512], in_=bank_ap[bi],
                                        func=AF.Silu, bias=wb1_c[d][:, mt:mt + 1]), aa)
                                bank_evac[bi] = aa.n
                            if mt >= 16:
                                a_now = aa.n
                                pl.wait("scalar", aa, a_now)
                                lg["sz_sp"] = pl.dma(
                                    "scalar", pSZ,
                                    sz_sp[d][(mt - 16) * 128:(mt - 15) * 128, t0:t0 + TCA],
                                    sz_t[:])
                            else:
                                xb = mt % 3
                                a_now = aa.n
                                if DEBUG_TAPS and mt < 4:
                                    pl.wait("scalar", aa, a_now)
                                    off3 = 3 if d == 0 else 0
                                    lg["xp_tap_sp"] = pl.dma(
                                        "scalar", pTAP,
                                        xp_tap[d, mt * 128:(mt + 1) * 128, t0:t0 + TCA],
                                        xp_buf[xb][:, off3:off3 + TCA])
                                pl.wait("vector", aa, a_now)
                                pl.wait("vector", aa, lg["cacc_a"])
                                if d == 0:
                                    pl.op("vector", lambda e, xb=xb, mt=mt: e.tensor_copy(
                                        out=xp_buf[xb][:, 0:3], in_=stash[mt][:]), None)
                                    pl.op("vector", lambda e, xb=xb, mt=mt: e.tensor_copy(
                                        out=stash[mt][:], in_=xp_buf[xb][:, TCA:TCA + 3]), None)
                                else:
                                    pl.op("vector", lambda e, xb=xb, mt=mt: e.tensor_copy(
                                        out=xp_buf[xb][:, TCA:TCA + 3], in_=stash[mt][:]), None)
                                    pl.op("vector", lambda e, xb=xb, mt=mt: e.tensor_copy(
                                        out=stash[mt][:], in_=xp_buf[xb][:, 0:3]), None)
                                tgt_cv = 0
                                for k in range(KC):
                                    coff = k if d == 0 else 3 - k
                                    if k == 0:
                                        pl.op("vector", lambda e, xb=xb, coff=coff, d=d, mt=mt: e.tensor_scalar(
                                            out=cacc[0][:], in0=xp_buf[xb][:, coff:coff + TCA],
                                            scalar1=convw_c[d][:, mt * KC:mt * KC + 1],
                                            scalar2=None, op0=ALU.mult), None)
                                    else:
                                        tgt_cv = pl.op("vector", lambda e, xb=xb, coff=coff, d=d, mt=mt, k=k: e.scalar_tensor_tensor(
                                            out=cacc[k % 2][:],
                                            in0=xp_buf[xb][:, coff:coff + TCA],
                                            scalar=convw_c[d][:, mt * KC + k:mt * KC + k + 1],
                                            in1=cacc[(k + 1) % 2][:],
                                            op0=ALU.mult, op1=ALU.add),
                                            va if k == KC - 1 else None)
                                lg["xp_v"][xb] = tgt_cv
                                pl.wait("scalar", va, tgt_cv)
                                if mt < 4:
                                    pl.wait("scalar", pXC, lg["xc_sp"][mt])
                                pl.wait("scalar", ta, lg["xc_ta"])
                                pl.op("scalar", lambda e, mt=mt, d=d: e.activation(
                                    out=xc_all[mt][:], in_=cacc[1][:], func=AF.Silu,
                                    bias=convb_c[d][:, mt:mt + 1]), aa)
                                lg["cacc_a"] = aa.n
                                if mt < 4:
                                    a_now = aa.n
                                    pl.wait("scalar", aa, a_now)
                                    lg["xc_sp"][mt] = pl.dma(
                                        "scalar", pXC,
                                        xc_sp[d][mt * 128:(mt + 1) * 128, t0:t0 + TCA],
                                        xc_all[mt][:])
                        a_xc = aa.n
                        pl.wait("tensor", aa, a_xc)
                        a_dtlo = 0
                        for mt2 in range(2):
                            rows = 128 if mt2 == 0 else 64
                            for f in range(2):
                                bi = grab_bank()
                                for k in range(16):
                                    pl.op("tensor", lambda e, mt2=mt2, f=f, k=k, bi=bi, d=d, rows=rows: e.matmul(
                                        bank_ap[bi][0:rows, :],
                                        w_xpT_bf[d][:, k * 192 + mt2 * 128:k * 192 + mt2 * 128 + rows],
                                        xc_all[k][:, f * 512:(f + 1) * 512],
                                        start=(k == 0), stop=(k == 15)),
                                        ta if k == 15 else None)
                                bank_stop[bi] = ta.n
                                if mt2 == 1 and f == 1:
                                    lg["xc_ta"] = ta.n
                                pl.wait("scalar", ta, bank_stop[bi])
                                if f == 0:
                                    pl.wait("scalar", pBC, lg["bc_sp"])
                                    if mt2 == 0:
                                        pl.wait("scalar", ta, lg["dtlo_ta"])
                                if mt2 == 0:
                                    pl.op("scalar", lambda e, bi=bi, f=f: e.copy(
                                        out=dtlo_bf[:, f * 512:(f + 1) * 512],
                                        in_=bank_ap[bi][0:64, :]), None)
                                    pl.op("scalar", lambda e, bi=bi, f=f: e.copy(
                                        out=bc_bf[:, f * 512:(f + 1) * 512],
                                        in_=bank_ap[bi][64:128, :]), aa)
                                else:
                                    pl.op("scalar", lambda e, bi=bi, f=f: e.copy(
                                        out=bc_bf[:, f * 512:(f + 1) * 512],
                                        in_=bank_ap[bi][0:64, :]), aa)
                                bank_evac[bi] = aa.n
                            a_now = aa.n
                            if mt2 == 0:
                                a_dtlo = a_now
                            pl.wait("scalar", aa, a_now)
                            lg["bc_sp"] = pl.dma(
                                "scalar", pBC,
                                (b_sp[d] if mt2 == 0 else c_sp[d])[:, t0:t0 + TCA], bc_bf[:])
                        pl.wait("tensor", aa, a_dtlo)
                        for mt in range(NEC):
                            mb = mt % 2
                            for f in range(2):
                                bi = grab_bank()
                                pl.op("tensor", lambda e, mt=mt, f=f, bi=bi, d=d: e.matmul(
                                    bank_ap[bi],
                                    w_dtT_bf[d][:, mt * 128:(mt + 1) * 128],
                                    dtlo_bf[:, f * 512:(f + 1) * 512],
                                    start=True, stop=True), ta)
                                bank_stop[bi] = ta.n
                                if mt == NEC - 1 and f == 1:
                                    lg["dtlo_ta"] = ta.n
                                pl.wait("scalar", ta, bank_stop[bi])
                                if f == 0:
                                    pl.wait("scalar", pDT, lg["dt_sp"][mb])
                                    pl.wait("scalar", va, lg["dt_v"][mb])
                                pl.op("scalar", lambda e, bi=bi, d=d, mt=mt: e.activation(
                                    out=e1_t, in_=bank_ap[bi], func=AF.Exp,
                                    bias=dtb_c[d][:, mt:mt + 1]), None)
                                pl.op("scalar", lambda e, bi=bi, mb=mb, f=f: e.activation(
                                    out=dt_t[mb][:, f * 512:(f + 1) * 512], in_=e1_t,
                                    func=AF.Ln, bias=1.0), aa)
                                bank_evac[bi] = aa.n
                            a_dt = aa.n
                            pl.wait("scalar", aa, a_dt)
                            lg["dt_sp"][mb] = pl.dma(
                                "scalar", pDT, dt_sp[d][mt * 128:(mt + 1) * 128, t0:t0 + TCA],
                                dt_t[mb][:])
                            pl.wait("vector", aa, a_dt)
                            pl.wait("vector", pW, lg["w_sp"])
                            tgt_w = pl.op("vector", lambda e, mb=mb, mt=mt: e.tensor_mul(
                                out=w_t[:], in0=dt_t[mb][:], in1=xc_all[mt][:]), va)
                            lg["dt_v"][mb] = tgt_w
                            pl.wait("scalar", va, tgt_w)
                            lg["w_sp"] = pl.dma(
                                "scalar", pW, w_sp[d][mt * 128:(mt + 1) * 128, t0:t0 + TCA],
                                w_t[:])
                    if d == 0:
                        pl.wait("gpsimd", pNF)
                        pl.wait("gpsimd", pXR)
                        pl.op("gpsimd", lambda e: e.collective_compute(
                            "ReduceScatter", ALU.max, replica_groups=G2,
                            ins=[xnpl_sp.opt()], outs=[xnpl_rx.opt()]), c1)
                        pl.op("gpsimd", lambda e: e.collective_compute(
                            "ReduceScatter", ALU.max, replica_groups=G2,
                            ins=[xr2_sp.opt()], outs=[xr2_rx.opt()]), c2)
                cpk_sp_lg = [0, 0]
                for kk in range(128):
                    pl.wait("sync", va)
                    t1 = pl.dma("sync", sA, stage[:, 0:1024],
                                cprojTi[kk * 128:(kk + 1) * 128, :])
                    pl.wait("vector", sA, t1)
                    pl.wait("vector", pCPK, cpk_sp_lg[kk % 2])
                    if kk < 2:
                        pl.wait("vector", ta, lg["xbf_ta"][kk % 2])
                    t_c = pl.op("vector", lambda e, kk=kk: e.tensor_copy(
                        out=xr2_bf[kk % 2][:], in_=stage[:, 0:1024]), va)
                    pl.wait("scalar", va, t_c)
                    cpk_sp_lg[kk % 2] = pl.dma(
                        "scalar", pCPK, cprojbf[kk * 128:(kk + 1) * 128, :],
                        xr2_bf[kk % 2][:])
            barrier()

        with contextlib.ExitStack() as ctxB:
            sbB = lambda nm, shp, dt: ctxB.enter_context(nc.sbuf_tensor(nm, shp, dt))
            dt_s = [[sbB(f"dtS{ec}_{p}", [128, TCB], F32) for p in range(2)]
                    for ec in range(NEC)]
            wv_s = [[sbB(f"wvS{ec}_{p}", [128, TCB], BF16) for p in range(2)]
                    for ec in range(NEC)]
            xcg = [sbB(f"xcg{i}", [128, TCB], BF16) for i in range(2)]
            szg = [sbB(f"szg{i}", [128, TCB], BF16) for i in range(2)]
            brep = [sbB(f"brep{p}", [128, TCB], BF16) for p in range(2)]
            crep = [sbB(f"crep{p}", [128, TCB], BF16) for p in range(2)]
            dA = [[sbB(f"dA{ei}_{p}", [128, TCB], F32) for p in range(2)]
                  for ei in range(2)]
            u_t = [[sbB(f"u{ei}_{p}", [128, TCB], BF16) for p in range(2)]
                   for ei in range(2)]
            h_t = [sbB(f"h{ei}", [128, TCB], BF16) for ei in range(2)]
            q_t = [[sbB(f"q{ei}_{p}", [128, TCB], BF16) for p in range(2)]
                   for ei in range(2)]
            yg = [sbB(f"yg{ec}", [128, TCB], BF16) for ec in range(NEC)]
            ysv = sbB("ysv", [128, TCB], F32)
            part_t = [sbB(f"part{p}", [128, 512], F32) for p in range(2)]

            lgB = {"ld": [[0, 0] for _ in range(NEC)],
                   "cons_v": [[0, 0] for _ in range(NEC)],
                   "cons_a": [[0, 0] for _ in range(NEC)],
                   "xcg_ld": [0, 0], "xcg_v": [0, 0],
                   "brep_v": [0, 0], "crep_v": [0, 0],
                   "u_v": [[0, 0], [0, 0]], "u_g": [[0, 0], [0, 0]],
                   "brep_g": [0, 0],
                   "cons_g": [[0, 0] for _ in range(NEC)],
                   "dA_v": [[0, 0], [0, 0]], "q_ta": [[0, 0], [0, 0]],
                   "ypsum_a": [0, 0], "yg_ta": 0, "part_gp": [0, 0]}
            ECB = TCB // 512

            if "B" in PHASES:
                for d in range(2):
                    for tci in range(NTCB):
                        tc = tci if d == 0 else NTCB - 1 - tci
                        t0 = tc * TCB
                        par = tci % 2
                        for ec in range(NEC):
                            pl.wait("sync", va, lgB["cons_v"][ec][par])
                            pl.wait("sync", aa, lgB["cons_a"][ec][par])
                            pl.wait("sync", gg, lgB["cons_g"][ec][par])
                            pl.dma("sync", sDW[par], dt_s[ec][par][:],
                                   dt_sp[d][ec * 128:(ec + 1) * 128, t0:t0 + TCB])
                            tl = pl.dma("sync", sDW[par], wv_s[ec][par][:],
                                        w_sp[d][ec * 128:(ec + 1) * 128, t0:t0 + TCB])
                            lgB["ld"][ec][par] = tl
                        for ecp in range(2):
                            ecs = (2 * ecp, 2 * ecp + 1)
                            yps = (psA, psB)
                            ybanks = ((0, 1), (4, 5))
                            pl.wait("sync", gg, lgB["brep_g"][0])
                            g_b_tgts = {0: pl.dma(
                                "sync", gBR[0], brep[0][:],
                                b_sp[d][0:1, t0:t0 + TCB].broadcast_to([128, TCB]))}
                            for n in range(N):
                                p = n % 2
                                if n + 1 < N:
                                    pl.wait("sync", gg, lgB["brep_g"][(n + 1) % 2])
                                    g_b_tgts[n + 1] = pl.dma(
                                        "sync", gBR[(n + 1) % 2], brep[(n + 1) % 2][:],
                                        b_sp[d][n + 1:n + 2, t0:t0 + TCB].broadcast_to([128, TCB]))
                                pl.wait("sync", va, lgB["crep_v"][p])
                                g_c = pl.dma("sync", sCR[p], crep[p][:],
                                             c_sp[d][n:n + 1, t0:t0 + TCB].broadcast_to([128, TCB]))
                                pl.wait("gpsimd", gBR[p], g_b_tgts[n])
                                for ei, ec in enumerate(ecs):
                                    pl.wait("gpsimd", sDW[par], lgB["ld"][ec][par])
                                    pl.wait("gpsimd", va, lgB["u_v"][ei][p])
                                    if d == 0:
                                        g_u = pl.op("gpsimd", lambda e, ei=ei, p=p, ec=ec, par=par: e.tensor_mul(
                                            out=u_t[ei][p][:], in0=wv_s[ec][par][:],
                                            in1=brep[p][:]), gg)
                                    else:
                                        g_u = pl.op("gpsimd", lambda e, ei=ei, p=p, ec=ec, par=par: e.tensor_mul(
                                            out=u_t[ei][p][:], in0=wv_s[ec][par][:, ::-1],
                                            in1=brep[p][:, ::-1]), gg)
                                    lgB["u_g"][ei][p] = g_u
                                    lgB["brep_g"][p] = g_u
                                    if n >= N - 2:
                                        lgB["cons_g"][ec][par] = g_u
                                    pl.wait("scalar", sDW[par], lgB["ld"][ec][par])
                                    pl.wait("scalar", va, lgB["dA_v"][ei][p])
                                    a_dA = pl.op("scalar", lambda e, ei=ei, p=p, ec=ec, par=par, d=d, n=n: e.activation(
                                        out=dA[ei][p][:], in_=dt_s[ec][par][:], func=AF.Exp,
                                        scale=a_dev[d][:, ec * N + n:ec * N + n + 1]), aa)
                                    if n == N - 1:
                                        lgB["cons_a"][ec][par] = a_dA
                                    pl.wait("vector", aa, a_dA)
                                    pl.wait("vector", gg, g_u)
                                    if d == 0:
                                        pl.op("vector", lambda e, ei=ei, p=p, ec=ec, d=d, n=n: e.tensor_tensor_scan(
                                            out=h_t[ei][:], data0=dA[ei][p][:], data1=u_t[ei][p][:],
                                            initial=carry[d][ec][:, n:n + 1],
                                            op0=ALU.mult, op1=ALU.add), va)
                                    else:
                                        pl.op("vector", lambda e, ei=ei, p=p, ec=ec, d=d, n=n: e.tensor_tensor_scan(
                                            out=h_t[ei][:], data0=dA[ei][p][:, ::-1],
                                            data1=u_t[ei][p][:],
                                            initial=carry[d][ec][:, n:n + 1],
                                            op0=ALU.mult, op1=ALU.add), va)
                                    lgB["dA_v"][ei][p] = va.n
                                    lgB["u_v"][ei][p] = va.n
                                    pl.wait("vector", sCR[p], g_c)
                                    pl.wait("vector", ta, lgB["q_ta"][ei][p])
                                    if d == 0:
                                        t_q = pl.op("vector", lambda e, ei=ei, p=p: e.tensor_mul(
                                            out=q_t[ei][p][:], in0=h_t[ei][:], in1=crep[p][:]), va)
                                    else:
                                        t_q = pl.op("vector", lambda e, ei=ei, p=p: e.tensor_mul(
                                            out=q_t[ei][p][:, ::-1], in0=h_t[ei][:],
                                            in1=crep[p][:, ::-1]), va)
                                    lgB["crep_v"][p] = t_q
                                    pl.op("vector", lambda e, ei=ei, ec=ec, d=d, n=n: e.tensor_max(
                                        out=carry[d][ec][:, n:n + 1],
                                        in0=h_t[ei][:, TCB - 1:TCB],
                                        in1=h_t[ei][:, TCB - 1:TCB]), None)
                                    pl.wait("tensor", va, t_q)
                                    if n == 0:
                                        pl.wait("tensor", aa, lgB["ypsum_a"][ecp])
                                        for bq in ybanks[ei]:
                                            pl.wait("tensor", aa, bank_evac[bq])
                                    for f in range(ECB):
                                        pl.op("tensor", lambda e, ei=ei, p=p, f=f, n=n: e.matmul(
                                            yps[ei][:, f * 512:(f + 1) * 512], ident_bf[:],
                                            q_t[ei][p][:, f * 512:(f + 1) * 512],
                                            start=(n == 0), stop=(n == N - 1)),
                                            ta if f == ECB - 1 else None)
                                    lgB["q_ta"][ei][p] = ta.n
                            stop_t = ta.n
                            for ei, ec in enumerate(ecs):
                                pl.wait("sync", sG[ei], lgB["xcg_ld"][ei])
                                pl.wait("sync", va, lgB["xcg_v"][ei])
                                pl.dma("sync", sG[ei], xcg[ei][:],
                                       xc_sp[d][ec * 128:(ec + 1) * 128, t0:t0 + TCB])
                                tlg = pl.dma("sync", sG[ei], szg[ei][:],
                                             sz_sp[d][ec * 128:(ec + 1) * 128, t0:t0 + TCB])
                                lgB["xcg_ld"][ei] = tlg
                            prev_yg = 0
                            for ei, ec in enumerate(ecs):
                                pl.wait("scalar", ta, stop_t)
                                if ei == 1:
                                    pl.wait("scalar", va, prev_yg)
                                a_ev = pl.op("scalar", lambda e, ei=ei: e.copy(
                                    out=ysv[:], in_=yps[ei][:, 0:TCB]), aa)
                                lgB["ypsum_a"][ecp] = a_ev
                                for bq in ybanks[ei]:
                                    bank_evac[bq] = a_ev
                                    bank_stop[bq] = stop_t
                                pl.wait("vector", aa, a_ev)
                                pl.wait("vector", sG[ei], lgB["xcg_ld"][ei])
                                pl.wait("vector", ta, lgB["yg_ta"])
                                pl.op("vector", lambda e, ei=ei, ec=ec, d=d: e.scalar_tensor_tensor(
                                    out=ysv[:], in0=xcg[ei][:],
                                    scalar=d_c[d][:, ec:ec + 1], in1=ysv[:],
                                    op0=ALU.mult, op1=ALU.add), None)
                                t_yg = pl.op("vector", lambda e, ei=ei, ec=ec: e.tensor_mul(
                                    out=yg[ec][:], in0=ysv[:], in1=szg[ei][:]), va)
                                prev_yg = t_yg
                                lgB["cons_v"][ec][par] = t_yg
                                lgB["xcg_v"][ei] = t_yg
                        pl.wait("tensor", va, va.n)
                        for mt in range(TCB // 128):
                            for f in range(2):
                                bi = grab_bank()
                                for ec in range(NEC):
                                    pl.op("tensor", lambda e, mt=mt, f=f, ec=ec, bi=bi, d=d: e.matmul(
                                        bank_ap[bi],
                                        yg[ec][:, mt * 128:(mt + 1) * 128],
                                        w_outT_bf[d][:, ec * D + f * 512:ec * D + f * 512 + 512],
                                        start=(ec == 0), stop=(ec == NEC - 1)),
                                        ta if ec == NEC - 1 else None)
                                bank_stop[bi] = ta.n
                                if mt == TCB // 128 - 1 and f == 1:
                                    lgB["yg_ta"] = ta.n
                                pr = (mt * 2 + f) % 2
                                pl.wait("scalar", ta, bank_stop[bi])
                                pl.wait("scalar", gP[pr], lgB["part_gp"][pr])
                                a_pv = pl.op("scalar", lambda e, bi=bi, pr=pr: e.copy(
                                    out=part_t[pr][:], in_=bank_ap[bi]), aa)
                                bank_evac[bi] = a_pv
                                pl.wait("gpsimd", aa, a_pv)
                                lgB["part_gp"][pr] = pl.dma(
                                    "gpsimd", gP[pr],
                                    xbuf[t0 + mt * 128:t0 + (mt + 1) * 128,
                                         f * 512:(f + 1) * 512],
                                    part_t[pr][:], accum=(d == 1))
                pl.wait("gpsimd", gP[0])
                pl.wait("gpsimd", gP[1])
                pl.op("gpsimd", lambda e: e.collective_compute(
                    "ReduceScatter", ALU.add, replica_groups=G2,
                    ins=[xbuf.opt()], outs=[mbuf.opt()]), c3)
            barrier()

        with contextlib.ExitStack() as ctxC:
            sbC = lambda nm, shp, dt: ctxC.enter_context(nc.sbuf_tensor(nm, shp, dt))
            stC = sbC("stageC", [128, 1024], F32)
            rails1 = sbC("rails1", [128, 32 * 128], BF16)
            rails2 = sbC("rails2", [128, 128 * 128], BF16)
            cpk = [sbC(f"cpk{p}", [128, 4 * D], BF16) for p in range(4)]
            xnsl = [sbC(f"xnsl{dc}", [128, 256], BF16) for dc in range(8)]
            y1 = sbC("y1", [128, 32 * 256], BF16)
            h2 = [sbC(f"h2_{p}", [128, 256], BF16) for p in range(3)]
            wbc = {nm: sbC(f"bc_{nm}", [128, D], F32) for nm in ("w1", "w2", "bb", "b2")}
            ffn_t = [sbC(f"ffn{jt}", [128, D], F32) for jt in range(2)]
            xtC = [sbC(f"xtC{jt}", [128, D], F32) for jt in range(2)]
            xr2C = [sbC(f"xr2C{jt}", [128, D], F32) for jt in range(2)]
            mbC = [sbC(f"mbC{jt}", [128, D], F32) for jt in range(2)]
            oC = [sbC(f"oC{jt}", [128, D], F32) for jt in range(2)]
            t1C = sbC("t1C", [128, D], F32)
            sqdC = stC[:, 0:1024]
            lnC = {nm: sbC(f"lnC_{nm}", [128, 4], F32)
                   for nm in ("s1", "mu", "s2", "ex2", "mu2", "var", "av", "la", "rr")}

            lgC = {"cpk_ta": [0, 0, 0, 0], "h2_ta": [0, 0, 0],
                   "xnsl_ta": 0, "w1_ta": 0, "out_sp": [0, 0],
                   "slot_a": [0] * 4, "tile_v": [0, 0], "stage_v": 0}

            if "C" in PHASES:
                for i in range(8):
                    pl.wait("sync", va, lgC["stage_v"])
                    t1 = pl.dma("sync", sA, stC[:, 0:512],
                                w_rail1[i * 4:(i + 1) * 4].transpose([1, 0, 2]))
                    pl.wait("vector", sA, t1)
                    lgC["stage_v"] = pl.op("vector", lambda e, i=i: e.tensor_copy(
                        out=rails1[:, i * 512:(i + 1) * 512], in_=stC[:, 0:512]), va)
                for i in range(32):
                    pl.wait("sync", va, lgC["stage_v"])
                    t1 = pl.dma("sync", sA, stC[:, 0:512],
                                w_rail2[i * 4:(i + 1) * 4].transpose([1, 0, 2]))
                    pl.wait("vector", sA, t1)
                    lgC["stage_v"] = pl.op("vector", lambda e, i=i: e.tensor_copy(
                        out=rails2[:, i * 512:(i + 1) * 512], in_=stC[:, 0:512]), va)
                pl.dma("sync", sWB, wbc["w1"][:], ln1_w_row[:].broadcast_to([128, D]))
                pl.dma("sync", sWB, wbc["w2"][:], ln2_w_row[:].broadcast_to([128, D]))
                pl.dma("sync", sWB, wbc["bb"][:], ln1_b_row[:].broadcast_to([128, D]))
                t1 = pl.dma("sync", sWB, wbc["b2"][:], ln2_b_row[:].broadcast_to([128, D]))
                pl.wait("vector", sWB, t1)
                pl.op("vector", lambda e: e.tensor_add(
                    out=wbc["bb"][:], in0=wbc["bb"][:], in1=wbc["b2"][:]), va)
                rails_v = va.n

                for blk in range(TSL // 256):
                    c0 = blk * 256
                    pl.wait("sync", c1, 1)
                    pl.wait("sync", ta, lgC["w1_ta"])
                    xs_t = 0
                    for dc in range(8):
                        xs_t = pl.dma("sync", sXS, xnsl[dc][:],
                                      xnpl_rx[dc * 128:(dc + 1) * 128, c0:c0 + 256])
                    pl.wait("tensor", sXS, xs_t)
                    if blk == 0:
                        pl.wait("tensor", va, rails_v)
                    for h in range(4):
                        for dc in range(8):
                            bi = 4 + (h * 8 + dc) % 2
                            pl.wait("tensor", aa, bank_evac[bi])
                            pl.op("tensor", lambda e, h=h, dc=dc, bi=bi: e.matmul(
                                bank_ap[bi][:, 0:256],
                                rails1[:, (h * 8 + dc) * 128:(h * 8 + dc + 1) * 128],
                                xnsl[dc][:, 0:256], start=True, stop=True), ta)
                            bank_stop[bi] = ta.n
                            pl.wait("scalar", ta, bank_stop[bi])
                            pl.op("scalar", lambda e, h=h, dc=dc, bi=bi: e.copy(
                                out=y1[:, (h * 8 + dc) * 256:(h * 8 + dc + 1) * 256],
                                in_=bank_ap[bi][:, 0:256]), aa)
                            bank_evac[bi] = aa.n
                    lgC["w1_ta"] = ta.n
                    a_y1 = aa.n
                    pl.wait("tensor", aa, a_y1)
                    cp_lds = [0, 0, 0, 0]
                    for kidx in range(128):
                        h, mt = kidx // 32, kidx % 32
                        dc = mt // 4
                        bi = 6 + kidx % 2
                        kb = kidx // 4
                        kp = kb % 4
                        ko = kidx % 4
                        if ko == 0:
                            if blk == 0 and kb < 4:
                                pl.wait("sync", pCPK)
                            pl.wait("sync", ta, lgC["cpk_ta"][kp])
                            cp_lds[kp] = pl.dma(
                                "sync", sCP[kp], cpk[kp][:],
                                cprojbf[kb * 512:(kb + 1) * 512, :].rearrange(
                                    "(f p) c -> p f c", p=128))
                        pl.wait("tensor", aa, bank_evac[bi])
                        pl.op("tensor", lambda e, kidx=kidx, h=h, dc=dc, bi=bi: e.matmul(
                            bank_ap[bi][:, 0:256],
                            rails2[:, kidx * 128:(kidx + 1) * 128],
                            y1[:, (h * 8 + dc) * 256:(h * 8 + dc + 1) * 256],
                            start=True, stop=True), ta)
                        bank_stop[bi] = ta.n
                        hb = kidx % 3
                        pl.wait("scalar", ta, bank_stop[bi])
                        pl.wait("scalar", ta, lgC["h2_ta"][hb])
                        a_h2 = pl.op("scalar", lambda e, bi=bi, hb=hb: e.activation(
                            out=h2[hb][:], in_=bank_ap[bi][:, 0:256], func=AF.Gelu), aa)
                        bank_evac[bi] = a_h2
                        pl.wait("tensor", aa, a_h2)
                        if ko == 0:
                            pl.wait("tensor", sCP[kp], cp_lds[kp])
                        if kidx == 0:
                            pl.wait("tensor", aa, bank_evac[0])
                        for m2 in range(2):
                            for fh in range(2):
                                pl.op("tensor", lambda e, ko=ko, kidx=kidx, m2=m2, fh=fh, hb=hb, kp=kp: e.matmul(
                                    psA[:, (m2 * 2 + fh) * 512:(m2 * 2 + fh + 1) * 512],
                                    h2[hb][:, m2 * 128:(m2 + 1) * 128],
                                    cpk[kp][:, ko * D + fh * 512:ko * D + fh * 512 + 512],
                                    start=(kidx == 0), stop=(kidx == 127)),
                                    ta if (m2 == 1 and fh == 1) else None)
                        lgC["h2_ta"][hb] = ta.n
                        if ko == 3:
                            lgC["cpk_ta"][kp] = ta.n
                    lgC["xnsl_ta"] = ta.n
                    stop_cp = ta.n
                    pl.wait("scalar", ta, stop_cp)
                    for jt in range(2):
                        pl.op("scalar", lambda e, jt=jt: e.copy(
                            out=ffn_t[jt][:], in_=psA[:, jt * 1024:(jt + 1) * 1024]), aa)
                    a_ffn = aa.n
                    for i in range(4):
                        bank_evac[i] = a_ffn
                        bank_stop[i] = stop_cp
                    for jt in range(2):
                        j = blk * 2 + jt
                        slt = j % 4
                        tok = j * 128
                        pl.wait("sync", va, lgC["tile_v"][jt])
                        pl.dma("sync", sF[jt], xtC[jt][:], x_sl[tok:tok + 128, :])
                        pl.wait("sync", c2, 1)
                        pl.dma("sync", sF[jt], xr2C[jt][:], xr2_rx[tok:tok + 128, :])
                        pl.wait("sync", c3, 1)
                        tl = pl.dma("sync", sF[jt], mbC[jt][:], mbuf[tok:tok + 128, :])
                        pl.wait("scalar", sF[jt], tl)
                        pl.wait("scalar", va, lgC["tile_v"][jt])
                        pl.op("scalar", lambda e, jt=jt, slt=slt: e.activation(
                            out=sqdC, in_=mbC[jt][:], func=AF.Square,
                            accum_out=lnC["s2"][:, slt:slt + 1]), aa)
                        a_s2 = aa.n
                        pl.wait("vector", sF[jt], tl)
                        pl.wait("vector", aa, lgC["slot_a"][slt])
                        pl.op("vector", lambda e, jt=jt, slt=slt: e.reduce_sum(
                            out=lnC["s1"][:, slt:slt + 1], in_=mbC[jt][:], axis=AXX), va)
                        pl.wait("vector", va)
                        pl.op("vector", lambda e, slt=slt: e.tensor_scalar(
                            out=lnC["mu"][:, slt:slt + 1], in0=lnC["s1"][:, slt:slt + 1],
                            scalar1=1.0 / D, scalar2=None, op0=ALU.mult), None)
                        pl.wait("vector", aa, a_s2)
                        pl.op("vector", lambda e, slt=slt: e.tensor_scalar(
                            out=lnC["ex2"][:, slt:slt + 1], in0=lnC["s2"][:, slt:slt + 1],
                            scalar1=1.0 / D, scalar2=None, op0=ALU.mult), None)
                        pl.op("vector", lambda e, slt=slt: e.tensor_mul(
                            out=lnC["mu2"][:, slt:slt + 1], in0=lnC["mu"][:, slt:slt + 1],
                            in1=lnC["mu"][:, slt:slt + 1]), va)
                        pl.wait("vector", va)
                        pl.op("vector", lambda e, slt=slt: e.tensor_sub(
                            out=lnC["var"][:, slt:slt + 1], in0=lnC["ex2"][:, slt:slt + 1],
                            in1=lnC["mu2"][:, slt:slt + 1]), va)
                        pl.wait("vector", va)
                        pl.op("vector", lambda e, slt=slt: e.tensor_scalar(
                            out=lnC["av"][:, slt:slt + 1], in0=lnC["var"][:, slt:slt + 1],
                            scalar1=EPS, scalar2=None, op0=ALU.add), va)
                        v_av = va.n
                        pl.wait("scalar", va, v_av)
                        pl.op("scalar", lambda e, slt=slt: e.activation(
                            out=lnC["la"][:, slt:slt + 1], in_=lnC["av"][:, slt:slt + 1],
                            func=AF.Ln), aa)
                        pl.wait("scalar", aa)
                        a_r = pl.op("scalar", lambda e, slt=slt: e.activation(
                            out=lnC["rr"][:, slt:slt + 1], in_=lnC["la"][:, slt:slt + 1],
                            func=AF.Exp, scale=-0.5), aa)
                        lgC["slot_a"][slt] = a_r
                        pl.wait("vector", aa, a_r)
                        pl.wait("vector", pOUT[jt], lgC["out_sp"][jt])
                        pl.op("vector", lambda e, jt=jt, slt=slt: e.tensor_scalar(
                            out=mbC[jt][:], in0=mbC[jt][:],
                            scalar1=lnC["mu"][:, slt:slt + 1],
                            scalar2=lnC["rr"][:, slt:slt + 1],
                            op0=ALU.subtract, op1=ALU.mult), None)
                        pl.op("vector", lambda e, jt=jt: e.tensor_mul(
                            out=mbC[jt][:], in0=mbC[jt][:], in1=wbc["w2"][:]), None)
                        pl.op("vector", lambda e, jt=jt: e.tensor_mul(
                            out=t1C[:], in0=xr2C[jt][:], in1=wbc["w1"][:]), None)
                        pl.op("vector", lambda e, jt=jt: e.tensor_add(
                            out=oC[jt][:], in0=mbC[jt][:], in1=t1C[:]), None)
                        pl.op("vector", lambda e, jt=jt: e.tensor_add(
                            out=oC[jt][:], in0=oC[jt][:], in1=wbc["bb"][:]), None)
                        pl.op("vector", lambda e, jt=jt: e.tensor_add(
                            out=oC[jt][:], in0=oC[jt][:], in1=xtC[jt][:]), None)
                        pl.wait("vector", aa, a_ffn)
                        t_o = pl.op("vector", lambda e, jt=jt: e.tensor_add(
                            out=oC[jt][:], in0=oC[jt][:], in1=ffn_t[jt][:]), va)
                        lgC["tile_v"][jt] = t_o
                        pl.wait("scalar", va, t_o)
                        lgC["out_sp"][jt] = pl.dma(
                            "scalar", pOUT[jt], out_sl[tok:tok + 128, :], oC[jt][:])
                pl.wait("scalar", pOUT[0])
                pl.wait("scalar", pOUT[1])
                pl.op("scalar", lambda e: e.copy(out=bscr[:, 3:4], in_=bscr[:, 3:4]), aa)
                pl.wait("sync", aa)

        with nc.Block() as block:
            @block.sync
            def _(e):
                for fn in pl.q["sync"]:
                    fn(e)

            @block.scalar
            def _(e):
                for fn in pl.q["scalar"]:
                    fn(e)

            @block.vector
            def _(e):
                for fn in pl.q["vector"]:
                    fn(e)

            @block.tensor
            def _(e):
                for fn in pl.q["tensor"]:
                    fn(e)

            @block.gpsimd
            def _(e):
                for fn in pl.q["gpsimd"]:
                    fn(e)

    return nc


_NC = None


def _get_nc():
    global _NC
    if _NC is None:
        _NC = build_nc()
    return _NC


def _in_map(inputs, b, q):
    g = lambda k: np.asarray(inputs[k])
    x = g("L0_attn_x")[b]
    lo, hi = q * EQ, (q + 1) * EQ
    perm = np.concatenate([np.arange(lo, hi), np.arange(0, lo), np.arange(hi, E)])
    m = {}
    m["x"] = np.ascontiguousarray(x)
    m["x_sl"] = np.ascontiguousarray(x[q * TSL:(q + 1) * TSL])
    for d, pre in enumerate(("L0_attn_m1_", "L0_attn_m2_")):
        W_in = g(pre + "W_in")
        m[f"w_inT{d}"] = np.ascontiguousarray(np.concatenate(
            [W_in[:E][perm].T, W_in[E + lo:E + hi].T], axis=1))
        m[f"conv_w{d}"] = np.ascontiguousarray(g(pre + "conv_w")[perm].reshape(16, 128, KC))
        m[f"conv_b{d}"] = np.ascontiguousarray(g(pre + "conv_b")[perm].reshape(16, 128))
        m[f"w_xpT{d}"] = np.ascontiguousarray(g(pre + "W_xp")[:, perm].T)
        m[f"w_dtT{d}"] = np.ascontiguousarray(g(pre + "W_dt")[lo:hi].T)
        m[f"dt_b{d}"] = np.ascontiguousarray(g(pre + "dt_b")[lo:hi].reshape(NEC, 128))
        m[f"a_log{d}"] = np.ascontiguousarray(g(pre + "A_log")[lo:hi].reshape(NEC, 128, N))
        m[f"d_col{d}"] = np.ascontiguousarray(g(pre + "D")[lo:hi].reshape(NEC, 128))
        m[f"w_outT{d}"] = np.ascontiguousarray(g(pre + "W_out")[:, lo:hi].T)
    m["ln1_w8"] = np.ascontiguousarray(g("L0_attn_ln1_w").reshape(8, 128))
    m["ln1_b8"] = np.ascontiguousarray(g("L0_attn_ln1_b").reshape(8, 128))
    m["ln1_w_row"] = np.ascontiguousarray(g("L0_attn_ln1_w").reshape(1, D))
    m["ln1_b_row"] = np.ascontiguousarray(g("L0_attn_ln1_b").reshape(1, D))
    m["ln2_w_row"] = np.ascontiguousarray(g("L0_attn_ln2_w").reshape(1, D))
    m["ln2_b_row"] = np.ascontiguousarray(g("L0_attn_ln2_b").reshape(1, D))
    rm = g("L0_ffn_rail_maker")
    rup = g("L0_ffn_rail_up_proj")
    W1 = np.zeros((4, 8, 128, 128), np.float32)
    for r in range(64):
        dcc, ri = r // 8, r % 8
        W1[:, dcc, ri * 16:(ri + 1) * 16, ri * 16:(ri + 1) * 16] = rm[:, r]
    m["w_rail1"] = np.ascontiguousarray(W1.reshape(32, 128, 128))
    W2 = np.zeros((4, 32, 128, 128), np.float32)
    for mt in range(32):
        for r2 in range(2):
            r = mt * 2 + r2
            ri = r % 8
            W2[:, mt, ri * 16:(ri + 1) * 16, r2 * 64:(r2 + 1) * 64] = \
                rup[:, r].transpose(0, 2, 1)
    m["w_rail2"] = np.ascontiguousarray(W2.reshape(128, 128, 128))
    m["cprojT"] = np.ascontiguousarray(g("L0_ffn_w_cproj").T)
    m["ident"] = np.eye(128, dtype=np.float32)
    return m


def kernel(**inputs):
    nc = _get_nc()
    in_maps = [_in_map(inputs, c // 4, c % 4) for c in range(8)]
    res = run_bass_kernel_spmd(nc, in_maps, core_ids=list(range(8)))
    out = np.empty((2, T, D), np.float32)
    for c in range(8):
        b, q = c // 4, c % 4
        out[b, q * TSL:(q + 1) * TSL] = res.results[c]["out_sl"]
    return out


# version 06
# speedup vs baseline: 1.4249x; 1.4249x over previous
import contextlib
import numpy as np

import concourse.bass as bass
import concourse.mybir as mybir
from concourse.bass_utils import run_bass_kernel_spmd

F32 = mybir.dt.float32
BF16 = mybir.dt.bfloat16
AF = mybir.ActivationFunctionType
ALU = mybir.AluOpType
AXX = mybir.AxisListType.X

T = 16384
D = 1024
E = 2048
N = 64
KC = 4
EQ = 512
NEC = 4
TCA = 1024
NTCA = T // TCA
TCB = 1024
NTCB = T // TCB
TSL = T // 4
EPS = 1e-5

DEBUG_TAPS = False
PHASES = "ABC"


class Ctr:
    def __init__(self, handle):
        self.h = handle
        self.n = 0

    def inc(self, k=1):
        self.n += k
        return self.n


class Plan:
    def __init__(self):
        self.q = {k: [] for k in ("sync", "scalar", "vector", "tensor", "gpsimd")}

    def wait(self, eng, ctr, tgt=None):
        t = ctr.n if tgt is None else tgt
        if t > 0:
            self.q[eng].append(lambda e, h=ctr.h, t=t: e.wait_ge(h, t))

    def dma(self, eng, ctr, out_ap, in_ap, accum=False, slow_ok=False):
        tgt = ctr.inc(16)
        if accum:
            self.q[eng].append(lambda e, o=out_ap, i=in_ap, h=ctr.h: e.dma_start(
                out=o, in_=i, accum_op=ALU.add).then_inc(h, 16))
        elif slow_ok:
            self.q[eng].append(lambda e, o=out_ap, i=in_ap, h=ctr.h: e.dma_start(
                out=o, in_=i, allow_slow_non_contiguous=True).then_inc(h, 16))
        else:
            self.q[eng].append(lambda e, o=out_ap, i=in_ap, h=ctr.h: e.dma_start(
                out=o, in_=i).then_inc(h, 16))
        return tgt

    def op(self, eng, fn, ctr=None):
        if ctr is None:
            self.q[eng].append(fn)
            return None
        tgt = ctr.inc(1)
        self.q[eng].append(lambda e, f=fn, h=ctr.h: f(e).then_inc(h, 1))
        return tgt


def build_nc():
    nc = bass.Bass("TRN2", target_bir_lowering=False, debug=False)

    def din(name, shape, dt=F32):
        return nc.dram_tensor(name, shape, dt, kind="ExternalInput").ap()

    def dint(name, shape, dt=F32, notap=False):
        kind = "ExternalOutput" if (DEBUG_TAPS and not notap) else "Internal"
        return nc.dram_tensor(name, shape, dt, kind=kind).ap()

    x_in = din("x", [T, D])
    x_sl = din("x_sl", [TSL, D])
    w_inT = [din(f"w_inT{d}", [D, E + EQ]) for d in range(2)]
    conv_wi = [din(f"conv_w{d}", [16, 128, KC]) for d in range(2)]
    conv_bi = [din(f"conv_b{d}", [16, 128]) for d in range(2)]
    w_xpTi = [din(f"w_xpT{d}", [E, 192]) for d in range(2)]
    w_dtTi = [din(f"w_dtT{d}", [N, EQ]) for d in range(2)]
    dt_b4 = [din(f"dt_b{d}", [NEC, 128]) for d in range(2)]
    a_log4 = [din(f"a_log{d}", [NEC, 128, N]) for d in range(2)]
    d_col4 = [din(f"d_col{d}", [NEC, 128]) for d in range(2)]
    w_outTi = [din(f"w_outT{d}", [EQ, D]) for d in range(2)]
    ln1_w8 = din("ln1_w8", [8, 128])
    ln1_b8 = din("ln1_b8", [8, 128])
    ln1_w_row = din("ln1_w_row", [1, D])
    ln1_b_row = din("ln1_b_row", [1, D])
    ln2_w_row = din("ln2_w_row", [1, D])
    ln2_b_row = din("ln2_b_row", [1, D])
    w_rail1 = din("w_rail1", [32, 128, 128])
    w_rail2 = din("w_rail2", [128, 128, 128])
    cprojTi = din("cprojT", [T, D])
    ident_in = din("ident", [128, 128])

    out_sl = nc.dram_tensor("out_sl", [TSL, D], F32, kind="ExternalOutput").ap()

    xnfm_sp = dint("xnfm_sp", [D, T], BF16)
    xnpl_sp = dint("xnpl_sp", [4, D, TSL], BF16, notap=True)
    xr2_sp = dint("xr2_sp", [4, TSL, D], F32, notap=True)
    xnpl_rx = nc.dram_tensor("xnpl_rx", [D, TSL], BF16, kind="Internal").ap()
    xr2_rx = nc.dram_tensor("xr2_rx", [TSL, D], F32, kind="Internal").ap()
    dt_sp = [dint(f"dt_sp{d}", [EQ, T], F32) for d in range(2)]
    w_sp = [dint(f"w_sp{d}", [EQ, T], BF16) for d in range(2)]
    xc_sp = [dint(f"xc_sp{d}", [EQ, T], BF16) for d in range(2)]
    sz_sp = [dint(f"sz_sp{d}", [EQ, T], BF16) for d in range(2)]
    b_sp = [dint(f"b_sp{d}", [N, T], BF16) for d in range(2)]
    c_sp = [dint(f"c_sp{d}", [N, T], BF16) for d in range(2)]
    xbuf = dint("xbuf", [T, D], F32, notap=True)
    mbuf = nc.dram_tensor("mbuf", [TSL, D], F32, kind="Internal").ap()
    cprojbf = nc.dram_tensor("cprojbf", [T, D], BF16, kind="Internal").ap()
    bdummy = nc.dram_tensor("bdummy", [1, 4], F32, kind="Internal").ap()
    xp_tap = dint("xp_tap", [2, EQ, T], F32) if DEBUG_TAPS else None

    pl = Plan()
    G2 = [[0, 1, 2, 3], [4, 5, 6, 7]]

    with contextlib.ExitStack() as ctx:
        S = lambda name: Ctr(ctx.enter_context(nc.semaphore(name)))
        sA = S("sA")
        sM = S("sM")
        sX = [S("sX0"), S("sX1")]
        sNF = S("sNF")
        sDW = [S("sDW0"), S("sDW1")]
        sCR = [S("sCR0"), S("sCR1")]
        sG = [S("sG0"), S("sG1")]
        sCP = [S("sCP0"), S("sCP1"), S("sCP2"), S("sCP3")]
        sXS = S("sXS")
        sF = [S("sF0"), S("sF1")]
        sWB = S("sWB")
        pXR = S("pXR")
        pNF = S("pNF")
        pSZ = S("pSZ")
        pXC = S("pXC")
        pBC = S("pBC")
        pDT = S("pDT")
        pW = S("pW")
        pCPK = S("pCPK")
        pOUT = [S("pOUT0"), S("pOUT1")]
        pTAP = S("pTAP")
        gBR = [S("gBR0"), S("gBR1")]
        gP = [S("gP0"), S("gP1")]
        va = S("va")
        aa = S("aa")
        ta = S("ta")
        gg = S("gg")
        c1 = S("c1")
        c2 = S("c2")
        c3 = S("c3")
        bs = S("bs")

        SYNCQ = [sA, sM] + sX + [sNF] + sDW + sCR + sG + sCP + [sXS] + sF + [sWB]
        SCALQ = [pXR, pNF, pSZ, pXC, pBC, pDT, pW, pCPK, pTAP] + pOUT
        GPQ = gBR + gP

        sb = lambda name, shape, dt: ctx.enter_context(nc.sbuf_tensor(name, shape, dt))

        psA = ctx.enter_context(nc.psum_tensor("psA", [128, 2048], F32))
        psB = ctx.enter_context(nc.psum_tensor("psB", [128, 2048], F32))
        ident_bf = sb("ident_bf", [128, 128], BF16)
        w_xpT_bf = [sb(f"w_xpT_bf{d}", [128, 16 * 192], BF16) for d in range(2)]
        w_dtT_bf = [sb(f"w_dtT_bf{d}", [64, EQ], BF16) for d in range(2)]
        w_outT_bf = [sb(f"w_outT_bf{d}", [128, NEC * D], BF16) for d in range(2)]
        a_dev = [sb(f"a_dev{d}", [128, NEC * N], F32) for d in range(2)]
        convw_c = [sb(f"convw_c{d}", [128, 16 * KC], F32) for d in range(2)]
        convb_c = [sb(f"convb_c{d}", [128, 16], F32) for d in range(2)]
        dtb_c = [sb(f"dtb_c{d}", [128, NEC], F32) for d in range(2)]
        d_c = [sb(f"d_c{d}", [128, NEC], F32) for d in range(2)]
        w1_c = sb("w1_c", [128, 8], F32)
        b1_cbf = sb("b1_cbf", [128, 8], BF16)
        wb1_c = [sb(f"wb1_c{d}", [128, 20], F32) for d in range(2)]
        carry = [[sb(f"carry{d}_{ec}", [128, N], F32) for ec in range(NEC)]
                 for d in range(2)]
        bscr = sb("bscr", [1, 8], F32)

        bank_ap = [psA[:, i * 512:(i + 1) * 512] for i in range(4)] + \
                  [psB[:, i * 512:(i + 1) * 512] for i in range(4)]
        bank_evac = [0] * 8
        bank_stop = [0] * 8
        bank_rr = [0]

        def grab_bank():
            i = bank_rr[0] % 8
            bank_rr[0] += 1
            pl.wait("tensor", aa, bank_evac[i])
            return i

        def barrier():
            for c in SYNCQ:
                pl.wait("sync", c)
            pl.dma("sync", bs, bscr[0:1, 4:8], bdummy[:])
            for c in SCALQ:
                pl.wait("scalar", c)
            pl.op("scalar", lambda e: e.copy(out=bscr[:, 0:1], in_=bscr[:, 0:1]), aa)
            for c in GPQ:
                pl.wait("gpsimd", c)
            pl.op("gpsimd", lambda e: e.tensor_copy(out=bscr[:, 2:3], in_=bscr[:, 2:3]), gg)
            pl.op("vector", lambda e: e.tensor_copy(out=bscr[:, 1:2], in_=bscr[:, 1:2]), va)
            marks = [(aa, aa.n), (va, va.n), (ta, ta.n), (gg, gg.n), (bs, bs.n)]
            for eng in ("sync", "scalar", "vector", "tensor", "gpsimd"):
                for (c, t) in marks:
                    pl.wait(eng, c, t)

        with contextlib.ExitStack() as ctx0:
            stage0 = ctx0.enter_context(nc.sbuf_tensor("stage0", [128, 2560], F32))
            t1 = pl.dma("sync", sA, stage0[:, 0:128], ident_in[:])
            pl.wait("vector", sA, t1)
            pl.op("vector", lambda e: e.tensor_copy(out=ident_bf[:], in_=stage0[:, 0:128]), va)
            for d in range(2):
                for mt in range(16):
                    pl.dma("sync", sM, convw_c[d][:, mt * KC:(mt + 1) * KC], conv_wi[d][mt])
                pl.dma("sync", sM, convb_c[d][:], conv_bi[d].transpose([1, 0]), slow_ok=True)
                pl.dma("sync", sM, dtb_c[d][:], dt_b4[d].transpose([1, 0]), slow_ok=True)
                pl.dma("sync", sM, d_c[d][:], d_col4[d].transpose([1, 0]), slow_ok=True)
                pl.wait("sync", aa)
                pl.wait("sync", va)
                for ec in range(NEC):
                    t1 = pl.dma("sync", sA, stage0[:, 1024 + ec * N:1024 + (ec + 1) * N],
                                a_log4[d][ec])
                pl.wait("scalar", sA, t1)
                pl.op("scalar", lambda e, d=d: e.activation(
                    out=a_dev[d][:], in_=stage0[:, 1024:1024 + NEC * N], func=AF.Exp), aa)
                pl.wait("vector", aa)
                pl.op("vector", lambda e, d=d: e.tensor_scalar(
                    out=a_dev[d][:], in0=a_dev[d][:], scalar1=-1.0, scalar2=None,
                    op0=ALU.mult), va)
                for k in range(16):
                    pl.wait("sync", va)
                    t1 = pl.dma("sync", sA, stage0[:, 0:192],
                                w_xpTi[d][k * 128:(k + 1) * 128, :])
                    pl.wait("vector", sA, t1)
                    pl.op("vector", lambda e, d=d, k=k: e.tensor_copy(
                        out=w_xpT_bf[d][:, k * 192:(k + 1) * 192], in_=stage0[:, 0:192]), va)
                pl.wait("sync", va)
                t1 = pl.dma("sync", sA, stage0[0:64, 0:EQ], w_dtTi[d][:])
                pl.wait("vector", sA, t1)
                pl.op("vector", lambda e, d=d: e.tensor_copy(
                    out=w_dtT_bf[d][:], in_=stage0[0:64, 0:EQ]), va)
                for ec in range(NEC):
                    pl.wait("sync", va)
                    t1 = pl.dma("sync", sA, stage0[:, 0:D],
                                w_outTi[d][ec * 128:(ec + 1) * 128, :])
                    pl.wait("vector", sA, t1)
                    pl.op("vector", lambda e, d=d, ec=ec: e.tensor_copy(
                        out=w_outT_bf[d][:, ec * D:(ec + 1) * D], in_=stage0[:, 0:D]), va)
            pl.dma("sync", sM, w1_c[:], ln1_w8.transpose([1, 0]), slow_ok=True)
            pl.wait("sync", va)
            t1 = pl.dma("sync", sA, stage0[:, 0:8], ln1_b8.transpose([1, 0]), slow_ok=True)
            pl.wait("vector", sA, t1)
            pl.op("vector", lambda e: e.tensor_copy(out=b1_cbf[:], in_=stage0[:, 0:8]), va)
            for d in range(2):
                for ec in range(NEC):
                    pl.op("vector", lambda e, d=d, ec=ec: e.memset(carry[d][ec][:], 0.0), va)
            barrier()

        with contextlib.ExitStack() as ctxA:
            sbA = lambda nm, shp, dt: ctxA.enter_context(nc.sbuf_tensor(nm, shp, dt))
            stage = sbA("stageA", [128, 2560], F32)
            w_in_buf = sbA("w_in_buf", [128, 8 * 2560], BF16)
            xt = [sbA(f"xt{i}", [128, D], F32) for i in range(2)]
            ln_nm = ("s1", "mu", "s2", "ex2", "mu2", "var", "av", "ra", "v2",
                     "bv", "la", "lb", "lab", "rr", "rt")
            ln = {nm: sbA(f"ln_{nm}", [128, 4], F32) for nm in ln_nm}
            xr2_bf = [sbA(f"xr2bf{i}", [128, D], BF16) for i in range(2)]
            xnp_bf = [sbA(f"xnpbf{i}", [128, D], BF16) for i in range(2)]
            xr2_f = sbA("xr2f", [128, D], F32)
            xnfm = [sbA(f"xnfm{dc}", [128, TCA], BF16) for dc in range(8)]
            xnfm_p = [sbA(f"xnfmp{dc}", [128, TCA], BF16) for dc in range(8)]
            xp_buf = [sbA(f"xp{i}", [128, TCA + 3], F32) for i in range(3)]
            cacc = [sbA(f"cacc{i}", [128, TCA], F32) for i in range(2)]
            stash = [sbA(f"stash{mt}", [128, 3], F32) for mt in range(16)]
            xc_all = [sbA(f"xc{mt}", [128, TCA], BF16) for mt in range(16)]
            dtlo_bf = sbA("dtlo_bf", [64, TCA], BF16)
            bc_bf = sbA("bc_bf", [64, TCA], BF16)
            dt_t = [sbA(f"dt_t{i}", [128, TCA], F32) for i in range(2)]
            w_t = sbA("w_tile", [128, TCA], BF16)
            sz_t = sbA("sz_tile", [128, TCA], BF16)
            sqd = stage[:, 0:1024]
            e1_t = stage[:, 1024:1536]

            lg = {"xt_v": [0, 0], "xt_a": [0, 0], "slot_a": [0] * 4, "slot_v": [0] * 4,
                  "xr2f_sp": 0, "xbf_ta": [0, 0], "xnfm_sp": 0, "xnfm_ta": 0,
                  "xp_v": [0, 0, 0], "xp_tap_sp": 0, "cacc_a": 0,
                  "xc_sp": [0] * 4, "xc_ta": 0, "dtlo_ta": 0, "bc_sp": 0,
                  "dt_sp": [0, 0], "dt_v": [0, 0], "w_sp": 0, "sz_sp": 0}

            if "A" in PHASES:
                for d in range(2):
                    pl.wait("sync", aa)
                    pl.wait("sync", va)
                    pl.wait("sync", ta)
                    pl.wait("vector", ta)
                    pl.wait("vector", aa)
                    for k in range(8):
                        pl.wait("sync", va)
                        t1 = pl.dma("sync", sA, stage[:, 0:2560],
                                    w_inT[d][k * 128:(k + 1) * 128, :])
                        pl.wait("vector", sA, t1)
                        pl.op("vector", lambda e, k=k: e.tensor_copy(
                            out=w_in_buf[:, k * 2560:(k + 1) * 2560],
                            in_=stage[:, 0:2560]), va)
                    pl.wait("tensor", va)
                    for mt in range(20):
                        bi = grab_bank()
                        for k in range(8):
                            pl.op("tensor", lambda e, mt=mt, k=k, bi=bi: e.matmul(
                                bank_ap[bi][:, 0:1],
                                w_in_buf[:, k * 2560 + mt * 128:k * 2560 + (mt + 1) * 128],
                                b1_cbf[:, k:k + 1], start=(k == 0), stop=(k == 7)),
                                ta if k == 7 else None)
                        bank_stop[bi] = ta.n
                        pl.wait("scalar", ta, bank_stop[bi])
                        pl.op("scalar", lambda e, mt=mt, bi=bi, d=d: e.copy(
                            out=wb1_c[d][:, mt:mt + 1], in_=bank_ap[bi][:, 0:1]), aa)
                        bank_evac[bi] = aa.n
                    wb1_ta = ta.n
                    for k in range(8):
                        pl.wait("sync", va)
                        t1 = pl.dma("sync", sA, stage[:, 0:2560],
                                    w_inT[d][k * 128:(k + 1) * 128, :])
                        pl.wait("vector", sA, t1)
                        if k == 0:
                            pl.wait("vector", ta, wb1_ta)
                        pl.op("vector", lambda e, k=k: e.tensor_scalar(
                            out=w_in_buf[:, k * 2560:(k + 1) * 2560], in0=stage[:, 0:2560],
                            scalar1=w1_c[:, k:k + 1], scalar2=None, op0=ALU.mult), va)
                    w_in_v = va.n
                    pl.wait("tensor", va, w_in_v)
                    for mt in range(16):
                        pl.op("vector", lambda e, mt=mt: e.memset(stash[mt][:], 0.0), va)

                    for tci in range(NTCA):
                        tc = tci if d == 0 else NTCA - 1 - tci
                        t0 = tc * TCA
                        qq, lt = t0 // TSL, t0 % TSL
                        a_fm = aa.n
                        if d == 0:
                            for j in range(8):
                                jj = j % 2
                                slt = j % 4
                                pl.wait("sync", va, lg["xt_v"][jj])
                                pl.wait("sync", aa, lg["xt_a"][jj])
                                tgt_x = pl.dma("sync", sX[jj], xt[jj][:],
                                               x_in[t0 + j * 128:t0 + (j + 1) * 128, :])
                                pl.wait("vector", sX[jj], tgt_x)
                                pl.wait("vector", aa, lg["slot_a"][slt])
                                pl.op("vector", lambda e, jj=jj, slt=slt: e.reduce_sum(
                                    out=ln["s1"][:, slt:slt + 1], in_=xt[jj][:], axis=AXX), va)
                                pl.wait("vector", va)
                                pl.op("vector", lambda e, slt=slt: e.tensor_scalar(
                                    out=ln["mu"][:, slt:slt + 1], in0=ln["s1"][:, slt:slt + 1],
                                    scalar1=1.0 / D, scalar2=None, op0=ALU.mult), va)
                                pl.wait("scalar", sX[jj], tgt_x)
                                pl.wait("scalar", va, lg["slot_v"][slt])
                                pl.op("scalar", lambda e, jj=jj, slt=slt: e.activation(
                                    out=sqd, in_=xt[jj][:], func=AF.Square,
                                    accum_out=ln["s2"][:, slt:slt + 1]), aa)
                                a_s2 = aa.n
                                lg["xt_a"][jj] = a_s2
                                pl.wait("vector", aa, a_s2)
                                pl.op("vector", lambda e, slt=slt: e.tensor_scalar(
                                    out=ln["ex2"][:, slt:slt + 1], in0=ln["s2"][:, slt:slt + 1],
                                    scalar1=1.0 / D, scalar2=None, op0=ALU.mult), None)
                                pl.op("vector", lambda e, slt=slt: e.tensor_mul(
                                    out=ln["mu2"][:, slt:slt + 1], in0=ln["mu"][:, slt:slt + 1],
                                    in1=ln["mu"][:, slt:slt + 1]), va)
                                pl.wait("vector", va)
                                pl.op("vector", lambda e, slt=slt: e.tensor_sub(
                                    out=ln["var"][:, slt:slt + 1], in0=ln["ex2"][:, slt:slt + 1],
                                    in1=ln["mu2"][:, slt:slt + 1]), va)
                                pl.wait("vector", va)
                                pl.op("vector", lambda e, slt=slt: e.tensor_scalar(
                                    out=ln["av"][:, slt:slt + 1], in0=ln["var"][:, slt:slt + 1],
                                    scalar1=EPS, scalar2=None, op0=ALU.add), va)
                                pl.wait("vector", va)
                                pl.op("vector", lambda e, slt=slt: e.reciprocal(
                                    out=ln["ra"][:, slt:slt + 1], in_=ln["av"][:, slt:slt + 1]), va)
                                pl.wait("vector", va)
                                pl.op("vector", lambda e, slt=slt: e.tensor_mul(
                                    out=ln["v2"][:, slt:slt + 1], in0=ln["var"][:, slt:slt + 1],
                                    in1=ln["ra"][:, slt:slt + 1]), va)
                                pl.wait("vector", va)
                                pl.op("vector", lambda e, slt=slt: e.tensor_scalar(
                                    out=ln["bv"][:, slt:slt + 1], in0=ln["v2"][:, slt:slt + 1],
                                    scalar1=EPS, scalar2=None, op0=ALU.add), va)
                                v_bv = va.n
                                pl.wait("scalar", va, v_bv)
                                pl.op("scalar", lambda e, slt=slt: e.activation(
                                    out=ln["la"][:, slt:slt + 1], in_=ln["av"][:, slt:slt + 1],
                                    func=AF.Ln), aa)
                                pl.op("scalar", lambda e, slt=slt: e.activation(
                                    out=ln["lb"][:, slt:slt + 1], in_=ln["bv"][:, slt:slt + 1],
                                    func=AF.Ln), aa)
                                pl.wait("scalar", aa)
                                pl.op("scalar", lambda e, slt=slt: e.activation(
                                    out=ln["rr"][:, slt:slt + 1], in_=ln["la"][:, slt:slt + 1],
                                    func=AF.Exp, scale=-0.5), aa)
                                a_lb = aa.n
                                pl.wait("vector", aa, a_lb)
                                pl.op("vector", lambda e, slt=slt: e.tensor_add(
                                    out=ln["lab"][:, slt:slt + 1], in0=ln["la"][:, slt:slt + 1],
                                    in1=ln["lb"][:, slt:slt + 1]), va)
                                v_lab = va.n
                                pl.wait("scalar", va, v_lab)
                                pl.op("scalar", lambda e, slt=slt: e.activation(
                                    out=ln["rt"][:, slt:slt + 1], in_=ln["lab"][:, slt:slt + 1],
                                    func=AF.Exp, scale=-0.5), aa)
                                a_rt = aa.n
                                lg["slot_a"][slt] = a_rt
                                pl.wait("vector", aa, a_rt)
                                pl.wait("vector", ta, lg["xbf_ta"][jj])
                                pl.wait("vector", pXR, lg["xr2f_sp"])
                                pl.op("vector", lambda e, jj=jj, slt=slt: e.tensor_scalar(
                                    out=xr2_bf[jj][:], in0=xt[jj][:],
                                    scalar1=ln["mu"][:, slt:slt + 1],
                                    scalar2=ln["rt"][:, slt:slt + 1],
                                    op0=ALU.subtract, op1=ALU.mult), None)
                                pl.op("vector", lambda e, jj=jj, slt=slt: e.tensor_scalar(
                                    out=xnp_bf[jj][:], in0=xt[jj][:],
                                    scalar1=ln["mu"][:, slt:slt + 1],
                                    scalar2=ln["rr"][:, slt:slt + 1],
                                    op0=ALU.subtract, op1=ALU.mult), None)
                                pl.op("vector", lambda e, jj=jj, slt=slt: e.tensor_scalar(
                                    out=xr2_f[:], in0=xt[jj][:],
                                    scalar1=ln["mu"][:, slt:slt + 1],
                                    scalar2=ln["rt"][:, slt:slt + 1],
                                    op0=ALU.subtract, op1=ALU.mult), va)
                                v_xn = va.n
                                lg["slot_v"][slt] = v_xn
                                lg["xt_v"][jj] = v_xn
                                pl.wait("scalar", va, v_xn)
                                lg["xr2f_sp"] = pl.dma(
                                    "scalar", pXR,
                                    xr2_sp[qq, lt + j * 128:lt + (j + 1) * 128, :], xr2_f[:])
                                pl.wait("tensor", va, v_xn)
                                for dc in range(8):
                                    bi = grab_bank()
                                    pl.op("tensor", lambda e, jj=jj, dc=dc, bi=bi: e.transpose(
                                        out=bank_ap[bi].bitcast(BF16)[:, 0:128],
                                        in_=xr2_bf[jj][:, dc * 128:(dc + 1) * 128],
                                        identity=ident_bf[:]), None)
                                    pl.op("tensor", lambda e, jj=jj, dc=dc, bi=bi: e.transpose(
                                        out=bank_ap[bi].bitcast(BF16)[:, 128:256],
                                        in_=xnp_bf[jj][:, dc * 128:(dc + 1) * 128],
                                        identity=ident_bf[:]), ta)
                                    bank_stop[bi] = ta.n
                                    if dc == 7:
                                        lg["xbf_ta"][jj] = ta.n
                                    pl.wait("scalar", ta, bank_stop[bi])
                                    if j == 0 and dc == 0:
                                        pl.wait("scalar", ta, lg["xnfm_ta"])
                                        pl.wait("scalar", pNF, lg["xnfm_sp"])
                                    pl.op("scalar", lambda e, dc=dc, j=j, bi=bi: e.copy(
                                        out=xnfm[dc][:, j * 128:(j + 1) * 128],
                                        in_=bank_ap[bi].bitcast(BF16)[:, 0:128]), None)
                                    pl.op("scalar", lambda e, dc=dc, j=j, bi=bi: e.copy(
                                        out=xnfm_p[dc][:, j * 128:(j + 1) * 128],
                                        in_=bank_ap[bi].bitcast(BF16)[:, 128:256]), aa)
                                    bank_evac[bi] = aa.n
                            a_fm = aa.n
                            pl.wait("scalar", aa, a_fm)
                            for dc in range(8):
                                pl.dma("scalar", pNF, xnfm_sp[dc * 128:(dc + 1) * 128,
                                                              t0:t0 + TCA], xnfm[dc][:])
                                s_t = pl.dma("scalar", pNF,
                                             xnpl_sp[qq, dc * 128:(dc + 1) * 128, lt:lt + TCA],
                                             xnfm_p[dc][:])
                            lg["xnfm_sp"] = s_t
                        else:
                            if tci == 0:
                                pl.wait("sync", pNF)
                            pl.wait("sync", ta, lg["xnfm_ta"])
                            nf_t = 0
                            for dc in range(8):
                                nf_t = pl.dma("sync", sNF, xnfm[dc][:],
                                              xnfm_sp[dc * 128:(dc + 1) * 128, t0:t0 + TCA])
                            pl.wait("tensor", sNF, nf_t)

                        if d == 0:
                            pl.wait("tensor", aa, a_fm)
                        for mt in range(20):
                            for f in range(2):
                                bi = grab_bank()
                                for k in range(8):
                                    pl.op("tensor", lambda e, mt=mt, f=f, k=k, bi=bi: e.matmul(
                                        bank_ap[bi],
                                        w_in_buf[:, k * 2560 + mt * 128:k * 2560 + (mt + 1) * 128],
                                        xnfm[k][:, f * 512:(f + 1) * 512],
                                        start=(k == 0), stop=(k == 7)),
                                        ta if k == 7 else None)
                                bank_stop[bi] = ta.n
                                if mt == 19 and f == 1:
                                    lg["xnfm_ta"] = ta.n
                                pl.wait("scalar", ta, bank_stop[bi])
                                if mt < 16:
                                    off = (3 if d == 0 else 0) + f * 512
                                    xb = mt % 3
                                    if f == 0:
                                        pl.wait("scalar", va, lg["xp_v"][xb])
                                        pl.wait("scalar", pTAP, lg["xp_tap_sp"])
                                    pl.op("scalar", lambda e, bi=bi, mt=mt, off=off, xb=xb, d=d: e.add(
                                        out=xp_buf[xb][:, off:off + 512], in_=bank_ap[bi],
                                        add=wb1_c[d][:, mt:mt + 1]), aa)
                                else:
                                    if f == 0:
                                        pl.wait("scalar", pSZ, lg["sz_sp"])
                                    pl.op("scalar", lambda e, bi=bi, mt=mt, f=f, d=d: e.activation(
                                        out=sz_t[:, f * 512:(f + 1) * 512], in_=bank_ap[bi],
                                        func=AF.Silu, bias=wb1_c[d][:, mt:mt + 1]), aa)
                                bank_evac[bi] = aa.n
                            if mt >= 16:
                                a_now = aa.n
                                pl.wait("scalar", aa, a_now)
                                lg["sz_sp"] = pl.dma(
                                    "scalar", pSZ,
                                    sz_sp[d][(mt - 16) * 128:(mt - 15) * 128, t0:t0 + TCA],
                                    sz_t[:])
                            else:
                                xb = mt % 3
                                a_now = aa.n
                                if DEBUG_TAPS and mt < 4:
                                    pl.wait("scalar", aa, a_now)
                                    off3 = 3 if d == 0 else 0
                                    lg["xp_tap_sp"] = pl.dma(
                                        "scalar", pTAP,
                                        xp_tap[d, mt * 128:(mt + 1) * 128, t0:t0 + TCA],
                                        xp_buf[xb][:, off3:off3 + TCA])
                                pl.wait("vector", aa, a_now)
                                pl.wait("vector", aa, lg["cacc_a"])
                                if d == 0:
                                    pl.op("vector", lambda e, xb=xb, mt=mt: e.tensor_copy(
                                        out=xp_buf[xb][:, 0:3], in_=stash[mt][:]), None)
                                    pl.op("vector", lambda e, xb=xb, mt=mt: e.tensor_copy(
                                        out=stash[mt][:], in_=xp_buf[xb][:, TCA:TCA + 3]), None)
                                else:
                                    pl.op("vector", lambda e, xb=xb, mt=mt: e.tensor_copy(
                                        out=xp_buf[xb][:, TCA:TCA + 3], in_=stash[mt][:]), None)
                                    pl.op("vector", lambda e, xb=xb, mt=mt: e.tensor_copy(
                                        out=stash[mt][:], in_=xp_buf[xb][:, 0:3]), None)
                                tgt_cv = 0
                                for k in range(KC):
                                    coff = k if d == 0 else 3 - k
                                    if k == 0:
                                        pl.op("vector", lambda e, xb=xb, coff=coff, d=d, mt=mt: e.tensor_scalar(
                                            out=cacc[0][:], in0=xp_buf[xb][:, coff:coff + TCA],
                                            scalar1=convw_c[d][:, mt * KC:mt * KC + 1],
                                            scalar2=None, op0=ALU.mult), None)
                                    else:
                                        tgt_cv = pl.op("vector", lambda e, xb=xb, coff=coff, d=d, mt=mt, k=k: e.scalar_tensor_tensor(
                                            out=cacc[k % 2][:],
                                            in0=xp_buf[xb][:, coff:coff + TCA],
                                            scalar=convw_c[d][:, mt * KC + k:mt * KC + k + 1],
                                            in1=cacc[(k + 1) % 2][:],
                                            op0=ALU.mult, op1=ALU.add),
                                            va if k == KC - 1 else None)
                                lg["xp_v"][xb] = tgt_cv
                                pl.wait("scalar", va, tgt_cv)
                                if mt < 4:
                                    pl.wait("scalar", pXC, lg["xc_sp"][mt])
                                pl.wait("scalar", ta, lg["xc_ta"])
                                pl.op("scalar", lambda e, mt=mt, d=d: e.activation(
                                    out=xc_all[mt][:], in_=cacc[1][:], func=AF.Silu,
                                    bias=convb_c[d][:, mt:mt + 1]), aa)
                                lg["cacc_a"] = aa.n
                                if mt < 4:
                                    a_now = aa.n
                                    pl.wait("scalar", aa, a_now)
                                    lg["xc_sp"][mt] = pl.dma(
                                        "scalar", pXC,
                                        xc_sp[d][mt * 128:(mt + 1) * 128, t0:t0 + TCA],
                                        xc_all[mt][:])
                        a_xc = aa.n
                        pl.wait("tensor", aa, a_xc)
                        a_dtlo = 0
                        for mt2 in range(2):
                            rows = 128 if mt2 == 0 else 64
                            for f in range(2):
                                bi = grab_bank()
                                for k in range(16):
                                    pl.op("tensor", lambda e, mt2=mt2, f=f, k=k, bi=bi, d=d, rows=rows: e.matmul(
                                        bank_ap[bi][0:rows, :],
                                        w_xpT_bf[d][:, k * 192 + mt2 * 128:k * 192 + mt2 * 128 + rows],
                                        xc_all[k][:, f * 512:(f + 1) * 512],
                                        start=(k == 0), stop=(k == 15)),
                                        ta if k == 15 else None)
                                bank_stop[bi] = ta.n
                                if mt2 == 1 and f == 1:
                                    lg["xc_ta"] = ta.n
                                pl.wait("scalar", ta, bank_stop[bi])
                                if f == 0:
                                    pl.wait("scalar", pBC, lg["bc_sp"])
                                    if mt2 == 0:
                                        pl.wait("scalar", ta, lg["dtlo_ta"])
                                if mt2 == 0:
                                    pl.op("scalar", lambda e, bi=bi, f=f: e.copy(
                                        out=dtlo_bf[:, f * 512:(f + 1) * 512],
                                        in_=bank_ap[bi][0:64, :]), None)
                                    pl.op("scalar", lambda e, bi=bi, f=f: e.copy(
                                        out=bc_bf[:, f * 512:(f + 1) * 512],
                                        in_=bank_ap[bi][64:128, :]), aa)
                                else:
                                    pl.op("scalar", lambda e, bi=bi, f=f: e.copy(
                                        out=bc_bf[:, f * 512:(f + 1) * 512],
                                        in_=bank_ap[bi][0:64, :]), aa)
                                bank_evac[bi] = aa.n
                            a_now = aa.n
                            if mt2 == 0:
                                a_dtlo = a_now
                            pl.wait("scalar", aa, a_now)
                            lg["bc_sp"] = pl.dma(
                                "scalar", pBC,
                                (b_sp[d] if mt2 == 0 else c_sp[d])[:, t0:t0 + TCA], bc_bf[:])
                        pl.wait("tensor", aa, a_dtlo)
                        for mt in range(NEC):
                            mb = mt % 2
                            for f in range(2):
                                bi = grab_bank()
                                pl.op("tensor", lambda e, mt=mt, f=f, bi=bi, d=d: e.matmul(
                                    bank_ap[bi],
                                    w_dtT_bf[d][:, mt * 128:(mt + 1) * 128],
                                    dtlo_bf[:, f * 512:(f + 1) * 512],
                                    start=True, stop=True), ta)
                                bank_stop[bi] = ta.n
                                if mt == NEC - 1 and f == 1:
                                    lg["dtlo_ta"] = ta.n
                                pl.wait("scalar", ta, bank_stop[bi])
                                if f == 0:
                                    pl.wait("scalar", pDT, lg["dt_sp"][mb])
                                    pl.wait("scalar", va, lg["dt_v"][mb])
                                pl.op("scalar", lambda e, bi=bi, d=d, mt=mt: e.activation(
                                    out=e1_t, in_=bank_ap[bi], func=AF.Exp,
                                    bias=dtb_c[d][:, mt:mt + 1]), None)
                                pl.op("scalar", lambda e, bi=bi, mb=mb, f=f: e.activation(
                                    out=dt_t[mb][:, f * 512:(f + 1) * 512], in_=e1_t,
                                    func=AF.Ln, bias=1.0), aa)
                                bank_evac[bi] = aa.n
                            a_dt = aa.n
                            pl.wait("scalar", aa, a_dt)
                            lg["dt_sp"][mb] = pl.dma(
                                "scalar", pDT, dt_sp[d][mt * 128:(mt + 1) * 128, t0:t0 + TCA],
                                dt_t[mb][:])
                            pl.wait("vector", aa, a_dt)
                            pl.wait("vector", pW, lg["w_sp"])
                            tgt_w = pl.op("vector", lambda e, mb=mb, mt=mt: e.tensor_mul(
                                out=w_t[:], in0=dt_t[mb][:], in1=xc_all[mt][:]), va)
                            lg["dt_v"][mb] = tgt_w
                            pl.wait("scalar", va, tgt_w)
                            lg["w_sp"] = pl.dma(
                                "scalar", pW, w_sp[d][mt * 128:(mt + 1) * 128, t0:t0 + TCA],
                                w_t[:])
                    if d == 0:
                        pl.wait("gpsimd", pNF)
                        pl.wait("gpsimd", pXR)
                        pl.op("gpsimd", lambda e: e.collective_compute(
                            "ReduceScatter", ALU.max, replica_groups=G2,
                            ins=[xnpl_sp.opt()], outs=[xnpl_rx.opt()]), c1)
                        pl.op("gpsimd", lambda e: e.collective_compute(
                            "ReduceScatter", ALU.max, replica_groups=G2,
                            ins=[xr2_sp.opt()], outs=[xr2_rx.opt()]), c2)
                cpk_sp_lg = [0, 0]
                for kk in range(128):
                    pl.wait("sync", va)
                    t1 = pl.dma("sync", sA, stage[:, 0:1024],
                                cprojTi[kk * 128:(kk + 1) * 128, :])
                    pl.wait("vector", sA, t1)
                    pl.wait("vector", pCPK, cpk_sp_lg[kk % 2])
                    if kk < 2:
                        pl.wait("vector", ta, lg["xbf_ta"][kk % 2])
                    t_c = pl.op("vector", lambda e, kk=kk: e.tensor_copy(
                        out=xr2_bf[kk % 2][:], in_=stage[:, 0:1024]), va)
                    pl.wait("scalar", va, t_c)
                    cpk_sp_lg[kk % 2] = pl.dma(
                        "scalar", pCPK, cprojbf[kk * 128:(kk + 1) * 128, :],
                        xr2_bf[kk % 2][:])
            barrier()

        with contextlib.ExitStack() as ctxB:
            sbB = lambda nm, shp, dt: ctxB.enter_context(nc.sbuf_tensor(nm, shp, dt))
            dt_s = [[sbB(f"dtS{ec}_{p}", [128, TCB], F32) for p in range(2)]
                    for ec in range(NEC)]
            wv_s = [[sbB(f"wvS{ec}_{p}", [128, TCB], BF16) for p in range(2)]
                    for ec in range(NEC)]
            xcg = [sbB(f"xcg{i}", [128, TCB], BF16) for i in range(2)]
            szg = [sbB(f"szg{i}", [128, TCB], BF16) for i in range(2)]
            brep = [sbB(f"brep{p}", [128, TCB], BF16) for p in range(2)]
            crep = [sbB(f"crep{p}", [128, TCB], BF16) for p in range(2)]
            dA = [[sbB(f"dA{ei}_{p}", [128, TCB], F32) for p in range(2)]
                  for ei in range(2)]
            u_t = [[sbB(f"u{ei}_{p}", [128, TCB], BF16) for p in range(2)]
                   for ei in range(2)]
            h_t = [sbB(f"h{ei}", [128, TCB], BF16) for ei in range(2)]
            q_t = [[sbB(f"q{ei}_{p}", [128, TCB], BF16) for p in range(2)]
                   for ei in range(2)]
            yg = [sbB(f"yg{ec}", [128, TCB], BF16) for ec in range(NEC)]
            ysv = sbB("ysv", [128, TCB], F32)
            part_t = [sbB(f"part{p}", [128, 512], F32) for p in range(2)]

            lgB = {"ld": [[0, 0] for _ in range(NEC)],
                   "cons_v": [[0, 0] for _ in range(NEC)],
                   "cons_a": [[0, 0] for _ in range(NEC)],
                   "xcg_ld": [0, 0], "xcg_v": [0, 0],
                   "brep_v": [0, 0], "crep_v": [0, 0],
                   "u_v": [[0, 0], [0, 0]], "u_g": [[0, 0], [0, 0]],
                   "brep_g": [0, 0],
                   "cons_g": [[0, 0] for _ in range(NEC)],
                   "dA_v": [[0, 0], [0, 0]], "q_ta": [[0, 0], [0, 0]],
                   "ypsum_a": [0, 0], "yg_ta": 0, "part_gp": [0, 0]}
            ECB = TCB // 512

            if "B" in PHASES:
                for d in range(2):
                    for tci in range(NTCB):
                        tc = tci if d == 0 else NTCB - 1 - tci
                        t0 = tc * TCB
                        par = tci % 2
                        for ec in range(NEC):
                            pl.wait("sync", va, lgB["cons_v"][ec][par])
                            pl.wait("sync", aa, lgB["cons_a"][ec][par])
                            pl.wait("sync", va, lgB["cons_g"][ec][par])
                            pl.dma("sync", sDW[par], dt_s[ec][par][:],
                                   dt_sp[d][ec * 128:(ec + 1) * 128, t0:t0 + TCB])
                            tl = pl.dma("sync", sDW[par], wv_s[ec][par][:],
                                        w_sp[d][ec * 128:(ec + 1) * 128, t0:t0 + TCB])
                            lgB["ld"][ec][par] = tl
                        for ecp in range(2):
                            ecs = (2 * ecp, 2 * ecp + 1)
                            yps = (psA, psB)
                            ybanks = ((0, 1), (4, 5))
                            pl.wait("sync", va, lgB["brep_g"][0])
                            g_b_tgts = {0: pl.dma(
                                "sync", gBR[0], brep[0][:],
                                b_sp[d][0:1, t0:t0 + TCB].broadcast_to([128, TCB]))}
                            for n in range(N):
                                p = n % 2
                                if n + 1 < N:
                                    pl.wait("sync", va, lgB["brep_g"][(n + 1) % 2])
                                    g_b_tgts[n + 1] = pl.dma(
                                        "sync", gBR[(n + 1) % 2], brep[(n + 1) % 2][:],
                                        b_sp[d][n + 1:n + 2, t0:t0 + TCB].broadcast_to([128, TCB]))
                                pl.wait("sync", va, lgB["crep_v"][p])
                                g_c = pl.dma("sync", sCR[p], crep[p][:],
                                             c_sp[d][n:n + 1, t0:t0 + TCB].broadcast_to([128, TCB]))
                                pl.wait("gpsimd", gBR[p], g_b_tgts[n])
                                for ei, ec in enumerate(ecs):
                                    pl.wait("scalar", sDW[par], lgB["ld"][ec][par])
                                    pl.wait("scalar", va, lgB["dA_v"][ei][p])
                                    a_dA = pl.op("scalar", lambda e, ei=ei, p=p, ec=ec, par=par, d=d, n=n: e.activation(
                                        out=dA[ei][p][:], in_=dt_s[ec][par][:], func=AF.Exp,
                                        scale=a_dev[d][:, ec * N + n:ec * N + n + 1]), aa)
                                    if n == N - 1:
                                        lgB["cons_a"][ec][par] = a_dA
                                    pl.wait("vector", gBR[p], g_b_tgts[n])
                                    pl.wait("vector", sDW[par], lgB["ld"][ec][par])
                                    pl.wait("vector", va, lgB["u_v"][ei][p])
                                    if d == 0:
                                        g_u = pl.op("vector", lambda e, ei=ei, p=p, ec=ec, par=par: e.tensor_mul(
                                            out=u_t[ei][p][:], in0=wv_s[ec][par][:],
                                            in1=brep[p][:]), va)
                                    else:
                                        g_u = pl.op("vector", lambda e, ei=ei, p=p, ec=ec, par=par: e.tensor_mul(
                                            out=u_t[ei][p][:], in0=wv_s[ec][par][:, ::-1],
                                            in1=brep[p][:, ::-1]), va)
                                    lgB["u_g"][ei][p] = g_u
                                    lgB["brep_g"][p] = g_u
                                    if n >= N - 2:
                                        lgB["cons_g"][ec][par] = g_u
                                    pl.wait("vector", aa, a_dA)
                                    if d == 0:
                                        pl.op("vector", lambda e, ei=ei, p=p, ec=ec, d=d, n=n: e.tensor_tensor_scan(
                                            out=h_t[ei][:], data0=dA[ei][p][:], data1=u_t[ei][p][:],
                                            initial=carry[d][ec][:, n:n + 1],
                                            op0=ALU.mult, op1=ALU.add), va)
                                    else:
                                        pl.op("vector", lambda e, ei=ei, p=p, ec=ec, d=d, n=n: e.tensor_tensor_scan(
                                            out=h_t[ei][:], data0=dA[ei][p][:, ::-1],
                                            data1=u_t[ei][p][:],
                                            initial=carry[d][ec][:, n:n + 1],
                                            op0=ALU.mult, op1=ALU.add), va)
                                    lgB["dA_v"][ei][p] = va.n
                                    lgB["u_v"][ei][p] = va.n
                                    pl.wait("vector", sCR[p], g_c)
                                    pl.wait("vector", ta, lgB["q_ta"][ei][p])
                                    if d == 0:
                                        t_q = pl.op("vector", lambda e, ei=ei, p=p: e.tensor_mul(
                                            out=q_t[ei][p][:], in0=h_t[ei][:], in1=crep[p][:]), va)
                                    else:
                                        t_q = pl.op("vector", lambda e, ei=ei, p=p: e.tensor_mul(
                                            out=q_t[ei][p][:, ::-1], in0=h_t[ei][:],
                                            in1=crep[p][:, ::-1]), va)
                                    lgB["crep_v"][p] = t_q
                                    pl.op("vector", lambda e, ei=ei, ec=ec, d=d, n=n: e.tensor_max(
                                        out=carry[d][ec][:, n:n + 1],
                                        in0=h_t[ei][:, TCB - 1:TCB],
                                        in1=h_t[ei][:, TCB - 1:TCB]), None)
                                    pl.wait("tensor", va, t_q)
                                    if n == 0:
                                        pl.wait("tensor", aa, lgB["ypsum_a"][ecp])
                                        for bq in ybanks[ei]:
                                            pl.wait("tensor", aa, bank_evac[bq])
                                    for f in range(ECB):
                                        pl.op("tensor", lambda e, ei=ei, p=p, f=f, n=n: e.matmul(
                                            yps[ei][:, f * 512:(f + 1) * 512], ident_bf[:],
                                            q_t[ei][p][:, f * 512:(f + 1) * 512],
                                            start=(n == 0), stop=(n == N - 1)),
                                            ta if f == ECB - 1 else None)
                                    lgB["q_ta"][ei][p] = ta.n
                            stop_t = ta.n
                            for ei, ec in enumerate(ecs):
                                pl.wait("sync", sG[ei], lgB["xcg_ld"][ei])
                                pl.wait("sync", va, lgB["xcg_v"][ei])
                                pl.dma("sync", sG[ei], xcg[ei][:],
                                       xc_sp[d][ec * 128:(ec + 1) * 128, t0:t0 + TCB])
                                tlg = pl.dma("sync", sG[ei], szg[ei][:],
                                             sz_sp[d][ec * 128:(ec + 1) * 128, t0:t0 + TCB])
                                lgB["xcg_ld"][ei] = tlg
                            prev_yg = 0
                            for ei, ec in enumerate(ecs):
                                pl.wait("scalar", ta, stop_t)
                                if ei == 1:
                                    pl.wait("scalar", va, prev_yg)
                                a_ev = pl.op("scalar", lambda e, ei=ei: e.copy(
                                    out=ysv[:], in_=yps[ei][:, 0:TCB]), aa)
                                lgB["ypsum_a"][ecp] = a_ev
                                for bq in ybanks[ei]:
                                    bank_evac[bq] = a_ev
                                    bank_stop[bq] = stop_t
                                pl.wait("vector", aa, a_ev)
                                pl.wait("vector", sG[ei], lgB["xcg_ld"][ei])
                                pl.wait("vector", ta, lgB["yg_ta"])
                                pl.op("vector", lambda e, ei=ei, ec=ec, d=d: e.scalar_tensor_tensor(
                                    out=ysv[:], in0=xcg[ei][:],
                                    scalar=d_c[d][:, ec:ec + 1], in1=ysv[:],
                                    op0=ALU.mult, op1=ALU.add), None)
                                t_yg = pl.op("vector", lambda e, ei=ei, ec=ec: e.tensor_mul(
                                    out=yg[ec][:], in0=ysv[:], in1=szg[ei][:]), va)
                                prev_yg = t_yg
                                lgB["cons_v"][ec][par] = t_yg
                                lgB["xcg_v"][ei] = t_yg
                        pl.wait("tensor", va, va.n)
                        for mt in range(TCB // 128):
                            for f in range(2):
                                bi = grab_bank()
                                for ec in range(NEC):
                                    pl.op("tensor", lambda e, mt=mt, f=f, ec=ec, bi=bi, d=d: e.matmul(
                                        bank_ap[bi],
                                        yg[ec][:, mt * 128:(mt + 1) * 128],
                                        w_outT_bf[d][:, ec * D + f * 512:ec * D + f * 512 + 512],
                                        start=(ec == 0), stop=(ec == NEC - 1)),
                                        ta if ec == NEC - 1 else None)
                                bank_stop[bi] = ta.n
                                if mt == TCB // 128 - 1 and f == 1:
                                    lgB["yg_ta"] = ta.n
                                pr = (mt * 2 + f) % 2
                                pl.wait("scalar", ta, bank_stop[bi])
                                pl.wait("scalar", gP[pr], lgB["part_gp"][pr])
                                a_pv = pl.op("scalar", lambda e, bi=bi, pr=pr: e.copy(
                                    out=part_t[pr][:], in_=bank_ap[bi]), aa)
                                bank_evac[bi] = a_pv
                                pl.wait("gpsimd", aa, a_pv)
                                lgB["part_gp"][pr] = pl.dma(
                                    "gpsimd", gP[pr],
                                    xbuf[t0 + mt * 128:t0 + (mt + 1) * 128,
                                         f * 512:(f + 1) * 512],
                                    part_t[pr][:], accum=(d == 1))
                pl.wait("gpsimd", gP[0])
                pl.wait("gpsimd", gP[1])
                pl.op("gpsimd", lambda e: e.collective_compute(
                    "ReduceScatter", ALU.add, replica_groups=G2,
                    ins=[xbuf.opt()], outs=[mbuf.opt()]), c3)
            barrier()

        with contextlib.ExitStack() as ctxC:
            sbC = lambda nm, shp, dt: ctxC.enter_context(nc.sbuf_tensor(nm, shp, dt))
            stC = sbC("stageC", [128, 1024], F32)
            rails1 = sbC("rails1", [128, 32 * 128], BF16)
            rails2 = sbC("rails2", [128, 128 * 128], BF16)
            cpk = [sbC(f"cpk{p}", [128, 4 * D], BF16) for p in range(4)]
            xnsl = [sbC(f"xnsl{dc}", [128, 256], BF16) for dc in range(8)]
            y1 = sbC("y1", [128, 32 * 256], BF16)
            h2 = [sbC(f"h2_{p}", [128, 256], BF16) for p in range(3)]
            wbc = {nm: sbC(f"bc_{nm}", [128, D], F32) for nm in ("w1", "w2", "bb", "b2")}
            ffn_t = [sbC(f"ffn{jt}", [128, D], F32) for jt in range(2)]
            xtC = [sbC(f"xtC{jt}", [128, D], F32) for jt in range(2)]
            xr2C = [sbC(f"xr2C{jt}", [128, D], F32) for jt in range(2)]
            mbC = [sbC(f"mbC{jt}", [128, D], F32) for jt in range(2)]
            oC = [sbC(f"oC{jt}", [128, D], F32) for jt in range(2)]
            t1C = sbC("t1C", [128, D], F32)
            sqdC = stC[:, 0:1024]
            lnC = {nm: sbC(f"lnC_{nm}", [128, 4], F32)
                   for nm in ("s1", "mu", "s2", "ex2", "mu2", "var", "av", "la", "rr")}

            lgC = {"cpk_ta": [0, 0, 0, 0], "h2_ta": [0, 0, 0],
                   "xnsl_ta": 0, "w1_ta": 0, "out_sp": [0, 0],
                   "slot_a": [0] * 4, "tile_v": [0, 0], "stage_v": 0}

            if "C" in PHASES:
                for i in range(8):
                    pl.wait("sync", va, lgC["stage_v"])
                    t1 = pl.dma("sync", sA, stC[:, 0:512],
                                w_rail1[i * 4:(i + 1) * 4].transpose([1, 0, 2]))
                    pl.wait("vector", sA, t1)
                    lgC["stage_v"] = pl.op("vector", lambda e, i=i: e.tensor_copy(
                        out=rails1[:, i * 512:(i + 1) * 512], in_=stC[:, 0:512]), va)
                for i in range(32):
                    pl.wait("sync", va, lgC["stage_v"])
                    t1 = pl.dma("sync", sA, stC[:, 0:512],
                                w_rail2[i * 4:(i + 1) * 4].transpose([1, 0, 2]))
                    pl.wait("vector", sA, t1)
                    lgC["stage_v"] = pl.op("vector", lambda e, i=i: e.tensor_copy(
                        out=rails2[:, i * 512:(i + 1) * 512], in_=stC[:, 0:512]), va)
                pl.dma("sync", sWB, wbc["w1"][:], ln1_w_row[:].broadcast_to([128, D]))
                pl.dma("sync", sWB, wbc["w2"][:], ln2_w_row[:].broadcast_to([128, D]))
                pl.dma("sync", sWB, wbc["bb"][:], ln1_b_row[:].broadcast_to([128, D]))
                t1 = pl.dma("sync", sWB, wbc["b2"][:], ln2_b_row[:].broadcast_to([128, D]))
                pl.wait("vector", sWB, t1)
                pl.op("vector", lambda e: e.tensor_add(
                    out=wbc["bb"][:], in0=wbc["bb"][:], in1=wbc["b2"][:]), va)
                rails_v = va.n

                for blk in range(TSL // 256):
                    c0 = blk * 256
                    pl.wait("sync", c1, 1)
                    pl.wait("sync", ta, lgC["w1_ta"])
                    xs_t = 0
                    for dc in range(8):
                        xs_t = pl.dma("sync", sXS, xnsl[dc][:],
                                      xnpl_rx[dc * 128:(dc + 1) * 128, c0:c0 + 256])
                    pl.wait("tensor", sXS, xs_t)
                    if blk == 0:
                        pl.wait("tensor", va, rails_v)
                    for h in range(4):
                        for dc in range(8):
                            bi = 4 + (h * 8 + dc) % 2
                            pl.wait("tensor", aa, bank_evac[bi])
                            pl.op("tensor", lambda e, h=h, dc=dc, bi=bi: e.matmul(
                                bank_ap[bi][:, 0:256],
                                rails1[:, (h * 8 + dc) * 128:(h * 8 + dc + 1) * 128],
                                xnsl[dc][:, 0:256], start=True, stop=True), ta)
                            bank_stop[bi] = ta.n
                            pl.wait("scalar", ta, bank_stop[bi])
                            pl.op("scalar", lambda e, h=h, dc=dc, bi=bi: e.copy(
                                out=y1[:, (h * 8 + dc) * 256:(h * 8 + dc + 1) * 256],
                                in_=bank_ap[bi][:, 0:256]), aa)
                            bank_evac[bi] = aa.n
                    lgC["w1_ta"] = ta.n
                    a_y1 = aa.n
                    pl.wait("tensor", aa, a_y1)
                    cp_lds = [0, 0, 0, 0]
                    for kidx in range(128):
                        h, mt = kidx // 32, kidx % 32
                        dc = mt // 4
                        bi = 6 + kidx % 2
                        kb = kidx // 4
                        kp = kb % 4
                        ko = kidx % 4
                        if ko == 0:
                            if blk == 0 and kb < 4:
                                pl.wait("sync", pCPK)
                            pl.wait("sync", ta, lgC["cpk_ta"][kp])
                            cp_lds[kp] = pl.dma(
                                "sync", sCP[kp], cpk[kp][:],
                                cprojbf[kb * 512:(kb + 1) * 512, :].rearrange(
                                    "(f p) c -> p f c", p=128))
                        pl.wait("tensor", aa, bank_evac[bi])
                        pl.op("tensor", lambda e, kidx=kidx, h=h, dc=dc, bi=bi: e.matmul(
                            bank_ap[bi][:, 0:256],
                            rails2[:, kidx * 128:(kidx + 1) * 128],
                            y1[:, (h * 8 + dc) * 256:(h * 8 + dc + 1) * 256],
                            start=True, stop=True), ta)
                        bank_stop[bi] = ta.n
                        hb = kidx % 3
                        pl.wait("scalar", ta, bank_stop[bi])
                        pl.wait("scalar", ta, lgC["h2_ta"][hb])
                        a_h2 = pl.op("scalar", lambda e, bi=bi, hb=hb: e.activation(
                            out=h2[hb][:], in_=bank_ap[bi][:, 0:256], func=AF.Gelu), aa)
                        bank_evac[bi] = a_h2
                        pl.wait("tensor", aa, a_h2)
                        if ko == 0:
                            pl.wait("tensor", sCP[kp], cp_lds[kp])
                        if kidx == 0:
                            pl.wait("tensor", aa, bank_evac[0])
                        for m2 in range(2):
                            for fh in range(2):
                                pl.op("tensor", lambda e, ko=ko, kidx=kidx, m2=m2, fh=fh, hb=hb, kp=kp: e.matmul(
                                    psA[:, (m2 * 2 + fh) * 512:(m2 * 2 + fh + 1) * 512],
                                    h2[hb][:, m2 * 128:(m2 + 1) * 128],
                                    cpk[kp][:, ko * D + fh * 512:ko * D + fh * 512 + 512],
                                    start=(kidx == 0), stop=(kidx == 127)),
                                    ta if (m2 == 1 and fh == 1) else None)
                        lgC["h2_ta"][hb] = ta.n
                        if ko == 3:
                            lgC["cpk_ta"][kp] = ta.n
                    lgC["xnsl_ta"] = ta.n
                    stop_cp = ta.n
                    pl.wait("scalar", ta, stop_cp)
                    for jt in range(2):
                        pl.op("scalar", lambda e, jt=jt: e.copy(
                            out=ffn_t[jt][:], in_=psA[:, jt * 1024:(jt + 1) * 1024]), aa)
                    a_ffn = aa.n
                    for i in range(4):
                        bank_evac[i] = a_ffn
                        bank_stop[i] = stop_cp
                    for jt in range(2):
                        j = blk * 2 + jt
                        slt = j % 4
                        tok = j * 128
                        pl.wait("sync", va, lgC["tile_v"][jt])
                        pl.dma("sync", sF[jt], xtC[jt][:], x_sl[tok:tok + 128, :])
                        pl.wait("sync", c2, 1)
                        pl.dma("sync", sF[jt], xr2C[jt][:], xr2_rx[tok:tok + 128, :])
                        pl.wait("sync", c3, 1)
                        tl = pl.dma("sync", sF[jt], mbC[jt][:], mbuf[tok:tok + 128, :])
                        pl.wait("scalar", sF[jt], tl)
                        pl.wait("scalar", va, lgC["tile_v"][jt])
                        pl.op("scalar", lambda e, jt=jt, slt=slt: e.activation(
                            out=sqdC, in_=mbC[jt][:], func=AF.Square,
                            accum_out=lnC["s2"][:, slt:slt + 1]), aa)
                        a_s2 = aa.n
                        pl.wait("vector", sF[jt], tl)
                        pl.wait("vector", aa, lgC["slot_a"][slt])
                        pl.op("vector", lambda e, jt=jt, slt=slt: e.reduce_sum(
                            out=lnC["s1"][:, slt:slt + 1], in_=mbC[jt][:], axis=AXX), va)
                        pl.wait("vector", va)
                        pl.op("vector", lambda e, slt=slt: e.tensor_scalar(
                            out=lnC["mu"][:, slt:slt + 1], in0=lnC["s1"][:, slt:slt + 1],
                            scalar1=1.0 / D, scalar2=None, op0=ALU.mult), None)
                        pl.wait("vector", aa, a_s2)
                        pl.op("vector", lambda e, slt=slt: e.tensor_scalar(
                            out=lnC["ex2"][:, slt:slt + 1], in0=lnC["s2"][:, slt:slt + 1],
                            scalar1=1.0 / D, scalar2=None, op0=ALU.mult), None)
                        pl.op("vector", lambda e, slt=slt: e.tensor_mul(
                            out=lnC["mu2"][:, slt:slt + 1], in0=lnC["mu"][:, slt:slt + 1],
                            in1=lnC["mu"][:, slt:slt + 1]), va)
                        pl.wait("vector", va)
                        pl.op("vector", lambda e, slt=slt: e.tensor_sub(
                            out=lnC["var"][:, slt:slt + 1], in0=lnC["ex2"][:, slt:slt + 1],
                            in1=lnC["mu2"][:, slt:slt + 1]), va)
                        pl.wait("vector", va)
                        pl.op("vector", lambda e, slt=slt: e.tensor_scalar(
                            out=lnC["av"][:, slt:slt + 1], in0=lnC["var"][:, slt:slt + 1],
                            scalar1=EPS, scalar2=None, op0=ALU.add), va)
                        v_av = va.n
                        pl.wait("scalar", va, v_av)
                        pl.op("scalar", lambda e, slt=slt: e.activation(
                            out=lnC["la"][:, slt:slt + 1], in_=lnC["av"][:, slt:slt + 1],
                            func=AF.Ln), aa)
                        pl.wait("scalar", aa)
                        a_r = pl.op("scalar", lambda e, slt=slt: e.activation(
                            out=lnC["rr"][:, slt:slt + 1], in_=lnC["la"][:, slt:slt + 1],
                            func=AF.Exp, scale=-0.5), aa)
                        lgC["slot_a"][slt] = a_r
                        pl.wait("vector", aa, a_r)
                        pl.wait("vector", pOUT[jt], lgC["out_sp"][jt])
                        pl.op("vector", lambda e, jt=jt, slt=slt: e.tensor_scalar(
                            out=mbC[jt][:], in0=mbC[jt][:],
                            scalar1=lnC["mu"][:, slt:slt + 1],
                            scalar2=lnC["rr"][:, slt:slt + 1],
                            op0=ALU.subtract, op1=ALU.mult), None)
                        pl.op("vector", lambda e, jt=jt: e.tensor_mul(
                            out=mbC[jt][:], in0=mbC[jt][:], in1=wbc["w2"][:]), None)
                        pl.op("vector", lambda e, jt=jt: e.tensor_mul(
                            out=t1C[:], in0=xr2C[jt][:], in1=wbc["w1"][:]), None)
                        pl.op("vector", lambda e, jt=jt: e.tensor_add(
                            out=oC[jt][:], in0=mbC[jt][:], in1=t1C[:]), None)
                        pl.op("vector", lambda e, jt=jt: e.tensor_add(
                            out=oC[jt][:], in0=oC[jt][:], in1=wbc["bb"][:]), None)
                        pl.op("vector", lambda e, jt=jt: e.tensor_add(
                            out=oC[jt][:], in0=oC[jt][:], in1=xtC[jt][:]), None)
                        pl.wait("vector", aa, a_ffn)
                        t_o = pl.op("vector", lambda e, jt=jt: e.tensor_add(
                            out=oC[jt][:], in0=oC[jt][:], in1=ffn_t[jt][:]), va)
                        lgC["tile_v"][jt] = t_o
                        pl.wait("scalar", va, t_o)
                        lgC["out_sp"][jt] = pl.dma(
                            "scalar", pOUT[jt], out_sl[tok:tok + 128, :], oC[jt][:])
                pl.wait("scalar", pOUT[0])
                pl.wait("scalar", pOUT[1])
                pl.op("scalar", lambda e: e.copy(out=bscr[:, 3:4], in_=bscr[:, 3:4]), aa)
                pl.wait("sync", aa)

        with nc.Block() as block:
            @block.sync
            def _(e):
                for fn in pl.q["sync"]:
                    fn(e)

            @block.scalar
            def _(e):
                for fn in pl.q["scalar"]:
                    fn(e)

            @block.vector
            def _(e):
                for fn in pl.q["vector"]:
                    fn(e)

            @block.tensor
            def _(e):
                for fn in pl.q["tensor"]:
                    fn(e)

            @block.gpsimd
            def _(e):
                for fn in pl.q["gpsimd"]:
                    fn(e)

    return nc


_NC = None


def _get_nc():
    global _NC
    if _NC is None:
        _NC = build_nc()
    return _NC


def _in_map(inputs, b, q):
    g = lambda k: np.asarray(inputs[k])
    x = g("L0_attn_x")[b]
    lo, hi = q * EQ, (q + 1) * EQ
    perm = np.concatenate([np.arange(lo, hi), np.arange(0, lo), np.arange(hi, E)])
    m = {}
    m["x"] = np.ascontiguousarray(x)
    m["x_sl"] = np.ascontiguousarray(x[q * TSL:(q + 1) * TSL])
    for d, pre in enumerate(("L0_attn_m1_", "L0_attn_m2_")):
        W_in = g(pre + "W_in")
        m[f"w_inT{d}"] = np.ascontiguousarray(np.concatenate(
            [W_in[:E][perm].T, W_in[E + lo:E + hi].T], axis=1))
        m[f"conv_w{d}"] = np.ascontiguousarray(g(pre + "conv_w")[perm].reshape(16, 128, KC))
        m[f"conv_b{d}"] = np.ascontiguousarray(g(pre + "conv_b")[perm].reshape(16, 128))
        m[f"w_xpT{d}"] = np.ascontiguousarray(g(pre + "W_xp")[:, perm].T)
        m[f"w_dtT{d}"] = np.ascontiguousarray(g(pre + "W_dt")[lo:hi].T)
        m[f"dt_b{d}"] = np.ascontiguousarray(g(pre + "dt_b")[lo:hi].reshape(NEC, 128))
        m[f"a_log{d}"] = np.ascontiguousarray(g(pre + "A_log")[lo:hi].reshape(NEC, 128, N))
        m[f"d_col{d}"] = np.ascontiguousarray(g(pre + "D")[lo:hi].reshape(NEC, 128))
        m[f"w_outT{d}"] = np.ascontiguousarray(g(pre + "W_out")[:, lo:hi].T)
    m["ln1_w8"] = np.ascontiguousarray(g("L0_attn_ln1_w").reshape(8, 128))
    m["ln1_b8"] = np.ascontiguousarray(g("L0_attn_ln1_b").reshape(8, 128))
    m["ln1_w_row"] = np.ascontiguousarray(g("L0_attn_ln1_w").reshape(1, D))
    m["ln1_b_row"] = np.ascontiguousarray(g("L0_attn_ln1_b").reshape(1, D))
    m["ln2_w_row"] = np.ascontiguousarray(g("L0_attn_ln2_w").reshape(1, D))
    m["ln2_b_row"] = np.ascontiguousarray(g("L0_attn_ln2_b").reshape(1, D))
    rm = g("L0_ffn_rail_maker")
    rup = g("L0_ffn_rail_up_proj")
    W1 = np.zeros((4, 8, 128, 128), np.float32)
    for r in range(64):
        dcc, ri = r // 8, r % 8
        W1[:, dcc, ri * 16:(ri + 1) * 16, ri * 16:(ri + 1) * 16] = rm[:, r]
    m["w_rail1"] = np.ascontiguousarray(W1.reshape(32, 128, 128))
    W2 = np.zeros((4, 32, 128, 128), np.float32)
    for mt in range(32):
        for r2 in range(2):
            r = mt * 2 + r2
            ri = r % 8
            W2[:, mt, ri * 16:(ri + 1) * 16, r2 * 64:(r2 + 1) * 64] = \
                rup[:, r].transpose(0, 2, 1)
    m["w_rail2"] = np.ascontiguousarray(W2.reshape(128, 128, 128))
    m["cprojT"] = np.ascontiguousarray(g("L0_ffn_w_cproj").T)
    m["ident"] = np.eye(128, dtype=np.float32)
    return m


def kernel(**inputs):
    nc = _get_nc()
    in_maps = [_in_map(inputs, c // 4, c % 4) for c in range(8)]
    res = run_bass_kernel_spmd(nc, in_maps, core_ids=list(range(8)))
    out = np.empty((2, T, D), np.float32)
    for c in range(8):
        b, q = c // 4, c % 4
        out[b, q * TSL:(q + 1) * TSL] = res.results[c]["out_sl"]
    return out


# version 09 (final)
# speedup vs baseline: 1.5116x; 1.0609x over previous
import contextlib
import numpy as np

import concourse.bass as bass
import concourse.mybir as mybir
from concourse.bass_utils import run_bass_kernel_spmd

F32 = mybir.dt.float32
BF16 = mybir.dt.bfloat16
AF = mybir.ActivationFunctionType
ALU = mybir.AluOpType
AXX = mybir.AxisListType.X

T = 16384
D = 1024
E = 2048
N = 64
KC = 4
EQ = 512
NEC = 4
TCA = 1024
NTCA = T // TCA
TCB = 1024
NTCB = T // TCB
TSL = T // 4
EPS = 1e-5

DEBUG_TAPS = False
PHASES = "ABC"


class Ctr:
    def __init__(self, handle):
        self.h = handle
        self.n = 0

    def inc(self, k=1):
        self.n += k
        return self.n


class Plan:
    def __init__(self):
        self.q = {k: [] for k in ("sync", "scalar", "vector", "tensor", "gpsimd")}

    def wait(self, eng, ctr, tgt=None):
        t = ctr.n if tgt is None else tgt
        if t > 0:
            self.q[eng].append(lambda e, h=ctr.h, t=t: e.wait_ge(h, t))

    def dma(self, eng, ctr, out_ap, in_ap, accum=False, slow_ok=False):
        tgt = ctr.inc(16)
        if accum:
            self.q[eng].append(lambda e, o=out_ap, i=in_ap, h=ctr.h: e.dma_start(
                out=o, in_=i, accum_op=ALU.add).then_inc(h, 16))
        elif slow_ok:
            self.q[eng].append(lambda e, o=out_ap, i=in_ap, h=ctr.h: e.dma_start(
                out=o, in_=i, allow_slow_non_contiguous=True).then_inc(h, 16))
        else:
            self.q[eng].append(lambda e, o=out_ap, i=in_ap, h=ctr.h: e.dma_start(
                out=o, in_=i).then_inc(h, 16))
        return tgt

    def op(self, eng, fn, ctr=None):
        if ctr is None:
            self.q[eng].append(fn)
            return None
        tgt = ctr.inc(1)
        self.q[eng].append(lambda e, f=fn, h=ctr.h: f(e).then_inc(h, 1))
        return tgt


def build_nc():
    nc = bass.Bass("TRN2", target_bir_lowering=False, debug=False)

    def din(name, shape, dt=F32):
        return nc.dram_tensor(name, shape, dt, kind="ExternalInput").ap()

    def dint(name, shape, dt=F32, notap=False):
        kind = "ExternalOutput" if (DEBUG_TAPS and not notap) else "Internal"
        return nc.dram_tensor(name, shape, dt, kind=kind).ap()

    x_in = din("x", [T, D])
    x_sl = din("x_sl", [TSL, D])
    w_inT = [din(f"w_inT{d}", [D, E + EQ]) for d in range(2)]
    conv_wi = [din(f"conv_w{d}", [16, 128, KC]) for d in range(2)]
    conv_bi = [din(f"conv_b{d}", [16, 128]) for d in range(2)]
    w_xpTi = [din(f"w_xpT{d}", [E, 192]) for d in range(2)]
    w_dtTi = [din(f"w_dtT{d}", [N, EQ]) for d in range(2)]
    dt_b4 = [din(f"dt_b{d}", [NEC, 128]) for d in range(2)]
    a_log4 = [din(f"a_log{d}", [NEC, 128, N]) for d in range(2)]
    d_col4 = [din(f"d_col{d}", [NEC, 128]) for d in range(2)]
    w_outTi = [din(f"w_outT{d}", [EQ, D]) for d in range(2)]
    ln1_w8 = din("ln1_w8", [8, 128])
    ln1_b8 = din("ln1_b8", [8, 128])
    ln1_w_row = din("ln1_w_row", [1, D])
    ln1_b_row = din("ln1_b_row", [1, D])
    ln2_w_row = din("ln2_w_row", [1, D])
    ln2_b_row = din("ln2_b_row", [1, D])
    w_rail1 = din("w_rail1", [32, 128, 128])
    w_rail2 = din("w_rail2", [128, 128, 128])
    cprojTi = din("cprojT", [T, D])
    ident_in = din("ident", [128, 128])

    out_sl = nc.dram_tensor("out_sl", [TSL, D], F32, kind="ExternalOutput").ap()

    xnfm_sp = dint("xnfm_sp", [D, T], BF16)
    xnpl_sp = dint("xnpl_sp", [4, D, TSL], BF16, notap=True)
    xr2_sp = dint("xr2_sp", [4, TSL, D], F32, notap=True)
    xnpl_rx = nc.dram_tensor("xnpl_rx", [D, TSL], BF16, kind="Internal").ap()
    xr2_rx = nc.dram_tensor("xr2_rx", [TSL, D], F32, kind="Internal").ap()
    dt_sp = [dint(f"dt_sp{d}", [EQ, T], F32) for d in range(2)]
    w_sp = [dint(f"w_sp{d}", [EQ, T], BF16) for d in range(2)]
    xc_sp = [dint(f"xc_sp{d}", [EQ, T], BF16) for d in range(2)]
    sz_sp = [dint(f"sz_sp{d}", [EQ, T], BF16) for d in range(2)]
    b_sp = [dint(f"b_sp{d}", [N, T], BF16) for d in range(2)]
    c_sp = [dint(f"c_sp{d}", [N, T], BF16) for d in range(2)]
    xbuf = dint("xbuf", [T, D], F32, notap=True)
    mbuf = nc.dram_tensor("mbuf", [TSL, D], F32, kind="Internal").ap()
    cprojbf = nc.dram_tensor("cprojbf", [T, D], BF16, kind="Internal").ap()
    bdummy = nc.dram_tensor("bdummy", [1, 4], F32, kind="Internal").ap()
    xp_tap = dint("xp_tap", [2, EQ, T], F32) if DEBUG_TAPS else None

    pl = Plan()
    G2 = [[0, 1, 2, 3], [4, 5, 6, 7]]

    with contextlib.ExitStack() as ctx:
        S = lambda name: Ctr(ctx.enter_context(nc.semaphore(name)))
        sA = S("sA")
        sM = S("sM")
        sX = [S("sX0"), S("sX1")]
        sNF = S("sNF")
        sDW = [S("sDW0"), S("sDW1")]
        sCR = [S("sCR0"), S("sCR1")]
        sG = [S("sG0"), S("sG1")]
        sCP = [S("sCP0"), S("sCP1"), S("sCP2"), S("sCP3")]
        sXS = S("sXS")
        sF = [S("sF0"), S("sF1")]
        sWB = S("sWB")
        pXR = S("pXR")
        pNF = S("pNF")
        pSZ = S("pSZ")
        pXC = S("pXC")
        pBC = S("pBC")
        pDT = S("pDT")
        pW = S("pW")
        pCPK = S("pCPK")
        pOUT = [S("pOUT0"), S("pOUT1")]
        pTAP = S("pTAP")
        gBR = [S("gBR0"), S("gBR1")]
        gP = [S("gP0"), S("gP1")]
        va = S("va")
        aa = S("aa")
        ta = S("ta")
        gg = S("gg")
        c1 = S("c1")
        c2 = S("c2")
        c3 = S("c3")
        bs = S("bs")

        SYNCQ = [sA, sM] + sX + [sNF] + sDW + sCR + sG + sCP + [sXS] + sF + [sWB]
        SCALQ = [pXR, pNF, pSZ, pXC, pBC, pDT, pW, pCPK, pTAP] + pOUT
        GPQ = gBR + gP

        sb = lambda name, shape, dt: ctx.enter_context(nc.sbuf_tensor(name, shape, dt))

        psA = ctx.enter_context(nc.psum_tensor("psA", [128, 2048], F32))
        psB = ctx.enter_context(nc.psum_tensor("psB", [128, 2048], F32))
        ident_bf = sb("ident_bf", [128, 128], BF16)
        w_xpT_bf = [sb(f"w_xpT_bf{d}", [128, 16 * 192], BF16) for d in range(2)]
        w_dtT_bf = [sb(f"w_dtT_bf{d}", [64, EQ], BF16) for d in range(2)]
        w_outT_bf = [sb(f"w_outT_bf{d}", [128, NEC * D], BF16) for d in range(2)]
        a_dev = [sb(f"a_dev{d}", [128, NEC * N], F32) for d in range(2)]
        convw_c = [sb(f"convw_c{d}", [128, 16 * KC], F32) for d in range(2)]
        convb_c = [sb(f"convb_c{d}", [128, 16], F32) for d in range(2)]
        dtb_c = [sb(f"dtb_c{d}", [128, NEC], F32) for d in range(2)]
        d_c = [sb(f"d_c{d}", [128, NEC], F32) for d in range(2)]
        w1_c = sb("w1_c", [128, 8], F32)
        b1_cbf = sb("b1_cbf", [128, 8], BF16)
        wb1_c = [sb(f"wb1_c{d}", [128, 20], F32) for d in range(2)]
        carry = [[sb(f"carry{d}_{ec}", [128, N], F32) for ec in range(NEC)]
                 for d in range(2)]
        bscr = sb("bscr", [1, 8], F32)

        bank_ap = [psA[:, i * 512:(i + 1) * 512] for i in range(4)] + \
                  [psB[:, i * 512:(i + 1) * 512] for i in range(4)]
        bank_evac = [0] * 8
        bank_stop = [0] * 8
        bank_rr = [0]

        def grab_bank():
            i = bank_rr[0] % 8
            bank_rr[0] += 1
            pl.wait("tensor", aa, bank_evac[i])
            return i

        def barrier():
            for c in SYNCQ:
                pl.wait("sync", c)
            pl.dma("sync", bs, bscr[0:1, 4:8], bdummy[:])
            for c in SCALQ:
                pl.wait("scalar", c)
            pl.op("scalar", lambda e: e.copy(out=bscr[:, 0:1], in_=bscr[:, 0:1]), aa)
            for c in GPQ:
                pl.wait("gpsimd", c)
            pl.op("gpsimd", lambda e: e.tensor_copy(out=bscr[:, 2:3], in_=bscr[:, 2:3]), gg)
            pl.op("vector", lambda e: e.tensor_copy(out=bscr[:, 1:2], in_=bscr[:, 1:2]), va)
            marks = [(aa, aa.n), (va, va.n), (ta, ta.n), (gg, gg.n), (bs, bs.n)]
            for eng in ("sync", "scalar", "vector", "tensor", "gpsimd"):
                for (c, t) in marks:
                    pl.wait(eng, c, t)

        with contextlib.ExitStack() as ctx0:
            stage0 = ctx0.enter_context(nc.sbuf_tensor("stage0", [128, 2560], F32))
            t1 = pl.dma("sync", sA, stage0[:, 0:128], ident_in[:])
            pl.wait("vector", sA, t1)
            pl.op("vector", lambda e: e.tensor_copy(out=ident_bf[:], in_=stage0[:, 0:128]), va)
            for d in range(2):
                for mt in range(16):
                    pl.dma("sync", sM, convw_c[d][:, mt * KC:(mt + 1) * KC], conv_wi[d][mt])
                pl.dma("sync", sM, convb_c[d][:], conv_bi[d].transpose([1, 0]), slow_ok=True)
                pl.dma("sync", sM, dtb_c[d][:], dt_b4[d].transpose([1, 0]), slow_ok=True)
                pl.dma("sync", sM, d_c[d][:], d_col4[d].transpose([1, 0]), slow_ok=True)
                pl.wait("sync", aa)
                pl.wait("sync", va)
                for ec in range(NEC):
                    t1 = pl.dma("sync", sA, stage0[:, 1024 + ec * N:1024 + (ec + 1) * N],
                                a_log4[d][ec])
                pl.wait("scalar", sA, t1)
                pl.op("scalar", lambda e, d=d: e.activation(
                    out=a_dev[d][:], in_=stage0[:, 1024:1024 + NEC * N], func=AF.Exp), aa)
                pl.wait("vector", aa)
                pl.op("vector", lambda e, d=d: e.tensor_scalar(
                    out=a_dev[d][:], in0=a_dev[d][:], scalar1=-1.0, scalar2=None,
                    op0=ALU.mult), va)
                for k in range(16):
                    pl.wait("sync", va)
                    t1 = pl.dma("sync", sA, stage0[:, 0:192],
                                w_xpTi[d][k * 128:(k + 1) * 128, :])
                    pl.wait("vector", sA, t1)
                    pl.op("vector", lambda e, d=d, k=k: e.tensor_copy(
                        out=w_xpT_bf[d][:, k * 192:(k + 1) * 192], in_=stage0[:, 0:192]), va)
                pl.wait("sync", va)
                t1 = pl.dma("sync", sA, stage0[0:64, 0:EQ], w_dtTi[d][:])
                pl.wait("vector", sA, t1)
                pl.op("vector", lambda e, d=d: e.tensor_copy(
                    out=w_dtT_bf[d][:], in_=stage0[0:64, 0:EQ]), va)
                for ec in range(NEC):
                    pl.wait("sync", va)
                    t1 = pl.dma("sync", sA, stage0[:, 0:D],
                                w_outTi[d][ec * 128:(ec + 1) * 128, :])
                    pl.wait("vector", sA, t1)
                    pl.op("vector", lambda e, d=d, ec=ec: e.tensor_copy(
                        out=w_outT_bf[d][:, ec * D:(ec + 1) * D], in_=stage0[:, 0:D]), va)
            pl.dma("sync", sM, w1_c[:], ln1_w8.transpose([1, 0]), slow_ok=True)
            pl.wait("sync", va)
            t1 = pl.dma("sync", sA, stage0[:, 0:8], ln1_b8.transpose([1, 0]), slow_ok=True)
            pl.wait("vector", sA, t1)
            pl.op("vector", lambda e: e.tensor_copy(out=b1_cbf[:], in_=stage0[:, 0:8]), va)
            for d in range(2):
                for ec in range(NEC):
                    pl.op("vector", lambda e, d=d, ec=ec: e.memset(carry[d][ec][:], 0.0), va)
            barrier()

        with contextlib.ExitStack() as ctxA:
            sbA = lambda nm, shp, dt: ctxA.enter_context(nc.sbuf_tensor(nm, shp, dt))
            stage = sbA("stageA", [128, 2560], F32)
            w_in_buf = sbA("w_in_buf", [128, 8 * 2560], BF16)
            xt = [sbA(f"xt{i}", [128, D], F32) for i in range(2)]
            ln_nm = ("s1", "mu", "s2", "ex2", "mu2", "var", "av", "ra", "v2",
                     "bv", "la", "lb", "lab", "rr", "rt")
            ln = {nm: sbA(f"ln_{nm}", [128, 4], F32) for nm in ln_nm}
            xr2_bf = [sbA(f"xr2bf{i}", [128, D], BF16) for i in range(2)]
            xnp_bf = [sbA(f"xnpbf{i}", [128, D], BF16) for i in range(2)]
            xr2_f = sbA("xr2f", [128, D], F32)
            xnfm = [sbA(f"xnfm{dc}", [128, TCA], BF16) for dc in range(8)]
            xnfm_p = [sbA(f"xnfmp{dc}", [128, TCA], BF16) for dc in range(8)]
            xp_buf = [sbA(f"xp{i}", [128, TCA + 3], F32) for i in range(3)]
            cacc = [sbA(f"cacc{i}", [128, TCA], F32) for i in range(2)]
            stash = [sbA(f"stash{mt}", [128, 3], F32) for mt in range(16)]
            xc_all = [sbA(f"xc{mt}", [128, TCA], BF16) for mt in range(16)]
            dtlo_bf = sbA("dtlo_bf", [64, TCA], BF16)
            bc_bf = sbA("bc_bf", [64, TCA], BF16)
            dt_t = [sbA(f"dt_t{i}", [128, TCA], F32) for i in range(2)]
            w_t = sbA("w_tile", [128, TCA], BF16)
            sz_t = sbA("sz_tile", [128, TCA], BF16)
            sqd = stage[:, 0:1024]
            e1_t = stage[:, 1024:1536]

            lg = {"xt_v": [0, 0], "xt_a": [0, 0], "slot_a": [0] * 4, "slot_v": [0] * 4,
                  "xr2f_sp": 0, "xbf_ta": [0, 0], "xnfm_sp": 0, "xnfm_ta": 0,
                  "xp_v": [0, 0, 0], "xp_tap_sp": 0, "cacc_a": 0,
                  "xc_sp": [0] * 4, "xc_ta": 0, "dtlo_ta": 0, "bc_sp": 0,
                  "dt_sp": [0, 0], "dt_v": [0, 0], "w_sp": 0, "sz_sp": 0}

            if "A" in PHASES:
                for d in range(2):
                    pl.wait("sync", aa)
                    pl.wait("sync", va)
                    pl.wait("sync", ta)
                    pl.wait("vector", ta)
                    pl.wait("vector", aa)
                    for k in range(8):
                        pl.wait("sync", va)
                        t1 = pl.dma("sync", sA, stage[:, 0:2560],
                                    w_inT[d][k * 128:(k + 1) * 128, :])
                        pl.wait("vector", sA, t1)
                        pl.op("vector", lambda e, k=k: e.tensor_copy(
                            out=w_in_buf[:, k * 2560:(k + 1) * 2560],
                            in_=stage[:, 0:2560]), va)
                    pl.wait("tensor", va)
                    for mt in range(20):
                        bi = grab_bank()
                        for k in range(8):
                            pl.op("tensor", lambda e, mt=mt, k=k, bi=bi: e.matmul(
                                bank_ap[bi][:, 0:1],
                                w_in_buf[:, k * 2560 + mt * 128:k * 2560 + (mt + 1) * 128],
                                b1_cbf[:, k:k + 1], start=(k == 0), stop=(k == 7)),
                                ta if k == 7 else None)
                        bank_stop[bi] = ta.n
                        pl.wait("scalar", ta, bank_stop[bi])
                        pl.op("scalar", lambda e, mt=mt, bi=bi, d=d: e.copy(
                            out=wb1_c[d][:, mt:mt + 1], in_=bank_ap[bi][:, 0:1]), aa)
                        bank_evac[bi] = aa.n
                    wb1_ta = ta.n
                    for k in range(8):
                        pl.wait("sync", va)
                        t1 = pl.dma("sync", sA, stage[:, 0:2560],
                                    w_inT[d][k * 128:(k + 1) * 128, :])
                        pl.wait("vector", sA, t1)
                        if k == 0:
                            pl.wait("vector", ta, wb1_ta)
                        pl.op("vector", lambda e, k=k: e.tensor_scalar(
                            out=w_in_buf[:, k * 2560:(k + 1) * 2560], in0=stage[:, 0:2560],
                            scalar1=w1_c[:, k:k + 1], scalar2=None, op0=ALU.mult), va)
                    w_in_v = va.n
                    pl.wait("tensor", va, w_in_v)
                    for mt in range(16):
                        pl.op("vector", lambda e, mt=mt: e.memset(stash[mt][:], 0.0), va)

                    for tci in range(NTCA):
                        tc = tci if d == 0 else NTCA - 1 - tci
                        t0 = tc * TCA
                        qq, lt = t0 // TSL, t0 % TSL
                        a_fm = aa.n
                        if d == 0:
                            for j in range(8):
                                jj = j % 2
                                slt = j % 4
                                pl.wait("sync", va, lg["xt_v"][jj])
                                pl.wait("sync", aa, lg["xt_a"][jj])
                                tgt_x = pl.dma("sync", sX[jj], xt[jj][:],
                                               x_in[t0 + j * 128:t0 + (j + 1) * 128, :])
                                pl.wait("vector", sX[jj], tgt_x)
                                pl.wait("vector", aa, lg["slot_a"][slt])
                                pl.op("vector", lambda e, jj=jj, slt=slt: e.reduce_sum(
                                    out=ln["s1"][:, slt:slt + 1], in_=xt[jj][:], axis=AXX), va)
                                pl.wait("vector", va)
                                pl.op("vector", lambda e, slt=slt: e.tensor_scalar(
                                    out=ln["mu"][:, slt:slt + 1], in0=ln["s1"][:, slt:slt + 1],
                                    scalar1=1.0 / D, scalar2=None, op0=ALU.mult), va)
                                pl.wait("scalar", sX[jj], tgt_x)
                                pl.wait("scalar", va, lg["slot_v"][slt])
                                pl.op("scalar", lambda e, jj=jj, slt=slt: e.activation(
                                    out=sqd, in_=xt[jj][:], func=AF.Square,
                                    accum_out=ln["s2"][:, slt:slt + 1]), aa)
                                a_s2 = aa.n
                                lg["xt_a"][jj] = a_s2
                                pl.wait("vector", aa, a_s2)
                                pl.op("vector", lambda e, slt=slt: e.tensor_scalar(
                                    out=ln["ex2"][:, slt:slt + 1], in0=ln["s2"][:, slt:slt + 1],
                                    scalar1=1.0 / D, scalar2=None, op0=ALU.mult), None)
                                pl.op("vector", lambda e, slt=slt: e.tensor_mul(
                                    out=ln["mu2"][:, slt:slt + 1], in0=ln["mu"][:, slt:slt + 1],
                                    in1=ln["mu"][:, slt:slt + 1]), va)
                                pl.wait("vector", va)
                                pl.op("vector", lambda e, slt=slt: e.tensor_sub(
                                    out=ln["var"][:, slt:slt + 1], in0=ln["ex2"][:, slt:slt + 1],
                                    in1=ln["mu2"][:, slt:slt + 1]), va)
                                pl.wait("vector", va)
                                pl.op("vector", lambda e, slt=slt: e.tensor_scalar(
                                    out=ln["av"][:, slt:slt + 1], in0=ln["var"][:, slt:slt + 1],
                                    scalar1=EPS, scalar2=None, op0=ALU.add), va)
                                pl.wait("vector", va)
                                pl.op("vector", lambda e, slt=slt: e.reciprocal(
                                    out=ln["ra"][:, slt:slt + 1], in_=ln["av"][:, slt:slt + 1]), va)
                                pl.wait("vector", va)
                                pl.op("vector", lambda e, slt=slt: e.tensor_mul(
                                    out=ln["v2"][:, slt:slt + 1], in0=ln["var"][:, slt:slt + 1],
                                    in1=ln["ra"][:, slt:slt + 1]), va)
                                pl.wait("vector", va)
                                pl.op("vector", lambda e, slt=slt: e.tensor_scalar(
                                    out=ln["bv"][:, slt:slt + 1], in0=ln["v2"][:, slt:slt + 1],
                                    scalar1=EPS, scalar2=None, op0=ALU.add), va)
                                v_bv = va.n
                                pl.wait("scalar", va, v_bv)
                                pl.op("scalar", lambda e, slt=slt: e.activation(
                                    out=ln["la"][:, slt:slt + 1], in_=ln["av"][:, slt:slt + 1],
                                    func=AF.Ln), aa)
                                pl.op("scalar", lambda e, slt=slt: e.activation(
                                    out=ln["lb"][:, slt:slt + 1], in_=ln["bv"][:, slt:slt + 1],
                                    func=AF.Ln), aa)
                                pl.wait("scalar", aa)
                                pl.op("scalar", lambda e, slt=slt: e.activation(
                                    out=ln["rr"][:, slt:slt + 1], in_=ln["la"][:, slt:slt + 1],
                                    func=AF.Exp, scale=-0.5), aa)
                                a_lb = aa.n
                                pl.wait("vector", aa, a_lb)
                                pl.op("vector", lambda e, slt=slt: e.tensor_add(
                                    out=ln["lab"][:, slt:slt + 1], in0=ln["la"][:, slt:slt + 1],
                                    in1=ln["lb"][:, slt:slt + 1]), va)
                                v_lab = va.n
                                pl.wait("scalar", va, v_lab)
                                pl.op("scalar", lambda e, slt=slt: e.activation(
                                    out=ln["rt"][:, slt:slt + 1], in_=ln["lab"][:, slt:slt + 1],
                                    func=AF.Exp, scale=-0.5), aa)
                                a_rt = aa.n
                                lg["slot_a"][slt] = a_rt
                                pl.wait("vector", aa, a_rt)
                                pl.wait("vector", ta, lg["xbf_ta"][jj])
                                pl.wait("vector", pXR, lg["xr2f_sp"])
                                pl.op("vector", lambda e, jj=jj, slt=slt: e.tensor_scalar(
                                    out=xr2_bf[jj][:], in0=xt[jj][:],
                                    scalar1=ln["mu"][:, slt:slt + 1],
                                    scalar2=ln["rt"][:, slt:slt + 1],
                                    op0=ALU.subtract, op1=ALU.mult), None)
                                pl.op("vector", lambda e, jj=jj, slt=slt: e.tensor_scalar(
                                    out=xnp_bf[jj][:], in0=xt[jj][:],
                                    scalar1=ln["mu"][:, slt:slt + 1],
                                    scalar2=ln["rr"][:, slt:slt + 1],
                                    op0=ALU.subtract, op1=ALU.mult), None)
                                pl.op("vector", lambda e, jj=jj, slt=slt: e.tensor_scalar(
                                    out=xr2_f[:], in0=xt[jj][:],
                                    scalar1=ln["mu"][:, slt:slt + 1],
                                    scalar2=ln["rt"][:, slt:slt + 1],
                                    op0=ALU.subtract, op1=ALU.mult), va)
                                v_xn = va.n
                                lg["slot_v"][slt] = v_xn
                                lg["xt_v"][jj] = v_xn
                                pl.wait("scalar", va, v_xn)
                                lg["xr2f_sp"] = pl.dma(
                                    "scalar", pXR,
                                    xr2_sp[qq, lt + j * 128:lt + (j + 1) * 128, :], xr2_f[:])
                                pl.wait("tensor", va, v_xn)
                                for dc in range(8):
                                    bi = grab_bank()
                                    pl.op("tensor", lambda e, jj=jj, dc=dc, bi=bi: e.transpose(
                                        out=bank_ap[bi].bitcast(BF16)[:, 0:128],
                                        in_=xr2_bf[jj][:, dc * 128:(dc + 1) * 128],
                                        identity=ident_bf[:]), None)
                                    pl.op("tensor", lambda e, jj=jj, dc=dc, bi=bi: e.transpose(
                                        out=bank_ap[bi].bitcast(BF16)[:, 128:256],
                                        in_=xnp_bf[jj][:, dc * 128:(dc + 1) * 128],
                                        identity=ident_bf[:]), ta)
                                    bank_stop[bi] = ta.n
                                    if dc == 7:
                                        lg["xbf_ta"][jj] = ta.n
                                    pl.wait("scalar", ta, bank_stop[bi])
                                    if j == 0 and dc == 0:
                                        pl.wait("scalar", ta, lg["xnfm_ta"])
                                        pl.wait("scalar", pNF, lg["xnfm_sp"])
                                    pl.op("scalar", lambda e, dc=dc, j=j, bi=bi: e.copy(
                                        out=xnfm[dc][:, j * 128:(j + 1) * 128],
                                        in_=bank_ap[bi].bitcast(BF16)[:, 0:128]), None)
                                    pl.op("scalar", lambda e, dc=dc, j=j, bi=bi: e.copy(
                                        out=xnfm_p[dc][:, j * 128:(j + 1) * 128],
                                        in_=bank_ap[bi].bitcast(BF16)[:, 128:256]), aa)
                                    bank_evac[bi] = aa.n
                            a_fm = aa.n
                            pl.wait("scalar", aa, a_fm)
                            for dc in range(8):
                                pl.dma("scalar", pNF, xnfm_sp[dc * 128:(dc + 1) * 128,
                                                              t0:t0 + TCA], xnfm[dc][:])
                                s_t = pl.dma("scalar", pNF,
                                             xnpl_sp[qq, dc * 128:(dc + 1) * 128, lt:lt + TCA],
                                             xnfm_p[dc][:])
                            lg["xnfm_sp"] = s_t
                        else:
                            if tci == 0:
                                pl.wait("sync", pNF)
                            pl.wait("sync", ta, lg["xnfm_ta"])
                            nf_t = 0
                            for dc in range(8):
                                nf_t = pl.dma("sync", sNF, xnfm[dc][:],
                                              xnfm_sp[dc * 128:(dc + 1) * 128, t0:t0 + TCA])
                            pl.wait("tensor", sNF, nf_t)

                        if d == 0:
                            pl.wait("tensor", aa, a_fm)
                        for mt in range(20):
                            for f in range(2):
                                bi = grab_bank()
                                for k in range(8):
                                    pl.op("tensor", lambda e, mt=mt, f=f, k=k, bi=bi: e.matmul(
                                        bank_ap[bi],
                                        w_in_buf[:, k * 2560 + mt * 128:k * 2560 + (mt + 1) * 128],
                                        xnfm[k][:, f * 512:(f + 1) * 512],
                                        start=(k == 0), stop=(k == 7)),
                                        ta if k == 7 else None)
                                bank_stop[bi] = ta.n
                                if mt == 19 and f == 1:
                                    lg["xnfm_ta"] = ta.n
                                pl.wait("scalar", ta, bank_stop[bi])
                                if mt < 16:
                                    off = (3 if d == 0 else 0) + f * 512
                                    xb = mt % 3
                                    if f == 0:
                                        pl.wait("scalar", va, lg["xp_v"][xb])
                                        pl.wait("scalar", pTAP, lg["xp_tap_sp"])
                                    pl.op("scalar", lambda e, bi=bi, mt=mt, off=off, xb=xb, d=d: e.add(
                                        out=xp_buf[xb][:, off:off + 512], in_=bank_ap[bi],
                                        add=wb1_c[d][:, mt:mt + 1]), aa)
                                else:
                                    if f == 0:
                                        pl.wait("scalar", pSZ, lg["sz_sp"])
                                    pl.op("scalar", lambda e, bi=bi, mt=mt, f=f, d=d: e.activation(
                                        out=sz_t[:, f * 512:(f + 1) * 512], in_=bank_ap[bi],
                                        func=AF.Silu, bias=wb1_c[d][:, mt:mt + 1]), aa)
                                bank_evac[bi] = aa.n
                            if mt >= 16:
                                a_now = aa.n
                                pl.wait("scalar", aa, a_now)
                                lg["sz_sp"] = pl.dma(
                                    "scalar", pSZ,
                                    sz_sp[d][(mt - 16) * 128:(mt - 15) * 128, t0:t0 + TCA],
                                    sz_t[:])
                            else:
                                xb = mt % 3
                                a_now = aa.n
                                if DEBUG_TAPS and mt < 4:
                                    pl.wait("scalar", aa, a_now)
                                    off3 = 3 if d == 0 else 0
                                    lg["xp_tap_sp"] = pl.dma(
                                        "scalar", pTAP,
                                        xp_tap[d, mt * 128:(mt + 1) * 128, t0:t0 + TCA],
                                        xp_buf[xb][:, off3:off3 + TCA])
                                pl.wait("vector", aa, a_now)
                                pl.wait("vector", aa, lg["cacc_a"])
                                if d == 0:
                                    pl.op("vector", lambda e, xb=xb, mt=mt: e.tensor_copy(
                                        out=xp_buf[xb][:, 0:3], in_=stash[mt][:]), None)
                                    pl.op("vector", lambda e, xb=xb, mt=mt: e.tensor_copy(
                                        out=stash[mt][:], in_=xp_buf[xb][:, TCA:TCA + 3]), None)
                                else:
                                    pl.op("vector", lambda e, xb=xb, mt=mt: e.tensor_copy(
                                        out=xp_buf[xb][:, TCA:TCA + 3], in_=stash[mt][:]), None)
                                    pl.op("vector", lambda e, xb=xb, mt=mt: e.tensor_copy(
                                        out=stash[mt][:], in_=xp_buf[xb][:, 0:3]), None)
                                tgt_cv = 0
                                for k in range(KC):
                                    coff = k if d == 0 else 3 - k
                                    if k == 0:
                                        pl.op("vector", lambda e, xb=xb, coff=coff, d=d, mt=mt: e.tensor_scalar(
                                            out=cacc[0][:], in0=xp_buf[xb][:, coff:coff + TCA],
                                            scalar1=convw_c[d][:, mt * KC:mt * KC + 1],
                                            scalar2=None, op0=ALU.mult), None)
                                    else:
                                        tgt_cv = pl.op("vector", lambda e, xb=xb, coff=coff, d=d, mt=mt, k=k: e.scalar_tensor_tensor(
                                            out=cacc[k % 2][:],
                                            in0=xp_buf[xb][:, coff:coff + TCA],
                                            scalar=convw_c[d][:, mt * KC + k:mt * KC + k + 1],
                                            in1=cacc[(k + 1) % 2][:],
                                            op0=ALU.mult, op1=ALU.add),
                                            va if k == KC - 1 else None)
                                lg["xp_v"][xb] = tgt_cv
                                pl.wait("scalar", va, tgt_cv)
                                if mt < 4:
                                    pl.wait("scalar", pXC, lg["xc_sp"][mt])
                                pl.wait("scalar", ta, lg["xc_ta"])
                                pl.op("scalar", lambda e, mt=mt, d=d: e.activation(
                                    out=xc_all[mt][:], in_=cacc[1][:], func=AF.Silu,
                                    bias=convb_c[d][:, mt:mt + 1]), aa)
                                lg["cacc_a"] = aa.n
                                if mt < 4:
                                    a_now = aa.n
                                    pl.wait("scalar", aa, a_now)
                                    lg["xc_sp"][mt] = pl.dma(
                                        "scalar", pXC,
                                        xc_sp[d][mt * 128:(mt + 1) * 128, t0:t0 + TCA],
                                        xc_all[mt][:])
                        a_xc = aa.n
                        pl.wait("tensor", aa, a_xc)
                        a_dtlo = 0
                        for mt2 in range(2):
                            rows = 128 if mt2 == 0 else 64
                            for f in range(2):
                                bi = grab_bank()
                                for k in range(16):
                                    pl.op("tensor", lambda e, mt2=mt2, f=f, k=k, bi=bi, d=d, rows=rows: e.matmul(
                                        bank_ap[bi][0:rows, :],
                                        w_xpT_bf[d][:, k * 192 + mt2 * 128:k * 192 + mt2 * 128 + rows],
                                        xc_all[k][:, f * 512:(f + 1) * 512],
                                        start=(k == 0), stop=(k == 15)),
                                        ta if k == 15 else None)
                                bank_stop[bi] = ta.n
                                if mt2 == 1 and f == 1:
                                    lg["xc_ta"] = ta.n
                                pl.wait("scalar", ta, bank_stop[bi])
                                if f == 0:
                                    pl.wait("scalar", pBC, lg["bc_sp"])
                                    if mt2 == 0:
                                        pl.wait("scalar", ta, lg["dtlo_ta"])
                                if mt2 == 0:
                                    pl.op("scalar", lambda e, bi=bi, f=f: e.copy(
                                        out=dtlo_bf[:, f * 512:(f + 1) * 512],
                                        in_=bank_ap[bi][0:64, :]), None)
                                    pl.op("scalar", lambda e, bi=bi, f=f: e.copy(
                                        out=bc_bf[:, f * 512:(f + 1) * 512],
                                        in_=bank_ap[bi][64:128, :]), aa)
                                else:
                                    pl.op("scalar", lambda e, bi=bi, f=f: e.copy(
                                        out=bc_bf[:, f * 512:(f + 1) * 512],
                                        in_=bank_ap[bi][0:64, :]), aa)
                                bank_evac[bi] = aa.n
                            a_now = aa.n
                            if mt2 == 0:
                                a_dtlo = a_now
                            pl.wait("scalar", aa, a_now)
                            lg["bc_sp"] = pl.dma(
                                "scalar", pBC,
                                (b_sp[d] if mt2 == 0 else c_sp[d])[:, t0:t0 + TCA], bc_bf[:])
                        pl.wait("tensor", aa, a_dtlo)
                        for mt in range(NEC):
                            mb = mt % 2
                            for f in range(2):
                                bi = grab_bank()
                                pl.op("tensor", lambda e, mt=mt, f=f, bi=bi, d=d: e.matmul(
                                    bank_ap[bi],
                                    w_dtT_bf[d][:, mt * 128:(mt + 1) * 128],
                                    dtlo_bf[:, f * 512:(f + 1) * 512],
                                    start=True, stop=True), ta)
                                bank_stop[bi] = ta.n
                                if mt == NEC - 1 and f == 1:
                                    lg["dtlo_ta"] = ta.n
                                pl.wait("scalar", ta, bank_stop[bi])
                                if f == 0:
                                    pl.wait("scalar", pDT, lg["dt_sp"][mb])
                                    pl.wait("scalar", va, lg["dt_v"][mb])
                                pl.op("scalar", lambda e, bi=bi, d=d, mt=mt: e.activation(
                                    out=e1_t, in_=bank_ap[bi], func=AF.Exp,
                                    bias=dtb_c[d][:, mt:mt + 1]), None)
                                pl.op("scalar", lambda e, bi=bi, mb=mb, f=f: e.activation(
                                    out=dt_t[mb][:, f * 512:(f + 1) * 512], in_=e1_t,
                                    func=AF.Ln, bias=1.0), aa)
                                bank_evac[bi] = aa.n
                            a_dt = aa.n
                            pl.wait("scalar", aa, a_dt)
                            lg["dt_sp"][mb] = pl.dma(
                                "scalar", pDT, dt_sp[d][mt * 128:(mt + 1) * 128, t0:t0 + TCA],
                                dt_t[mb][:])
                            pl.wait("vector", aa, a_dt)
                            pl.wait("vector", pW, lg["w_sp"])
                            tgt_w = pl.op("vector", lambda e, mb=mb, mt=mt: e.tensor_mul(
                                out=w_t[:], in0=dt_t[mb][:], in1=xc_all[mt][:]), va)
                            lg["dt_v"][mb] = tgt_w
                            pl.wait("scalar", va, tgt_w)
                            lg["w_sp"] = pl.dma(
                                "scalar", pW, w_sp[d][mt * 128:(mt + 1) * 128, t0:t0 + TCA],
                                w_t[:])
                    if d == 0:
                        pl.wait("gpsimd", pNF)
                        pl.wait("gpsimd", pXR)
                        pl.op("gpsimd", lambda e: e.collective_compute(
                            "ReduceScatter", ALU.max, replica_groups=G2,
                            ins=[xnpl_sp.opt()], outs=[xnpl_rx.opt()]), c1)
                        pl.op("gpsimd", lambda e: e.collective_compute(
                            "ReduceScatter", ALU.max, replica_groups=G2,
                            ins=[xr2_sp.opt()], outs=[xr2_rx.opt()]), c2)
                cpk_sp_lg = [0, 0]
                for kk in range(128):
                    pl.wait("sync", va)
                    t1 = pl.dma("sync", sA, stage[:, 0:1024],
                                cprojTi[kk * 128:(kk + 1) * 128, :])
                    pl.wait("vector", sA, t1)
                    pl.wait("vector", pCPK, cpk_sp_lg[kk % 2])
                    if kk < 2:
                        pl.wait("vector", ta, lg["xbf_ta"][kk % 2])
                    t_c = pl.op("vector", lambda e, kk=kk: e.tensor_copy(
                        out=xr2_bf[kk % 2][:], in_=stage[:, 0:1024]), va)
                    pl.wait("scalar", va, t_c)
                    cpk_sp_lg[kk % 2] = pl.dma(
                        "scalar", pCPK, cprojbf[kk * 128:(kk + 1) * 128, :],
                        xr2_bf[kk % 2][:])
            barrier()

        with contextlib.ExitStack() as ctxB:
            sbB = lambda nm, shp, dt: ctxB.enter_context(nc.sbuf_tensor(nm, shp, dt))
            dt_s = [[sbB(f"dtS{ec}_{p}", [128, TCB], F32) for p in range(2)]
                    for ec in range(NEC)]
            wv_s = [[sbB(f"wvS{ec}_{p}", [128, TCB], BF16) for p in range(2)]
                    for ec in range(NEC)]
            xcg = [sbB(f"xcg{i}", [128, TCB], BF16) for i in range(2)]
            szg = [sbB(f"szg{i}", [128, TCB], BF16) for i in range(2)]
            brep = [sbB(f"brep{p}", [128, TCB], BF16) for p in range(2)]
            crep = [sbB(f"crep{p}", [128, TCB], BF16) for p in range(2)]
            dA = [[sbB(f"dA{ei}_{p}", [128, TCB], F32) for p in range(2)]
                  for ei in range(2)]
            u_t = [[sbB(f"u{ei}_{p}", [128, TCB], BF16) for p in range(2)]
                   for ei in range(2)]
            h_t = [sbB(f"h{ei}", [128, TCB], BF16) for ei in range(2)]
            q_t = [[sbB(f"q{ei}_{p}", [128, TCB], BF16) for p in range(2)]
                   for ei in range(2)]
            yg = [sbB(f"yg{ec}", [128, TCB], BF16) for ec in range(NEC)]
            ysv = sbB("ysv", [128, TCB], F32)
            part_t = [sbB(f"part{p}", [128, 512], F32) for p in range(2)]

            lgB = {"ld": [[0, 0] for _ in range(NEC)],
                   "cons_v": [[0, 0] for _ in range(NEC)],
                   "cons_a": [[0, 0] for _ in range(NEC)],
                   "xcg_ld": [0, 0], "xcg_v": [0, 0],
                   "brep_v": [0, 0], "crep_v": [0, 0],
                   "u_v": [[0, 0], [0, 0]], "u_g": [[0, 0], [0, 0]],
                   "brep_g": [0, 0],
                   "cons_g": [[0, 0] for _ in range(NEC)],
                   "dA_v": [[0, 0], [0, 0]], "q_ta": [[0, 0], [0, 0]],
                   "ypsum_a": [0, 0], "yg_ta": 0, "part_gp": [0, 0]}
            ECB = TCB // 512

            if "B" in PHASES:
                for d in range(2):
                    for tci in range(NTCB):
                        tc = tci if d == 0 else NTCB - 1 - tci
                        t0 = tc * TCB
                        par = tci % 2
                        for ec in range(NEC):
                            pl.wait("sync", va, lgB["cons_v"][ec][par])
                            pl.wait("sync", aa, lgB["cons_a"][ec][par])
                            pl.wait("sync", va, lgB["cons_g"][ec][par])
                            pl.dma("sync", sDW[par], dt_s[ec][par][:],
                                   dt_sp[d][ec * 128:(ec + 1) * 128, t0:t0 + TCB])
                            tl = pl.dma("sync", sDW[par], wv_s[ec][par][:],
                                        w_sp[d][ec * 128:(ec + 1) * 128, t0:t0 + TCB])
                            lgB["ld"][ec][par] = tl
                        for ecp in range(2):
                            ecs = (2 * ecp, 2 * ecp + 1)
                            yps = (psA, psB)
                            ybanks = ((0, 1), (4, 5))
                            pl.wait("sync", va, lgB["brep_g"][0])
                            g_b_tgts = {0: pl.dma(
                                "sync", gBR[0], brep[0][:],
                                b_sp[d][0:1, t0:t0 + TCB].broadcast_to([128, TCB]))}
                            for n in range(N):
                                p = n % 2
                                if n + 1 < N:
                                    pl.wait("sync", va, lgB["brep_g"][(n + 1) % 2])
                                    g_b_tgts[n + 1] = pl.dma(
                                        "sync", gBR[(n + 1) % 2], brep[(n + 1) % 2][:],
                                        b_sp[d][n + 1:n + 2, t0:t0 + TCB].broadcast_to([128, TCB]))
                                pl.wait("sync", va, lgB["crep_v"][p])
                                g_c = pl.dma("sync", sCR[p], crep[p][:],
                                             c_sp[d][n:n + 1, t0:t0 + TCB].broadcast_to([128, TCB]))
                                pl.wait("gpsimd", gBR[p], g_b_tgts[n])
                                for ei, ec in enumerate(ecs):
                                    pl.wait("scalar", sDW[par], lgB["ld"][ec][par])
                                    pl.wait("scalar", va, lgB["dA_v"][ei][p])
                                    a_dA = pl.op("scalar", lambda e, ei=ei, p=p, ec=ec, par=par, d=d, n=n: e.activation(
                                        out=dA[ei][p][:], in_=dt_s[ec][par][:], func=AF.Exp,
                                        scale=a_dev[d][:, ec * N + n:ec * N + n + 1]), aa)
                                    if n == N - 1:
                                        lgB["cons_a"][ec][par] = a_dA
                                    pl.wait("vector", gBR[p], g_b_tgts[n])
                                    pl.wait("vector", sDW[par], lgB["ld"][ec][par])
                                    pl.wait("vector", va, lgB["u_v"][ei][p])
                                    if d == 0:
                                        g_u = pl.op("vector", lambda e, ei=ei, p=p, ec=ec, par=par: e.tensor_mul(
                                            out=u_t[ei][p][:], in0=wv_s[ec][par][:],
                                            in1=brep[p][:]), va)
                                    else:
                                        g_u = pl.op("vector", lambda e, ei=ei, p=p, ec=ec, par=par: e.tensor_mul(
                                            out=u_t[ei][p][:], in0=wv_s[ec][par][:, ::-1],
                                            in1=brep[p][:, ::-1]), va)
                                    lgB["u_g"][ei][p] = g_u
                                    lgB["brep_g"][p] = g_u
                                    if n >= N - 2:
                                        lgB["cons_g"][ec][par] = g_u
                                    pl.wait("vector", aa, a_dA)
                                    if d == 0:
                                        pl.op("vector", lambda e, ei=ei, p=p, ec=ec, d=d, n=n: e.tensor_tensor_scan(
                                            out=h_t[ei][:], data0=dA[ei][p][:], data1=u_t[ei][p][:],
                                            initial=carry[d][ec][:, n:n + 1],
                                            op0=ALU.mult, op1=ALU.add), va)
                                    else:
                                        pl.op("vector", lambda e, ei=ei, p=p, ec=ec, d=d, n=n: e.tensor_tensor_scan(
                                            out=h_t[ei][:], data0=dA[ei][p][:, ::-1],
                                            data1=u_t[ei][p][:],
                                            initial=carry[d][ec][:, n:n + 1],
                                            op0=ALU.mult, op1=ALU.add), va)
                                    lgB["dA_v"][ei][p] = va.n
                                    lgB["u_v"][ei][p] = va.n
                                    pl.wait("vector", sCR[p], g_c)
                                    pl.wait("vector", ta, lgB["q_ta"][ei][p])
                                    if d == 0:
                                        t_q = pl.op("vector", lambda e, ei=ei, p=p: e.tensor_mul(
                                            out=q_t[ei][p][:], in0=h_t[ei][:], in1=crep[p][:]), va)
                                    else:
                                        t_q = pl.op("vector", lambda e, ei=ei, p=p: e.tensor_mul(
                                            out=q_t[ei][p][:, ::-1], in0=h_t[ei][:],
                                            in1=crep[p][:, ::-1]), va)
                                    lgB["crep_v"][p] = t_q
                                    pl.op("vector", lambda e, ei=ei, ec=ec, d=d, n=n: e.tensor_max(
                                        out=carry[d][ec][:, n:n + 1],
                                        in0=h_t[ei][:, TCB - 1:TCB],
                                        in1=h_t[ei][:, TCB - 1:TCB]), None)
                                    pl.wait("tensor", va, t_q)
                                    if n == 0:
                                        pl.wait("tensor", aa, lgB["ypsum_a"][ecp])
                                        for bq in ybanks[ei]:
                                            pl.wait("tensor", aa, bank_evac[bq])
                                    for f in range(ECB):
                                        pl.op("tensor", lambda e, ei=ei, p=p, f=f, n=n: e.matmul(
                                            yps[ei][:, f * 512:(f + 1) * 512], ident_bf[:],
                                            q_t[ei][p][:, f * 512:(f + 1) * 512],
                                            start=(n == 0), stop=(n == N - 1)),
                                            ta if f == ECB - 1 else None)
                                    lgB["q_ta"][ei][p] = ta.n
                            stop_t = ta.n
                            for ei, ec in enumerate(ecs):
                                pl.wait("sync", sG[ei], lgB["xcg_ld"][ei])
                                pl.wait("sync", va, lgB["xcg_v"][ei])
                                pl.dma("sync", sG[ei], xcg[ei][:],
                                       xc_sp[d][ec * 128:(ec + 1) * 128, t0:t0 + TCB])
                                tlg = pl.dma("sync", sG[ei], szg[ei][:],
                                             sz_sp[d][ec * 128:(ec + 1) * 128, t0:t0 + TCB])
                                lgB["xcg_ld"][ei] = tlg
                            prev_yg = 0
                            for ei, ec in enumerate(ecs):
                                pl.wait("scalar", ta, stop_t)
                                if ei == 1:
                                    pl.wait("scalar", va, prev_yg)
                                a_ev = pl.op("scalar", lambda e, ei=ei: e.copy(
                                    out=ysv[:], in_=yps[ei][:, 0:TCB]), aa)
                                lgB["ypsum_a"][ecp] = a_ev
                                for bq in ybanks[ei]:
                                    bank_evac[bq] = a_ev
                                    bank_stop[bq] = stop_t
                                pl.wait("vector", aa, a_ev)
                                pl.wait("vector", sG[ei], lgB["xcg_ld"][ei])
                                pl.wait("vector", ta, lgB["yg_ta"])
                                pl.op("vector", lambda e, ei=ei, ec=ec, d=d: e.scalar_tensor_tensor(
                                    out=ysv[:], in0=xcg[ei][:],
                                    scalar=d_c[d][:, ec:ec + 1], in1=ysv[:],
                                    op0=ALU.mult, op1=ALU.add), None)
                                t_yg = pl.op("vector", lambda e, ei=ei, ec=ec: e.tensor_mul(
                                    out=yg[ec][:], in0=ysv[:], in1=szg[ei][:]), va)
                                prev_yg = t_yg
                                lgB["cons_v"][ec][par] = t_yg
                                lgB["xcg_v"][ei] = t_yg
                        pl.wait("tensor", va, va.n)
                        for mt in range(TCB // 128):
                            for f in range(2):
                                bi = grab_bank()
                                for ec in range(NEC):
                                    pl.op("tensor", lambda e, mt=mt, f=f, ec=ec, bi=bi, d=d: e.matmul(
                                        bank_ap[bi],
                                        yg[ec][:, mt * 128:(mt + 1) * 128],
                                        w_outT_bf[d][:, ec * D + f * 512:ec * D + f * 512 + 512],
                                        start=(ec == 0), stop=(ec == NEC - 1)),
                                        ta if ec == NEC - 1 else None)
                                bank_stop[bi] = ta.n
                                if mt == TCB // 128 - 1 and f == 1:
                                    lgB["yg_ta"] = ta.n
                                pr = (mt * 2 + f) % 2
                                pl.wait("scalar", ta, bank_stop[bi])
                                pl.wait("scalar", gP[pr], lgB["part_gp"][pr])
                                a_pv = pl.op("scalar", lambda e, bi=bi, pr=pr: e.copy(
                                    out=part_t[pr][:], in_=bank_ap[bi]), aa)
                                bank_evac[bi] = a_pv
                                pl.wait("gpsimd", aa, a_pv)
                                lgB["part_gp"][pr] = pl.dma(
                                    "gpsimd", gP[pr],
                                    xbuf[t0 + mt * 128:t0 + (mt + 1) * 128,
                                         f * 512:(f + 1) * 512],
                                    part_t[pr][:], accum=(d == 1))
                pl.wait("gpsimd", gP[0])
                pl.wait("gpsimd", gP[1])
                pl.op("gpsimd", lambda e: e.collective_compute(
                    "ReduceScatter", ALU.add, replica_groups=G2,
                    ins=[xbuf.opt()], outs=[mbuf.opt()]), c3)
            barrier()

        with contextlib.ExitStack() as ctxC:
            sbC = lambda nm, shp, dt: ctxC.enter_context(nc.sbuf_tensor(nm, shp, dt))
            stC = sbC("stageC", [128, 1024], F32)
            rails1 = sbC("rails1", [128, 32 * 128], BF16)
            rails2 = sbC("rails2", [128, 128 * 128], BF16)
            cpk = [sbC(f"cpk{p}", [128, 4 * D], BF16) for p in range(2)]
            xnsl = [sbC(f"xnsl{dc}", [128, 256], BF16) for dc in range(8)]
            y1 = sbC("y1", [128, 32 * 256], BF16)
            h2a = sbC("h2a", [128, 64 * 256], BF16)
            wbc = {nm: sbC(f"bc_{nm}", [128, D], F32) for nm in ("w1", "w2", "bb", "b2")}
            ffn_t = [sbC(f"ffn{jt}", [128, D], F32) for jt in range(2)]
            xtC = [sbC(f"xtC{jt}", [128, D], F32) for jt in range(2)]
            xr2C = [sbC(f"xr2C{jt}", [128, D], F32) for jt in range(2)]
            mbC = [sbC(f"mbC{jt}", [128, D], F32) for jt in range(2)]
            oC = [sbC(f"oC{jt}", [128, D], F32) for jt in range(2)]
            t1C = sbC("t1C", [128, D], F32)
            sqdC = stC[:, 0:1024]
            lnC = {nm: sbC(f"lnC_{nm}", [128, 4], F32)
                   for nm in ("s1", "mu", "s2", "ex2", "mu2", "var", "av", "la", "rr")}

            lgC = {"cpk_ta": [0, 0], "h2_ta": 0,
                   "xnsl_ta": 0, "w1_ta": 0, "out_sp": [0, 0],
                   "slot_a": [0] * 4, "tile_v": [0, 0], "stage_v": 0}

            if "C" in PHASES:
                for i in range(8):
                    pl.wait("sync", va, lgC["stage_v"])
                    t1 = pl.dma("sync", sA, stC[:, 0:512],
                                w_rail1[i * 4:(i + 1) * 4].transpose([1, 0, 2]))
                    pl.wait("vector", sA, t1)
                    lgC["stage_v"] = pl.op("vector", lambda e, i=i: e.tensor_copy(
                        out=rails1[:, i * 512:(i + 1) * 512], in_=stC[:, 0:512]), va)
                for i in range(32):
                    pl.wait("sync", va, lgC["stage_v"])
                    t1 = pl.dma("sync", sA, stC[:, 0:512],
                                w_rail2[i * 4:(i + 1) * 4].transpose([1, 0, 2]))
                    pl.wait("vector", sA, t1)
                    lgC["stage_v"] = pl.op("vector", lambda e, i=i: e.tensor_copy(
                        out=rails2[:, i * 512:(i + 1) * 512], in_=stC[:, 0:512]), va)
                pl.dma("sync", sWB, wbc["w1"][:], ln1_w_row[:].broadcast_to([128, D]))
                pl.dma("sync", sWB, wbc["w2"][:], ln2_w_row[:].broadcast_to([128, D]))
                pl.dma("sync", sWB, wbc["bb"][:], ln1_b_row[:].broadcast_to([128, D]))
                t1 = pl.dma("sync", sWB, wbc["b2"][:], ln2_b_row[:].broadcast_to([128, D]))
                pl.wait("vector", sWB, t1)
                pl.op("vector", lambda e: e.tensor_add(
                    out=wbc["bb"][:], in0=wbc["bb"][:], in1=wbc["b2"][:]), va)
                rails_v = va.n

                for blk in range(TSL // 256):
                    c0 = blk * 256
                    pl.wait("sync", c1, 1)
                    pl.wait("sync", ta, lgC["w1_ta"])
                    xs_t = 0
                    for dc in range(8):
                        xs_t = pl.dma("sync", sXS, xnsl[dc][:],
                                      xnpl_rx[dc * 128:(dc + 1) * 128, c0:c0 + 256])
                    pl.wait("tensor", sXS, xs_t)
                    if blk == 0:
                        pl.wait("tensor", va, rails_v)
                    for h in range(4):
                        for dc in range(8):
                            bi = 4 + (h * 8 + dc) % 4
                            pl.wait("tensor", aa, bank_evac[bi])
                            pl.op("tensor", lambda e, h=h, dc=dc, bi=bi: e.matmul(
                                bank_ap[bi][:, 0:256],
                                rails1[:, (h * 8 + dc) * 128:(h * 8 + dc + 1) * 128],
                                xnsl[dc][:, 0:256], start=True, stop=True), ta)
                            bank_stop[bi] = ta.n
                            pl.wait("scalar", ta, bank_stop[bi])
                            pl.op("scalar", lambda e, h=h, dc=dc, bi=bi: e.copy(
                                out=y1[:, (h * 8 + dc) * 256:(h * 8 + dc + 1) * 256],
                                in_=bank_ap[bi][:, 0:256]), aa)
                            bank_evac[bi] = aa.n
                    lgC["w1_ta"] = ta.n
                    a_y1 = aa.n
                    pl.wait("tensor", aa, a_y1)
                    cp_lds = [0, 0]
                    for half in range(2):
                        kbase = half * 64
                        pl.wait("scalar", ta, lgC["h2_ta"])
                        for ki in range(64):
                            kidx = kbase + ki
                            h, mt = kidx // 32, kidx % 32
                            dc = mt // 4
                            bi = 4 + ki % 4
                            pl.wait("tensor", aa, bank_evac[bi])
                            pl.op("tensor", lambda e, kidx=kidx, h=h, dc=dc, bi=bi: e.matmul(
                                bank_ap[bi][:, 0:256],
                                rails2[:, kidx * 128:(kidx + 1) * 128],
                                y1[:, (h * 8 + dc) * 256:(h * 8 + dc + 1) * 256],
                                start=True, stop=True), ta)
                            bank_stop[bi] = ta.n
                            pl.wait("scalar", ta, bank_stop[bi])
                            a_h2 = pl.op("scalar", lambda e, bi=bi, ki=ki: e.activation(
                                out=h2a[:, ki * 256:(ki + 1) * 256],
                                in_=bank_ap[bi][:, 0:256], func=AF.Gelu), aa)
                            bank_evac[bi] = a_h2
                        a_h2_all = aa.n
                        pl.wait("tensor", aa, a_h2_all)
                        for ki in range(64):
                            kidx = kbase + ki
                            kb = kidx // 4
                            kp = kb % 2
                            ko = kidx % 4
                            if ko == 0:
                                if blk == 0 and kb < 2:
                                    pl.wait("sync", pCPK)
                                pl.wait("sync", ta, lgC["cpk_ta"][kp])
                                cp_lds[kp] = pl.dma(
                                    "sync", sCP[kp], cpk[kp][:],
                                    cprojbf[kb * 512:(kb + 1) * 512, :].rearrange(
                                        "(f p) c -> p f c", p=128))
                            if ko == 0:
                                pl.wait("tensor", sCP[kp], cp_lds[kp])
                            if kidx == 0:
                                pl.wait("tensor", aa, bank_evac[0])
                            for m2 in range(2):
                                for fh in range(2):
                                    pl.op("tensor", lambda e, ko=ko, ki=ki, kidx=kidx, m2=m2, fh=fh, kp=kp: e.matmul(
                                        psA[:, (m2 * 2 + fh) * 512:(m2 * 2 + fh + 1) * 512],
                                        h2a[:, ki * 256 + m2 * 128:ki * 256 + (m2 + 1) * 128],
                                        cpk[kp][:, ko * D + fh * 512:ko * D + fh * 512 + 512],
                                        start=(kidx == 0), stop=(kidx == 127)),
                                        ta if (m2 == 1 and fh == 1) else None)
                            if ko == 3:
                                lgC["cpk_ta"][kp] = ta.n
                        lgC["h2_ta"] = ta.n
                    lgC["xnsl_ta"] = ta.n
                    stop_cp = ta.n
                    pl.wait("scalar", ta, stop_cp)
                    for jt in range(2):
                        pl.op("scalar", lambda e, jt=jt: e.copy(
                            out=ffn_t[jt][:], in_=psA[:, jt * 1024:(jt + 1) * 1024]), aa)
                    a_ffn = aa.n
                    for i in range(4):
                        bank_evac[i] = a_ffn
                        bank_stop[i] = stop_cp
                    for jt in range(2):
                        j = blk * 2 + jt
                        slt = j % 4
                        tok = j * 128
                        pl.wait("gpsimd", va, lgC["tile_v"][jt])
                        pl.dma("gpsimd", sF[jt], xtC[jt][:], x_sl[tok:tok + 128, :])
                        pl.wait("gpsimd", c2, 1)
                        pl.dma("gpsimd", sF[jt], xr2C[jt][:], xr2_rx[tok:tok + 128, :])
                        pl.wait("gpsimd", c3, 1)
                        tl = pl.dma("gpsimd", sF[jt], mbC[jt][:], mbuf[tok:tok + 128, :])
                        pl.wait("scalar", sF[jt], tl)
                        pl.wait("scalar", va, lgC["tile_v"][jt])
                        pl.op("scalar", lambda e, jt=jt, slt=slt: e.activation(
                            out=sqdC, in_=mbC[jt][:], func=AF.Square,
                            accum_out=lnC["s2"][:, slt:slt + 1]), aa)
                        a_s2 = aa.n
                        pl.wait("vector", sF[jt], tl)
                        pl.wait("vector", aa, lgC["slot_a"][slt])
                        pl.op("vector", lambda e, jt=jt, slt=slt: e.reduce_sum(
                            out=lnC["s1"][:, slt:slt + 1], in_=mbC[jt][:], axis=AXX), va)
                        pl.wait("vector", va)
                        pl.op("vector", lambda e, slt=slt: e.tensor_scalar(
                            out=lnC["mu"][:, slt:slt + 1], in0=lnC["s1"][:, slt:slt + 1],
                            scalar1=1.0 / D, scalar2=None, op0=ALU.mult), None)
                        pl.wait("vector", aa, a_s2)
                        pl.op("vector", lambda e, slt=slt: e.tensor_scalar(
                            out=lnC["ex2"][:, slt:slt + 1], in0=lnC["s2"][:, slt:slt + 1],
                            scalar1=1.0 / D, scalar2=None, op0=ALU.mult), None)
                        pl.op("vector", lambda e, slt=slt: e.tensor_mul(
                            out=lnC["mu2"][:, slt:slt + 1], in0=lnC["mu"][:, slt:slt + 1],
                            in1=lnC["mu"][:, slt:slt + 1]), va)
                        pl.wait("vector", va)
                        pl.op("vector", lambda e, slt=slt: e.tensor_sub(
                            out=lnC["var"][:, slt:slt + 1], in0=lnC["ex2"][:, slt:slt + 1],
                            in1=lnC["mu2"][:, slt:slt + 1]), va)
                        pl.wait("vector", va)
                        pl.op("vector", lambda e, slt=slt: e.tensor_scalar(
                            out=lnC["av"][:, slt:slt + 1], in0=lnC["var"][:, slt:slt + 1],
                            scalar1=EPS, scalar2=None, op0=ALU.add), va)
                        v_av = va.n
                        pl.wait("scalar", va, v_av)
                        pl.op("scalar", lambda e, slt=slt: e.activation(
                            out=lnC["la"][:, slt:slt + 1], in_=lnC["av"][:, slt:slt + 1],
                            func=AF.Ln), aa)
                        pl.wait("scalar", aa)
                        a_r = pl.op("scalar", lambda e, slt=slt: e.activation(
                            out=lnC["rr"][:, slt:slt + 1], in_=lnC["la"][:, slt:slt + 1],
                            func=AF.Exp, scale=-0.5), aa)
                        lgC["slot_a"][slt] = a_r
                        pl.wait("vector", aa, a_r)
                        pl.wait("vector", pOUT[jt], lgC["out_sp"][jt])
                        pl.op("vector", lambda e, jt=jt, slt=slt: e.tensor_scalar(
                            out=mbC[jt][:], in0=mbC[jt][:],
                            scalar1=lnC["mu"][:, slt:slt + 1],
                            scalar2=lnC["rr"][:, slt:slt + 1],
                            op0=ALU.subtract, op1=ALU.mult), None)
                        pl.op("vector", lambda e, jt=jt: e.tensor_mul(
                            out=mbC[jt][:], in0=mbC[jt][:], in1=wbc["w2"][:]), None)
                        pl.op("vector", lambda e, jt=jt: e.tensor_mul(
                            out=t1C[:], in0=xr2C[jt][:], in1=wbc["w1"][:]), None)
                        pl.op("vector", lambda e, jt=jt: e.tensor_add(
                            out=oC[jt][:], in0=mbC[jt][:], in1=t1C[:]), None)
                        pl.op("vector", lambda e, jt=jt: e.tensor_add(
                            out=oC[jt][:], in0=oC[jt][:], in1=wbc["bb"][:]), None)
                        pl.op("vector", lambda e, jt=jt: e.tensor_add(
                            out=oC[jt][:], in0=oC[jt][:], in1=xtC[jt][:]), None)
                        pl.wait("vector", aa, a_ffn)
                        t_o = pl.op("vector", lambda e, jt=jt: e.tensor_add(
                            out=oC[jt][:], in0=oC[jt][:], in1=ffn_t[jt][:]), va)
                        lgC["tile_v"][jt] = t_o
                        pl.wait("scalar", va, t_o)
                        lgC["out_sp"][jt] = pl.dma(
                            "scalar", pOUT[jt], out_sl[tok:tok + 128, :], oC[jt][:])
                pl.wait("scalar", pOUT[0])
                pl.wait("scalar", pOUT[1])
                pl.op("scalar", lambda e: e.copy(out=bscr[:, 3:4], in_=bscr[:, 3:4]), aa)
                pl.wait("sync", aa)

        with nc.Block() as block:
            @block.sync
            def _(e):
                for fn in pl.q["sync"]:
                    fn(e)

            @block.scalar
            def _(e):
                for fn in pl.q["scalar"]:
                    fn(e)

            @block.vector
            def _(e):
                for fn in pl.q["vector"]:
                    fn(e)

            @block.tensor
            def _(e):
                for fn in pl.q["tensor"]:
                    fn(e)

            @block.gpsimd
            def _(e):
                for fn in pl.q["gpsimd"]:
                    fn(e)

    return nc


_NC = None


def _get_nc():
    global _NC
    if _NC is None:
        _NC = build_nc()
    return _NC


def _in_map(inputs, b, q):
    g = lambda k: np.asarray(inputs[k])
    x = g("L0_attn_x")[b]
    lo, hi = q * EQ, (q + 1) * EQ
    perm = np.concatenate([np.arange(lo, hi), np.arange(0, lo), np.arange(hi, E)])
    m = {}
    m["x"] = np.ascontiguousarray(x)
    m["x_sl"] = np.ascontiguousarray(x[q * TSL:(q + 1) * TSL])
    for d, pre in enumerate(("L0_attn_m1_", "L0_attn_m2_")):
        W_in = g(pre + "W_in")
        m[f"w_inT{d}"] = np.ascontiguousarray(np.concatenate(
            [W_in[:E][perm].T, W_in[E + lo:E + hi].T], axis=1))
        m[f"conv_w{d}"] = np.ascontiguousarray(g(pre + "conv_w")[perm].reshape(16, 128, KC))
        m[f"conv_b{d}"] = np.ascontiguousarray(g(pre + "conv_b")[perm].reshape(16, 128))
        m[f"w_xpT{d}"] = np.ascontiguousarray(g(pre + "W_xp")[:, perm].T)
        m[f"w_dtT{d}"] = np.ascontiguousarray(g(pre + "W_dt")[lo:hi].T)
        m[f"dt_b{d}"] = np.ascontiguousarray(g(pre + "dt_b")[lo:hi].reshape(NEC, 128))
        m[f"a_log{d}"] = np.ascontiguousarray(g(pre + "A_log")[lo:hi].reshape(NEC, 128, N))
        m[f"d_col{d}"] = np.ascontiguousarray(g(pre + "D")[lo:hi].reshape(NEC, 128))
        m[f"w_outT{d}"] = np.ascontiguousarray(g(pre + "W_out")[:, lo:hi].T)
    m["ln1_w8"] = np.ascontiguousarray(g("L0_attn_ln1_w").reshape(8, 128))
    m["ln1_b8"] = np.ascontiguousarray(g("L0_attn_ln1_b").reshape(8, 128))
    m["ln1_w_row"] = np.ascontiguousarray(g("L0_attn_ln1_w").reshape(1, D))
    m["ln1_b_row"] = np.ascontiguousarray(g("L0_attn_ln1_b").reshape(1, D))
    m["ln2_w_row"] = np.ascontiguousarray(g("L0_attn_ln2_w").reshape(1, D))
    m["ln2_b_row"] = np.ascontiguousarray(g("L0_attn_ln2_b").reshape(1, D))
    rm = g("L0_ffn_rail_maker")
    rup = g("L0_ffn_rail_up_proj")
    W1 = np.zeros((4, 8, 128, 128), np.float32)
    for r in range(64):
        dcc, ri = r // 8, r % 8
        W1[:, dcc, ri * 16:(ri + 1) * 16, ri * 16:(ri + 1) * 16] = rm[:, r]
    m["w_rail1"] = np.ascontiguousarray(W1.reshape(32, 128, 128))
    W2 = np.zeros((4, 32, 128, 128), np.float32)
    for mt in range(32):
        for r2 in range(2):
            r = mt * 2 + r2
            ri = r % 8
            W2[:, mt, ri * 16:(ri + 1) * 16, r2 * 64:(r2 + 1) * 64] = \
                rup[:, r].transpose(0, 2, 1)
    m["w_rail2"] = np.ascontiguousarray(W2.reshape(128, 128, 128))
    m["cprojT"] = np.ascontiguousarray(g("L0_ffn_w_cproj").T)
    m["ident"] = np.eye(128, dtype=np.float32)
    return m


def kernel(**inputs):
    nc = _get_nc()
    in_maps = [_in_map(inputs, c // 4, c % 4) for c in range(8)]
    res = run_bass_kernel_spmd(nc, in_maps, core_ids=list(range(8)))
    out = np.empty((2, T, D), np.float32)
    for c in range(8):
        b, q = c // 4, c % 4
        out[b, q * TSL:(q + 1) * TSL] = res.results[c]["out_sl"]
    return out


# version 10
# speedup vs baseline: 1.5210x; 1.0062x over previous
import contextlib
import numpy as np

import concourse.bass as bass
import concourse.mybir as mybir
from concourse.bass_utils import run_bass_kernel_spmd

F32 = mybir.dt.float32
BF16 = mybir.dt.bfloat16
AF = mybir.ActivationFunctionType
ALU = mybir.AluOpType
AXX = mybir.AxisListType.X

T = 16384
D = 1024
E = 2048
N = 64
KC = 4
EQ = 512
NEC = 4
TCA = 1024
NTCA = T // TCA
TCB = 1024
NTCB = T // TCB
TSL = T // 4
EPS = 1e-5

DEBUG_TAPS = False
PHASES = "ABC"


class Ctr:
    def __init__(self, handle):
        self.h = handle
        self.n = 0

    def inc(self, k=1):
        self.n += k
        return self.n


class Plan:
    def __init__(self):
        self.q = {k: [] for k in ("sync", "scalar", "vector", "tensor", "gpsimd")}

    def wait(self, eng, ctr, tgt=None):
        t = ctr.n if tgt is None else tgt
        if t > 0:
            self.q[eng].append(lambda e, h=ctr.h, t=t: e.wait_ge(h, t))

    def dma(self, eng, ctr, out_ap, in_ap, accum=False, slow_ok=False):
        tgt = ctr.inc(16)
        if accum:
            self.q[eng].append(lambda e, o=out_ap, i=in_ap, h=ctr.h: e.dma_start(
                out=o, in_=i, accum_op=ALU.add).then_inc(h, 16))
        elif slow_ok:
            self.q[eng].append(lambda e, o=out_ap, i=in_ap, h=ctr.h: e.dma_start(
                out=o, in_=i, allow_slow_non_contiguous=True).then_inc(h, 16))
        else:
            self.q[eng].append(lambda e, o=out_ap, i=in_ap, h=ctr.h: e.dma_start(
                out=o, in_=i).then_inc(h, 16))
        return tgt

    def op(self, eng, fn, ctr=None):
        if ctr is None:
            self.q[eng].append(fn)
            return None
        tgt = ctr.inc(1)
        self.q[eng].append(lambda e, f=fn, h=ctr.h: f(e).then_inc(h, 1))
        return tgt


def build_nc():
    nc = bass.Bass("TRN2", target_bir_lowering=False, debug=False)

    def din(name, shape, dt=F32):
        return nc.dram_tensor(name, shape, dt, kind="ExternalInput").ap()

    def dint(name, shape, dt=F32, notap=False):
        kind = "ExternalOutput" if (DEBUG_TAPS and not notap) else "Internal"
        return nc.dram_tensor(name, shape, dt, kind=kind).ap()

    x_in = din("x", [T, D])
    x_sl = din("x_sl", [TSL, D])
    w_inT = [din(f"w_inT{d}", [D, E + EQ]) for d in range(2)]
    conv_wi = [din(f"conv_w{d}", [16, 128, KC]) for d in range(2)]
    conv_bi = [din(f"conv_b{d}", [16, 128]) for d in range(2)]
    w_xpTi = [din(f"w_xpT{d}", [E, 192]) for d in range(2)]
    w_dtTi = [din(f"w_dtT{d}", [N, EQ]) for d in range(2)]
    dt_b4 = [din(f"dt_b{d}", [NEC, 128]) for d in range(2)]
    a_log4 = [din(f"a_log{d}", [NEC, 128, N]) for d in range(2)]
    d_col4 = [din(f"d_col{d}", [NEC, 128]) for d in range(2)]
    w_outTi = [din(f"w_outT{d}", [EQ, D]) for d in range(2)]
    ln1_w8 = din("ln1_w8", [8, 128])
    ln1_b8 = din("ln1_b8", [8, 128])
    ln1_w_row = din("ln1_w_row", [1, D])
    ln1_b_row = din("ln1_b_row", [1, D])
    ln2_w_row = din("ln2_w_row", [1, D])
    ln2_b_row = din("ln2_b_row", [1, D])
    w_rail1 = din("w_rail1", [32, 128, 128])
    w_rail2 = din("w_rail2", [128, 128, 128])
    cprojTi = din("cprojT", [T, D])
    ident_in = din("ident", [128, 128])

    out_sl = nc.dram_tensor("out_sl", [TSL, D], F32, kind="ExternalOutput").ap()

    xnfm_sp = dint("xnfm_sp", [D, T], BF16)
    xnpl_sp = dint("xnpl_sp", [4, D, TSL], BF16, notap=True)
    xr2_sp = dint("xr2_sp", [4, TSL, D], F32, notap=True)
    xnpl_rx = nc.dram_tensor("xnpl_rx", [D, TSL], BF16, kind="Internal").ap()
    xr2_rx = nc.dram_tensor("xr2_rx", [TSL, D], F32, kind="Internal").ap()
    dt_sp = [dint(f"dt_sp{d}", [EQ, T], F32) for d in range(2)]
    w_sp = [dint(f"w_sp{d}", [EQ, T], BF16) for d in range(2)]
    xc_sp = [dint(f"xc_sp{d}", [EQ, T], BF16) for d in range(2)]
    sz_sp = [dint(f"sz_sp{d}", [EQ, T], BF16) for d in range(2)]
    b_sp = [dint(f"b_sp{d}", [N, T], BF16) for d in range(2)]
    c_sp = [dint(f"c_sp{d}", [N, T], BF16) for d in range(2)]
    xbuf = dint("xbuf", [T, D], F32, notap=True)
    mbuf = nc.dram_tensor("mbuf", [TSL, D], F32, kind="Internal").ap()
    cprojbf = nc.dram_tensor("cprojbf", [T, D], BF16, kind="Internal").ap()
    bdummy = nc.dram_tensor("bdummy", [1, 4], F32, kind="Internal").ap()
    xp_tap = dint("xp_tap", [2, EQ, T], F32) if DEBUG_TAPS else None

    pl = Plan()
    G2 = [[0, 1, 2, 3], [4, 5, 6, 7]]

    with contextlib.ExitStack() as ctx:
        S = lambda name: Ctr(ctx.enter_context(nc.semaphore(name)))
        sA = S("sA")
        sM = S("sM")
        sX = [S("sX0"), S("sX1")]
        sNF = S("sNF")
        sDW = [S("sDW0"), S("sDW1")]
        sCR = [S("sCR0"), S("sCR1")]
        sG = [S("sG0"), S("sG1")]
        sCP = [S("sCP0"), S("sCP1"), S("sCP2"), S("sCP3")]
        sXS = S("sXS")
        sF = [S("sF0"), S("sF1")]
        sWB = S("sWB")
        pXR = S("pXR")
        pNF = S("pNF")
        pSZ = S("pSZ")
        pXC = S("pXC")
        pBC = S("pBC")
        pDT = S("pDT")
        pW = S("pW")
        pCPK = S("pCPK")
        pOUT = [S("pOUT0"), S("pOUT1")]
        pTAP = S("pTAP")
        gBR = [S("gBR0"), S("gBR1")]
        gP = [S("gP0"), S("gP1")]
        va = S("va")
        aa = S("aa")
        ta = S("ta")
        gg = S("gg")
        c1 = S("c1")
        c2 = S("c2")
        c3 = S("c3")
        bs = S("bs")

        SYNCQ = [sA, sM] + sX + [sNF] + sDW + sCR + sG + sCP + [sXS] + sF + [sWB]
        SCALQ = [pXR, pNF, pSZ, pXC, pBC, pDT, pW, pCPK, pTAP] + pOUT
        GPQ = gBR + gP

        sb = lambda name, shape, dt: ctx.enter_context(nc.sbuf_tensor(name, shape, dt))

        psA = ctx.enter_context(nc.psum_tensor("psA", [128, 2048], F32))
        psB = ctx.enter_context(nc.psum_tensor("psB", [128, 2048], F32))
        ident_bf = sb("ident_bf", [128, 128], BF16)
        w_xpT_bf = [sb(f"w_xpT_bf{d}", [128, 16 * 192], BF16) for d in range(2)]
        w_dtT_bf = [sb(f"w_dtT_bf{d}", [64, EQ], BF16) for d in range(2)]
        w_outT_bf = [sb(f"w_outT_bf{d}", [128, NEC * D], BF16) for d in range(2)]
        a_dev = [sb(f"a_dev{d}", [128, NEC * N], F32) for d in range(2)]
        convw_c = [sb(f"convw_c{d}", [128, 16 * KC], F32) for d in range(2)]
        convb_c = [sb(f"convb_c{d}", [128, 16], F32) for d in range(2)]
        dtb_c = [sb(f"dtb_c{d}", [128, NEC], F32) for d in range(2)]
        d_c = [sb(f"d_c{d}", [128, NEC], F32) for d in range(2)]
        w1_c = sb("w1_c", [128, 8], F32)
        b1_cbf = sb("b1_cbf", [128, 8], BF16)
        wb1_c = [sb(f"wb1_c{d}", [128, 20], F32) for d in range(2)]
        carry = [[sb(f"carry{d}_{ec}", [128, N], F32) for ec in range(NEC)]
                 for d in range(2)]
        bscr = sb("bscr", [1, 8], F32)

        bank_ap = [psA[:, i * 512:(i + 1) * 512] for i in range(4)] + \
                  [psB[:, i * 512:(i + 1) * 512] for i in range(4)]
        bank_evac = [0] * 8
        bank_stop = [0] * 8
        bank_rr = [0]

        def grab_bank():
            i = bank_rr[0] % 8
            bank_rr[0] += 1
            pl.wait("tensor", aa, bank_evac[i])
            return i

        def barrier():
            for c in SYNCQ:
                pl.wait("sync", c)
            pl.dma("sync", bs, bscr[0:1, 4:8], bdummy[:])
            for c in SCALQ:
                pl.wait("scalar", c)
            pl.op("scalar", lambda e: e.copy(out=bscr[:, 0:1], in_=bscr[:, 0:1]), aa)
            for c in GPQ:
                pl.wait("gpsimd", c)
            pl.op("gpsimd", lambda e: e.tensor_copy(out=bscr[:, 2:3], in_=bscr[:, 2:3]), gg)
            pl.op("vector", lambda e: e.tensor_copy(out=bscr[:, 1:2], in_=bscr[:, 1:2]), va)
            marks = [(aa, aa.n), (va, va.n), (ta, ta.n), (gg, gg.n), (bs, bs.n)]
            for eng in ("sync", "scalar", "vector", "tensor", "gpsimd"):
                for (c, t) in marks:
                    pl.wait(eng, c, t)

        with contextlib.ExitStack() as ctx0:
            stage0 = ctx0.enter_context(nc.sbuf_tensor("stage0", [128, 2560], F32))
            t1 = pl.dma("sync", sA, stage0[:, 0:128], ident_in[:])
            pl.wait("vector", sA, t1)
            pl.op("vector", lambda e: e.tensor_copy(out=ident_bf[:], in_=stage0[:, 0:128]), va)
            for d in range(2):
                for mt in range(16):
                    pl.dma("sync", sM, convw_c[d][:, mt * KC:(mt + 1) * KC], conv_wi[d][mt])
                pl.dma("sync", sM, convb_c[d][:], conv_bi[d].transpose([1, 0]), slow_ok=True)
                pl.dma("sync", sM, dtb_c[d][:], dt_b4[d].transpose([1, 0]), slow_ok=True)
                pl.dma("sync", sM, d_c[d][:], d_col4[d].transpose([1, 0]), slow_ok=True)
                pl.wait("sync", aa)
                pl.wait("sync", va)
                for ec in range(NEC):
                    t1 = pl.dma("sync", sA, stage0[:, 1024 + ec * N:1024 + (ec + 1) * N],
                                a_log4[d][ec])
                pl.wait("scalar", sA, t1)
                pl.op("scalar", lambda e, d=d: e.activation(
                    out=a_dev[d][:], in_=stage0[:, 1024:1024 + NEC * N], func=AF.Exp), aa)
                pl.wait("vector", aa)
                pl.op("vector", lambda e, d=d: e.tensor_scalar(
                    out=a_dev[d][:], in0=a_dev[d][:], scalar1=-1.0, scalar2=None,
                    op0=ALU.mult), va)
                for k in range(16):
                    pl.wait("sync", va)
                    t1 = pl.dma("sync", sA, stage0[:, 0:192],
                                w_xpTi[d][k * 128:(k + 1) * 128, :])
                    pl.wait("vector", sA, t1)
                    pl.op("vector", lambda e, d=d, k=k: e.tensor_copy(
                        out=w_xpT_bf[d][:, k * 192:(k + 1) * 192], in_=stage0[:, 0:192]), va)
                pl.wait("sync", va)
                t1 = pl.dma("sync", sA, stage0[0:64, 0:EQ], w_dtTi[d][:])
                pl.wait("vector", sA, t1)
                pl.op("vector", lambda e, d=d: e.tensor_copy(
                    out=w_dtT_bf[d][:], in_=stage0[0:64, 0:EQ]), va)
                for ec in range(NEC):
                    pl.wait("sync", va)
                    t1 = pl.dma("sync", sA, stage0[:, 0:D],
                                w_outTi[d][ec * 128:(ec + 1) * 128, :])
                    pl.wait("vector", sA, t1)
                    pl.op("vector", lambda e, d=d, ec=ec: e.tensor_copy(
                        out=w_outT_bf[d][:, ec * D:(ec + 1) * D], in_=stage0[:, 0:D]), va)
            pl.dma("sync", sM, w1_c[:], ln1_w8.transpose([1, 0]), slow_ok=True)
            pl.wait("sync", va)
            t1 = pl.dma("sync", sA, stage0[:, 0:8], ln1_b8.transpose([1, 0]), slow_ok=True)
            pl.wait("vector", sA, t1)
            pl.op("vector", lambda e: e.tensor_copy(out=b1_cbf[:], in_=stage0[:, 0:8]), va)
            for d in range(2):
                for ec in range(NEC):
                    pl.op("vector", lambda e, d=d, ec=ec: e.memset(carry[d][ec][:], 0.0), va)
            barrier()

        with contextlib.ExitStack() as ctxA:
            sbA = lambda nm, shp, dt: ctxA.enter_context(nc.sbuf_tensor(nm, shp, dt))
            stage = sbA("stageA", [128, 2560], F32)
            w_in_buf = sbA("w_in_buf", [128, 8 * 2560], BF16)
            xt = [sbA(f"xt{i}", [128, D], F32) for i in range(2)]
            ln_nm = ("s1", "mu", "s2", "ex2", "mu2", "var", "av", "ra", "v2",
                     "bv", "la", "lb", "lab", "rr", "rt")
            ln = {nm: sbA(f"ln_{nm}", [128, 4], F32) for nm in ln_nm}
            xr2_bf = [sbA(f"xr2bf{i}", [128, D], BF16) for i in range(2)]
            xnp_bf = [sbA(f"xnpbf{i}", [128, D], BF16) for i in range(2)]
            xr2_f = sbA("xr2f", [128, D], F32)
            xnfm = [sbA(f"xnfm{dc}", [128, TCA], BF16) for dc in range(8)]
            xnfm_p = [sbA(f"xnfmp{dc}", [128, TCA], BF16) for dc in range(8)]
            xp_buf = [sbA(f"xp{i}", [128, TCA + 3], F32) for i in range(3)]
            cacc = [sbA(f"cacc{i}", [128, TCA], F32) for i in range(2)]
            stash = [sbA(f"stash{mt}", [128, 3], F32) for mt in range(16)]
            xc_all = [sbA(f"xc{mt}", [128, TCA], BF16) for mt in range(16)]
            dtlo_bf = sbA("dtlo_bf", [64, TCA], BF16)
            bc_bf = sbA("bc_bf", [64, TCA], BF16)
            dt_t = [sbA(f"dt_t{i}", [128, TCA], F32) for i in range(2)]
            w_t = sbA("w_tile", [128, TCA], BF16)
            sz_t = sbA("sz_tile", [128, TCA], BF16)
            sqd = stage[:, 0:1024]
            e1_t = stage[:, 1024:1536]

            lg = {"xt_v": [0, 0], "xt_a": [0, 0], "slot_a": [0] * 4, "slot_v": [0] * 4,
                  "xr2f_sp": 0, "xbf_ta": [0, 0], "xnfm_sp": 0, "xnfm_ta": 0,
                  "xp_v": [0, 0, 0], "xp_tap_sp": 0, "cacc_a": 0,
                  "xc_sp": [0] * 4, "xc_ta": 0, "dtlo_ta": 0, "bc_sp": 0,
                  "dt_sp": [0, 0], "dt_v": [0, 0], "w_sp": 0, "sz_sp": 0}

            if "A" in PHASES:
                for d in range(2):
                    pl.wait("sync", aa)
                    pl.wait("sync", va)
                    pl.wait("sync", ta)
                    pl.wait("vector", ta)
                    pl.wait("vector", aa)
                    for k in range(8):
                        pl.wait("sync", va)
                        t1 = pl.dma("sync", sA, stage[:, 0:2560],
                                    w_inT[d][k * 128:(k + 1) * 128, :])
                        pl.wait("vector", sA, t1)
                        pl.op("vector", lambda e, k=k: e.tensor_copy(
                            out=w_in_buf[:, k * 2560:(k + 1) * 2560],
                            in_=stage[:, 0:2560]), va)
                    pl.wait("tensor", va)
                    for mt in range(20):
                        bi = grab_bank()
                        for k in range(8):
                            pl.op("tensor", lambda e, mt=mt, k=k, bi=bi: e.matmul(
                                bank_ap[bi][:, 0:1],
                                w_in_buf[:, k * 2560 + mt * 128:k * 2560 + (mt + 1) * 128],
                                b1_cbf[:, k:k + 1], start=(k == 0), stop=(k == 7)),
                                ta if k == 7 else None)
                        bank_stop[bi] = ta.n
                        pl.wait("scalar", ta, bank_stop[bi])
                        pl.op("scalar", lambda e, mt=mt, bi=bi, d=d: e.copy(
                            out=wb1_c[d][:, mt:mt + 1], in_=bank_ap[bi][:, 0:1]), aa)
                        bank_evac[bi] = aa.n
                    wb1_ta = ta.n
                    for k in range(8):
                        pl.wait("sync", va)
                        t1 = pl.dma("sync", sA, stage[:, 0:2560],
                                    w_inT[d][k * 128:(k + 1) * 128, :])
                        pl.wait("vector", sA, t1)
                        if k == 0:
                            pl.wait("vector", ta, wb1_ta)
                        pl.op("vector", lambda e, k=k: e.tensor_scalar(
                            out=w_in_buf[:, k * 2560:(k + 1) * 2560], in0=stage[:, 0:2560],
                            scalar1=w1_c[:, k:k + 1], scalar2=None, op0=ALU.mult), va)
                    w_in_v = va.n
                    pl.wait("tensor", va, w_in_v)
                    for mt in range(16):
                        pl.op("vector", lambda e, mt=mt: e.memset(stash[mt][:], 0.0), va)

                    for tci in range(NTCA):
                        tc = tci if d == 0 else NTCA - 1 - tci
                        t0 = tc * TCA
                        qq, lt = t0 // TSL, t0 % TSL
                        a_fm = aa.n
                        if d == 0:
                            for j in range(8):
                                jj = j % 2
                                slt = j % 4
                                pl.wait("sync", va, lg["xt_v"][jj])
                                pl.wait("sync", aa, lg["xt_a"][jj])
                                tgt_x = pl.dma("sync", sX[jj], xt[jj][:],
                                               x_in[t0 + j * 128:t0 + (j + 1) * 128, :])
                                pl.wait("vector", sX[jj], tgt_x)
                                pl.wait("vector", aa, lg["slot_a"][slt])
                                pl.op("vector", lambda e, jj=jj, slt=slt: e.reduce_sum(
                                    out=ln["s1"][:, slt:slt + 1], in_=xt[jj][:], axis=AXX), va)
                                pl.wait("vector", va)
                                pl.op("vector", lambda e, slt=slt: e.tensor_scalar(
                                    out=ln["mu"][:, slt:slt + 1], in0=ln["s1"][:, slt:slt + 1],
                                    scalar1=1.0 / D, scalar2=None, op0=ALU.mult), va)
                                pl.wait("scalar", sX[jj], tgt_x)
                                pl.wait("scalar", va, lg["slot_v"][slt])
                                pl.op("scalar", lambda e, jj=jj, slt=slt: e.activation(
                                    out=sqd, in_=xt[jj][:], func=AF.Square,
                                    accum_out=ln["s2"][:, slt:slt + 1]), aa)
                                a_s2 = aa.n
                                lg["xt_a"][jj] = a_s2
                                pl.wait("vector", aa, a_s2)
                                pl.op("vector", lambda e, slt=slt: e.tensor_scalar(
                                    out=ln["ex2"][:, slt:slt + 1], in0=ln["s2"][:, slt:slt + 1],
                                    scalar1=1.0 / D, scalar2=None, op0=ALU.mult), None)
                                pl.op("vector", lambda e, slt=slt: e.tensor_mul(
                                    out=ln["mu2"][:, slt:slt + 1], in0=ln["mu"][:, slt:slt + 1],
                                    in1=ln["mu"][:, slt:slt + 1]), va)
                                pl.wait("vector", va)
                                pl.op("vector", lambda e, slt=slt: e.tensor_sub(
                                    out=ln["var"][:, slt:slt + 1], in0=ln["ex2"][:, slt:slt + 1],
                                    in1=ln["mu2"][:, slt:slt + 1]), va)
                                pl.wait("vector", va)
                                pl.op("vector", lambda e, slt=slt: e.tensor_scalar(
                                    out=ln["av"][:, slt:slt + 1], in0=ln["var"][:, slt:slt + 1],
                                    scalar1=EPS, scalar2=None, op0=ALU.add), va)
                                pl.wait("vector", va)
                                pl.op("vector", lambda e, slt=slt: e.reciprocal(
                                    out=ln["ra"][:, slt:slt + 1], in_=ln["av"][:, slt:slt + 1]), va)
                                pl.wait("vector", va)
                                pl.op("vector", lambda e, slt=slt: e.tensor_mul(
                                    out=ln["v2"][:, slt:slt + 1], in0=ln["var"][:, slt:slt + 1],
                                    in1=ln["ra"][:, slt:slt + 1]), va)
                                pl.wait("vector", va)
                                pl.op("vector", lambda e, slt=slt: e.tensor_scalar(
                                    out=ln["bv"][:, slt:slt + 1], in0=ln["v2"][:, slt:slt + 1],
                                    scalar1=EPS, scalar2=None, op0=ALU.add), va)
                                v_bv = va.n
                                pl.wait("scalar", va, v_bv)
                                pl.op("scalar", lambda e, slt=slt: e.activation(
                                    out=ln["la"][:, slt:slt + 1], in_=ln["av"][:, slt:slt + 1],
                                    func=AF.Ln), aa)
                                pl.op("scalar", lambda e, slt=slt: e.activation(
                                    out=ln["lb"][:, slt:slt + 1], in_=ln["bv"][:, slt:slt + 1],
                                    func=AF.Ln), aa)
                                pl.wait("scalar", aa)
                                pl.op("scalar", lambda e, slt=slt: e.activation(
                                    out=ln["rr"][:, slt:slt + 1], in_=ln["la"][:, slt:slt + 1],
                                    func=AF.Exp, scale=-0.5), aa)
                                a_lb = aa.n
                                pl.wait("vector", aa, a_lb)
                                pl.op("vector", lambda e, slt=slt: e.tensor_add(
                                    out=ln["lab"][:, slt:slt + 1], in0=ln["la"][:, slt:slt + 1],
                                    in1=ln["lb"][:, slt:slt + 1]), va)
                                v_lab = va.n
                                pl.wait("scalar", va, v_lab)
                                pl.op("scalar", lambda e, slt=slt: e.activation(
                                    out=ln["rt"][:, slt:slt + 1], in_=ln["lab"][:, slt:slt + 1],
                                    func=AF.Exp, scale=-0.5), aa)
                                a_rt = aa.n
                                lg["slot_a"][slt] = a_rt
                                pl.wait("vector", aa, a_rt)
                                pl.wait("vector", ta, lg["xbf_ta"][jj])
                                pl.wait("vector", pXR, lg["xr2f_sp"])
                                pl.op("vector", lambda e, jj=jj, slt=slt: e.tensor_scalar(
                                    out=xr2_bf[jj][:], in0=xt[jj][:],
                                    scalar1=ln["mu"][:, slt:slt + 1],
                                    scalar2=ln["rt"][:, slt:slt + 1],
                                    op0=ALU.subtract, op1=ALU.mult), None)
                                pl.op("vector", lambda e, jj=jj, slt=slt: e.tensor_scalar(
                                    out=xnp_bf[jj][:], in0=xt[jj][:],
                                    scalar1=ln["mu"][:, slt:slt + 1],
                                    scalar2=ln["rr"][:, slt:slt + 1],
                                    op0=ALU.subtract, op1=ALU.mult), None)
                                pl.op("vector", lambda e, jj=jj, slt=slt: e.tensor_scalar(
                                    out=xr2_f[:], in0=xt[jj][:],
                                    scalar1=ln["mu"][:, slt:slt + 1],
                                    scalar2=ln["rt"][:, slt:slt + 1],
                                    op0=ALU.subtract, op1=ALU.mult), va)
                                v_xn = va.n
                                lg["slot_v"][slt] = v_xn
                                lg["xt_v"][jj] = v_xn
                                pl.wait("scalar", va, v_xn)
                                lg["xr2f_sp"] = pl.dma(
                                    "scalar", pXR,
                                    xr2_sp[qq, lt + j * 128:lt + (j + 1) * 128, :], xr2_f[:])
                                pl.wait("tensor", va, v_xn)
                                for dc in range(8):
                                    bi = grab_bank()
                                    pl.op("tensor", lambda e, jj=jj, dc=dc, bi=bi: e.transpose(
                                        out=bank_ap[bi].bitcast(BF16)[:, 0:128],
                                        in_=xr2_bf[jj][:, dc * 128:(dc + 1) * 128],
                                        identity=ident_bf[:]), None)
                                    pl.op("tensor", lambda e, jj=jj, dc=dc, bi=bi: e.transpose(
                                        out=bank_ap[bi].bitcast(BF16)[:, 128:256],
                                        in_=xnp_bf[jj][:, dc * 128:(dc + 1) * 128],
                                        identity=ident_bf[:]), ta)
                                    bank_stop[bi] = ta.n
                                    if dc == 7:
                                        lg["xbf_ta"][jj] = ta.n
                                    pl.wait("scalar", ta, bank_stop[bi])
                                    if j == 0 and dc == 0:
                                        pl.wait("scalar", ta, lg["xnfm_ta"])
                                        pl.wait("scalar", pNF, lg["xnfm_sp"])
                                    pl.op("scalar", lambda e, dc=dc, j=j, bi=bi: e.copy(
                                        out=xnfm[dc][:, j * 128:(j + 1) * 128],
                                        in_=bank_ap[bi].bitcast(BF16)[:, 0:128]), None)
                                    pl.op("scalar", lambda e, dc=dc, j=j, bi=bi: e.copy(
                                        out=xnfm_p[dc][:, j * 128:(j + 1) * 128],
                                        in_=bank_ap[bi].bitcast(BF16)[:, 128:256]), aa)
                                    bank_evac[bi] = aa.n
                            a_fm = aa.n
                            pl.wait("scalar", aa, a_fm)
                            for dc in range(8):
                                pl.dma("scalar", pNF, xnfm_sp[dc * 128:(dc + 1) * 128,
                                                              t0:t0 + TCA], xnfm[dc][:])
                                s_t = pl.dma("scalar", pNF,
                                             xnpl_sp[qq, dc * 128:(dc + 1) * 128, lt:lt + TCA],
                                             xnfm_p[dc][:])
                            lg["xnfm_sp"] = s_t
                        else:
                            if tci == 0:
                                pl.wait("sync", pNF)
                            pl.wait("sync", ta, lg["xnfm_ta"])
                            nf_t = 0
                            for dc in range(8):
                                nf_t = pl.dma("sync", sNF, xnfm[dc][:],
                                              xnfm_sp[dc * 128:(dc + 1) * 128, t0:t0 + TCA])
                            pl.wait("tensor", sNF, nf_t)

                        if d == 0:
                            pl.wait("tensor", aa, a_fm)
                        for mt in range(20):
                            for f in range(2):
                                bi = grab_bank()
                                for k in range(8):
                                    pl.op("tensor", lambda e, mt=mt, f=f, k=k, bi=bi: e.matmul(
                                        bank_ap[bi],
                                        w_in_buf[:, k * 2560 + mt * 128:k * 2560 + (mt + 1) * 128],
                                        xnfm[k][:, f * 512:(f + 1) * 512],
                                        start=(k == 0), stop=(k == 7)),
                                        ta if k == 7 else None)
                                bank_stop[bi] = ta.n
                                if mt == 19 and f == 1:
                                    lg["xnfm_ta"] = ta.n
                                pl.wait("scalar", ta, bank_stop[bi])
                                if mt < 16:
                                    off = (3 if d == 0 else 0) + f * 512
                                    xb = mt % 3
                                    if f == 0:
                                        pl.wait("scalar", va, lg["xp_v"][xb])
                                        pl.wait("scalar", pTAP, lg["xp_tap_sp"])
                                    pl.op("scalar", lambda e, bi=bi, mt=mt, off=off, xb=xb, d=d: e.add(
                                        out=xp_buf[xb][:, off:off + 512], in_=bank_ap[bi],
                                        add=wb1_c[d][:, mt:mt + 1]), aa)
                                else:
                                    if f == 0:
                                        pl.wait("scalar", pSZ, lg["sz_sp"])
                                    pl.op("scalar", lambda e, bi=bi, mt=mt, f=f, d=d: e.activation(
                                        out=sz_t[:, f * 512:(f + 1) * 512], in_=bank_ap[bi],
                                        func=AF.Silu, bias=wb1_c[d][:, mt:mt + 1]), aa)
                                bank_evac[bi] = aa.n
                            if mt >= 16:
                                a_now = aa.n
                                pl.wait("scalar", aa, a_now)
                                lg["sz_sp"] = pl.dma(
                                    "scalar", pSZ,
                                    sz_sp[d][(mt - 16) * 128:(mt - 15) * 128, t0:t0 + TCA],
                                    sz_t[:])
                            else:
                                xb = mt % 3
                                a_now = aa.n
                                if DEBUG_TAPS and mt < 4:
                                    pl.wait("scalar", aa, a_now)
                                    off3 = 3 if d == 0 else 0
                                    lg["xp_tap_sp"] = pl.dma(
                                        "scalar", pTAP,
                                        xp_tap[d, mt * 128:(mt + 1) * 128, t0:t0 + TCA],
                                        xp_buf[xb][:, off3:off3 + TCA])
                                pl.wait("vector", aa, a_now)
                                pl.wait("vector", aa, lg["cacc_a"])
                                if d == 0:
                                    pl.op("vector", lambda e, xb=xb, mt=mt: e.tensor_copy(
                                        out=xp_buf[xb][:, 0:3], in_=stash[mt][:]), None)
                                    pl.op("vector", lambda e, xb=xb, mt=mt: e.tensor_copy(
                                        out=stash[mt][:], in_=xp_buf[xb][:, TCA:TCA + 3]), None)
                                else:
                                    pl.op("vector", lambda e, xb=xb, mt=mt: e.tensor_copy(
                                        out=xp_buf[xb][:, TCA:TCA + 3], in_=stash[mt][:]), None)
                                    pl.op("vector", lambda e, xb=xb, mt=mt: e.tensor_copy(
                                        out=stash[mt][:], in_=xp_buf[xb][:, 0:3]), None)
                                tgt_cv = 0
                                for k in range(KC):
                                    coff = k if d == 0 else 3 - k
                                    if k == 0:
                                        pl.op("vector", lambda e, xb=xb, coff=coff, d=d, mt=mt: e.tensor_scalar(
                                            out=cacc[0][:], in0=xp_buf[xb][:, coff:coff + TCA],
                                            scalar1=convw_c[d][:, mt * KC:mt * KC + 1],
                                            scalar2=None, op0=ALU.mult), None)
                                    else:
                                        tgt_cv = pl.op("vector", lambda e, xb=xb, coff=coff, d=d, mt=mt, k=k: e.scalar_tensor_tensor(
                                            out=cacc[k % 2][:],
                                            in0=xp_buf[xb][:, coff:coff + TCA],
                                            scalar=convw_c[d][:, mt * KC + k:mt * KC + k + 1],
                                            in1=cacc[(k + 1) % 2][:],
                                            op0=ALU.mult, op1=ALU.add),
                                            va if k == KC - 1 else None)
                                lg["xp_v"][xb] = tgt_cv
                                pl.wait("scalar", va, tgt_cv)
                                if mt < 4:
                                    pl.wait("scalar", pXC, lg["xc_sp"][mt])
                                pl.wait("scalar", ta, lg["xc_ta"])
                                pl.op("scalar", lambda e, mt=mt, d=d: e.activation(
                                    out=xc_all[mt][:], in_=cacc[1][:], func=AF.Silu,
                                    bias=convb_c[d][:, mt:mt + 1]), aa)
                                lg["cacc_a"] = aa.n
                                if mt < 4:
                                    a_now = aa.n
                                    pl.wait("scalar", aa, a_now)
                                    lg["xc_sp"][mt] = pl.dma(
                                        "scalar", pXC,
                                        xc_sp[d][mt * 128:(mt + 1) * 128, t0:t0 + TCA],
                                        xc_all[mt][:])
                        a_xc = aa.n
                        pl.wait("tensor", aa, a_xc)
                        a_dtlo = 0
                        for mt2 in range(2):
                            rows = 128 if mt2 == 0 else 64
                            for f in range(2):
                                bi = grab_bank()
                                for k in range(16):
                                    pl.op("tensor", lambda e, mt2=mt2, f=f, k=k, bi=bi, d=d, rows=rows: e.matmul(
                                        bank_ap[bi][0:rows, :],
                                        w_xpT_bf[d][:, k * 192 + mt2 * 128:k * 192 + mt2 * 128 + rows],
                                        xc_all[k][:, f * 512:(f + 1) * 512],
                                        start=(k == 0), stop=(k == 15)),
                                        ta if k == 15 else None)
                                bank_stop[bi] = ta.n
                                if mt2 == 1 and f == 1:
                                    lg["xc_ta"] = ta.n
                                pl.wait("scalar", ta, bank_stop[bi])
                                if f == 0:
                                    pl.wait("scalar", pBC, lg["bc_sp"])
                                    if mt2 == 0:
                                        pl.wait("scalar", ta, lg["dtlo_ta"])
                                if mt2 == 0:
                                    pl.op("scalar", lambda e, bi=bi, f=f: e.copy(
                                        out=dtlo_bf[:, f * 512:(f + 1) * 512],
                                        in_=bank_ap[bi][0:64, :]), None)
                                    pl.op("scalar", lambda e, bi=bi, f=f: e.copy(
                                        out=bc_bf[:, f * 512:(f + 1) * 512],
                                        in_=bank_ap[bi][64:128, :]), aa)
                                else:
                                    pl.op("scalar", lambda e, bi=bi, f=f: e.copy(
                                        out=bc_bf[:, f * 512:(f + 1) * 512],
                                        in_=bank_ap[bi][0:64, :]), aa)
                                bank_evac[bi] = aa.n
                            a_now = aa.n
                            if mt2 == 0:
                                a_dtlo = a_now
                            pl.wait("scalar", aa, a_now)
                            lg["bc_sp"] = pl.dma(
                                "scalar", pBC,
                                (b_sp[d] if mt2 == 0 else c_sp[d])[:, t0:t0 + TCA], bc_bf[:])
                        pl.wait("tensor", aa, a_dtlo)
                        for mt in range(NEC):
                            mb = mt % 2
                            for f in range(2):
                                bi = grab_bank()
                                pl.op("tensor", lambda e, mt=mt, f=f, bi=bi, d=d: e.matmul(
                                    bank_ap[bi],
                                    w_dtT_bf[d][:, mt * 128:(mt + 1) * 128],
                                    dtlo_bf[:, f * 512:(f + 1) * 512],
                                    start=True, stop=True), ta)
                                bank_stop[bi] = ta.n
                                if mt == NEC - 1 and f == 1:
                                    lg["dtlo_ta"] = ta.n
                                pl.wait("scalar", ta, bank_stop[bi])
                                if f == 0:
                                    pl.wait("scalar", pDT, lg["dt_sp"][mb])
                                    pl.wait("scalar", va, lg["dt_v"][mb])
                                pl.op("scalar", lambda e, bi=bi, d=d, mt=mt: e.activation(
                                    out=e1_t, in_=bank_ap[bi], func=AF.Exp,
                                    bias=dtb_c[d][:, mt:mt + 1]), None)
                                pl.op("scalar", lambda e, bi=bi, mb=mb, f=f: e.activation(
                                    out=dt_t[mb][:, f * 512:(f + 1) * 512], in_=e1_t,
                                    func=AF.Ln, bias=1.0), aa)
                                bank_evac[bi] = aa.n
                            a_dt = aa.n
                            pl.wait("scalar", aa, a_dt)
                            lg["dt_sp"][mb] = pl.dma(
                                "scalar", pDT, dt_sp[d][mt * 128:(mt + 1) * 128, t0:t0 + TCA],
                                dt_t[mb][:])
                            pl.wait("vector", aa, a_dt)
                            pl.wait("vector", pW, lg["w_sp"])
                            tgt_w = pl.op("vector", lambda e, mb=mb, mt=mt: e.tensor_mul(
                                out=w_t[:], in0=dt_t[mb][:], in1=xc_all[mt][:]), va)
                            lg["dt_v"][mb] = tgt_w
                            pl.wait("scalar", va, tgt_w)
                            lg["w_sp"] = pl.dma(
                                "scalar", pW, w_sp[d][mt * 128:(mt + 1) * 128, t0:t0 + TCA],
                                w_t[:])
                    if d == 0:
                        pl.wait("gpsimd", pNF)
                        pl.wait("gpsimd", pXR)
                        pl.op("gpsimd", lambda e: e.collective_compute(
                            "ReduceScatter", ALU.max, replica_groups=G2,
                            ins=[xnpl_sp.opt()], outs=[xnpl_rx.opt()]), c1)
                        pl.op("gpsimd", lambda e: e.collective_compute(
                            "ReduceScatter", ALU.max, replica_groups=G2,
                            ins=[xr2_sp.opt()], outs=[xr2_rx.opt()]), c2)
                cpk_sp_lg = [0, 0]
                for kk in range(128):
                    pl.wait("sync", va)
                    t1 = pl.dma("sync", sA, stage[:, 0:1024],
                                cprojTi[kk * 128:(kk + 1) * 128, :])
                    pl.wait("vector", sA, t1)
                    pl.wait("vector", pCPK, cpk_sp_lg[kk % 2])
                    if kk < 2:
                        pl.wait("vector", ta, lg["xbf_ta"][kk % 2])
                    t_c = pl.op("vector", lambda e, kk=kk: e.tensor_copy(
                        out=xr2_bf[kk % 2][:], in_=stage[:, 0:1024]), va)
                    pl.wait("scalar", va, t_c)
                    cpk_sp_lg[kk % 2] = pl.dma(
                        "scalar", pCPK, cprojbf[kk * 128:(kk + 1) * 128, :],
                        xr2_bf[kk % 2][:])
            barrier()

        with contextlib.ExitStack() as ctxB:
            sbB = lambda nm, shp, dt: ctxB.enter_context(nc.sbuf_tensor(nm, shp, dt))
            dt_s = [[sbB(f"dtS{ec}_{p}", [128, TCB], F32) for p in range(2)]
                    for ec in range(NEC)]
            wv_s = [[sbB(f"wvS{ec}_{p}", [128, TCB], BF16) for p in range(2)]
                    for ec in range(NEC)]
            xcg = [sbB(f"xcg{i}", [128, TCB], BF16) for i in range(2)]
            szg = [sbB(f"szg{i}", [128, TCB], BF16) for i in range(2)]
            brep = [sbB(f"brep{p}", [128, TCB], BF16) for p in range(2)]
            crep = [sbB(f"crep{p}", [128, TCB], BF16) for p in range(2)]
            dA = [[sbB(f"dA{ei}_{p}", [128, TCB], F32) for p in range(2)]
                  for ei in range(2)]
            u_t = [[sbB(f"u{ei}_{p}", [128, TCB], BF16) for p in range(2)]
                   for ei in range(2)]
            h_t = [sbB(f"h{ei}", [128, TCB], BF16) for ei in range(2)]
            q_t = [[sbB(f"q{ei}_{p}", [128, TCB], BF16) for p in range(2)]
                   for ei in range(2)]
            yg = [sbB(f"yg{ec}", [128, TCB], BF16) for ec in range(NEC)]
            ysv = sbB("ysv", [128, TCB], F32)
            part_t = [sbB(f"part{p}", [128, 512], F32) for p in range(2)]

            lgB = {"ld": [[0, 0] for _ in range(NEC)],
                   "cons_v": [[0, 0] for _ in range(NEC)],
                   "cons_a": [[0, 0] for _ in range(NEC)],
                   "xcg_ld": [0, 0], "xcg_v": [0, 0],
                   "brep_v": [0, 0], "crep_v": [0, 0],
                   "u_v": [[0, 0], [0, 0]], "u_g": [[0, 0], [0, 0]],
                   "brep_g": [0, 0],
                   "cons_g": [[0, 0] for _ in range(NEC)],
                   "dA_v": [[0, 0], [0, 0]], "q_ta": [[0, 0], [0, 0]],
                   "ypsum_a": [0, 0], "yg_ta": 0, "part_gp": [0, 0],
                   "pend_cy": [None, None], "cy_a": [0, 0]}
            ECB = TCB // 512

            if "B" in PHASES:
                for d in range(2):
                    for tci in range(NTCB):
                        tc = tci if d == 0 else NTCB - 1 - tci
                        t0 = tc * TCB
                        par = tci % 2
                        for ec in range(NEC):
                            pl.wait("sync", va, lgB["cons_v"][ec][par])
                            pl.wait("sync", aa, lgB["cons_a"][ec][par])
                            pl.wait("sync", va, lgB["cons_g"][ec][par])
                            pl.dma("sync", sDW[par], dt_s[ec][par][:],
                                   dt_sp[d][ec * 128:(ec + 1) * 128, t0:t0 + TCB])
                            tl = pl.dma("sync", sDW[par], wv_s[ec][par][:],
                                        w_sp[d][ec * 128:(ec + 1) * 128, t0:t0 + TCB])
                            lgB["ld"][ec][par] = tl
                        for ecp in range(2):
                            ecs = (2 * ecp, 2 * ecp + 1)
                            yps = (psA, psB)
                            ybanks = ((0, 1), (4, 5))
                            pl.wait("sync", va, lgB["brep_g"][0])
                            g_b_tgts = {0: pl.dma(
                                "sync", gBR[0], brep[0][:],
                                b_sp[d][0:1, t0:t0 + TCB].broadcast_to([128, TCB]))}
                            for n in range(N):
                                p = n % 2
                                if n + 1 < N:
                                    pl.wait("sync", va, lgB["brep_g"][(n + 1) % 2])
                                    g_b_tgts[n + 1] = pl.dma(
                                        "sync", gBR[(n + 1) % 2], brep[(n + 1) % 2][:],
                                        b_sp[d][n + 1:n + 2, t0:t0 + TCB].broadcast_to([128, TCB]))
                                pl.wait("sync", va, lgB["crep_v"][p])
                                g_c = pl.dma("sync", sCR[p], crep[p][:],
                                             c_sp[d][n:n + 1, t0:t0 + TCB].broadcast_to([128, TCB]))
                                pl.wait("gpsimd", gBR[p], g_b_tgts[n])
                                for ei, ec in enumerate(ecs):
                                    pl.wait("scalar", sDW[par], lgB["ld"][ec][par])
                                    pl.wait("scalar", va, lgB["dA_v"][ei][p])
                                    a_dA = pl.op("scalar", lambda e, ei=ei, p=p, ec=ec, par=par, d=d, n=n: e.activation(
                                        out=dA[ei][p][:], in_=dt_s[ec][par][:], func=AF.Exp,
                                        scale=a_dev[d][:, ec * N + n:ec * N + n + 1]), aa)
                                    if n == N - 1:
                                        lgB["cons_a"][ec][par] = a_dA
                                    if lgB["pend_cy"][ei] is not None:
                                        pd, pec, pn, psc = lgB["pend_cy"][ei]
                                        pl.wait("scalar", va, psc)
                                        a_cy = pl.op("scalar", lambda e, pd=pd, pec=pec, pn=pn, ei=ei: e.copy(
                                            out=carry[pd][pec][:, pn:pn + 1],
                                            in_=h_t[ei][:, TCB - 1:TCB]), aa)
                                        lgB["cy_a"][ei] = a_cy
                                    pl.wait("vector", gBR[p], g_b_tgts[n])
                                    pl.wait("vector", sDW[par], lgB["ld"][ec][par])
                                    pl.wait("vector", va, lgB["u_v"][ei][p])
                                    if d == 0:
                                        g_u = pl.op("vector", lambda e, ei=ei, p=p, ec=ec, par=par: e.tensor_mul(
                                            out=u_t[ei][p][:], in0=wv_s[ec][par][:],
                                            in1=brep[p][:]), va)
                                    else:
                                        g_u = pl.op("vector", lambda e, ei=ei, p=p, ec=ec, par=par: e.tensor_mul(
                                            out=u_t[ei][p][:], in0=wv_s[ec][par][:, ::-1],
                                            in1=brep[p][:, ::-1]), va)
                                    lgB["u_g"][ei][p] = g_u
                                    lgB["brep_g"][p] = g_u
                                    if n >= N - 2:
                                        lgB["cons_g"][ec][par] = g_u
                                    pl.wait("vector", aa, max(a_dA, lgB["cy_a"][ei]))
                                    if d == 0:
                                        pl.op("vector", lambda e, ei=ei, p=p, ec=ec, d=d, n=n: e.tensor_tensor_scan(
                                            out=h_t[ei][:], data0=dA[ei][p][:], data1=u_t[ei][p][:],
                                            initial=carry[d][ec][:, n:n + 1],
                                            op0=ALU.mult, op1=ALU.add), va)
                                    else:
                                        pl.op("vector", lambda e, ei=ei, p=p, ec=ec, d=d, n=n: e.tensor_tensor_scan(
                                            out=h_t[ei][:], data0=dA[ei][p][:, ::-1],
                                            data1=u_t[ei][p][:],
                                            initial=carry[d][ec][:, n:n + 1],
                                            op0=ALU.mult, op1=ALU.add), va)
                                    lgB["dA_v"][ei][p] = va.n
                                    lgB["u_v"][ei][p] = va.n
                                    lgB["pend_cy"][ei] = (d, ec, n, va.n)
                                    pl.wait("vector", sCR[p], g_c)
                                    pl.wait("vector", ta, lgB["q_ta"][ei][p])
                                    if d == 0:
                                        t_q = pl.op("vector", lambda e, ei=ei, p=p: e.tensor_mul(
                                            out=q_t[ei][p][:], in0=h_t[ei][:], in1=crep[p][:]), va)
                                    else:
                                        t_q = pl.op("vector", lambda e, ei=ei, p=p: e.tensor_mul(
                                            out=q_t[ei][p][:, ::-1], in0=h_t[ei][:],
                                            in1=crep[p][:, ::-1]), va)
                                    lgB["crep_v"][p] = t_q
                                    pl.wait("tensor", va, t_q)
                                    if n == 0:
                                        pl.wait("tensor", aa, lgB["ypsum_a"][ecp])
                                        for bq in ybanks[ei]:
                                            pl.wait("tensor", aa, bank_evac[bq])
                                    for f in range(ECB):
                                        pl.op("tensor", lambda e, ei=ei, p=p, f=f, n=n: e.matmul(
                                            yps[ei][:, f * 512:(f + 1) * 512], ident_bf[:],
                                            q_t[ei][p][:, f * 512:(f + 1) * 512],
                                            start=(n == 0), stop=(n == N - 1)),
                                            ta if f == ECB - 1 else None)
                                    lgB["q_ta"][ei][p] = ta.n
                            stop_t = ta.n
                            for ei, ec in enumerate(ecs):
                                pl.wait("sync", sG[ei], lgB["xcg_ld"][ei])
                                pl.wait("sync", va, lgB["xcg_v"][ei])
                                pl.dma("sync", sG[ei], xcg[ei][:],
                                       xc_sp[d][ec * 128:(ec + 1) * 128, t0:t0 + TCB])
                                tlg = pl.dma("sync", sG[ei], szg[ei][:],
                                             sz_sp[d][ec * 128:(ec + 1) * 128, t0:t0 + TCB])
                                lgB["xcg_ld"][ei] = tlg
                            prev_yg = 0
                            for ei, ec in enumerate(ecs):
                                pl.wait("scalar", ta, stop_t)
                                if ei == 1:
                                    pl.wait("scalar", va, prev_yg)
                                a_ev = pl.op("scalar", lambda e, ei=ei: e.copy(
                                    out=ysv[:], in_=yps[ei][:, 0:TCB]), aa)
                                lgB["ypsum_a"][ecp] = a_ev
                                for bq in ybanks[ei]:
                                    bank_evac[bq] = a_ev
                                    bank_stop[bq] = stop_t
                                pl.wait("vector", aa, a_ev)
                                pl.wait("vector", sG[ei], lgB["xcg_ld"][ei])
                                pl.wait("vector", ta, lgB["yg_ta"])
                                pl.op("vector", lambda e, ei=ei, ec=ec, d=d: e.scalar_tensor_tensor(
                                    out=ysv[:], in0=xcg[ei][:],
                                    scalar=d_c[d][:, ec:ec + 1], in1=ysv[:],
                                    op0=ALU.mult, op1=ALU.add), None)
                                t_yg = pl.op("vector", lambda e, ei=ei, ec=ec: e.tensor_mul(
                                    out=yg[ec][:], in0=ysv[:], in1=szg[ei][:]), va)
                                prev_yg = t_yg
                                lgB["cons_v"][ec][par] = t_yg
                                lgB["xcg_v"][ei] = t_yg
                        pl.wait("tensor", va, va.n)
                        for mt in range(TCB // 128):
                            for f in range(2):
                                bi = grab_bank()
                                for ec in range(NEC):
                                    pl.op("tensor", lambda e, mt=mt, f=f, ec=ec, bi=bi, d=d: e.matmul(
                                        bank_ap[bi],
                                        yg[ec][:, mt * 128:(mt + 1) * 128],
                                        w_outT_bf[d][:, ec * D + f * 512:ec * D + f * 512 + 512],
                                        start=(ec == 0), stop=(ec == NEC - 1)),
                                        ta if ec == NEC - 1 else None)
                                bank_stop[bi] = ta.n
                                if mt == TCB // 128 - 1 and f == 1:
                                    lgB["yg_ta"] = ta.n
                                pr = (mt * 2 + f) % 2
                                pl.wait("scalar", ta, bank_stop[bi])
                                pl.wait("scalar", gP[pr], lgB["part_gp"][pr])
                                a_pv = pl.op("scalar", lambda e, bi=bi, pr=pr: e.copy(
                                    out=part_t[pr][:], in_=bank_ap[bi]), aa)
                                bank_evac[bi] = a_pv
                                pl.wait("gpsimd", aa, a_pv)
                                lgB["part_gp"][pr] = pl.dma(
                                    "gpsimd", gP[pr],
                                    xbuf[t0 + mt * 128:t0 + (mt + 1) * 128,
                                         f * 512:(f + 1) * 512],
                                    part_t[pr][:], accum=(d == 1))
                pl.wait("gpsimd", gP[0])
                pl.wait("gpsimd", gP[1])
                pl.op("gpsimd", lambda e: e.collective_compute(
                    "ReduceScatter", ALU.add, replica_groups=G2,
                    ins=[xbuf.opt()], outs=[mbuf.opt()]), c3)
            barrier()

        with contextlib.ExitStack() as ctxC:
            sbC = lambda nm, shp, dt: ctxC.enter_context(nc.sbuf_tensor(nm, shp, dt))
            stC = sbC("stageC", [128, 1024], F32)
            rails1 = sbC("rails1", [128, 32 * 128], BF16)
            rails2 = sbC("rails2", [128, 128 * 128], BF16)
            cpk = [sbC(f"cpk{p}", [128, 4 * D], BF16) for p in range(2)]
            xnsl = [sbC(f"xnsl{dc}", [128, 256], BF16) for dc in range(8)]
            y1 = sbC("y1", [128, 32 * 256], BF16)
            h2a = sbC("h2a", [128, 64 * 256], BF16)
            wbc = {nm: sbC(f"bc_{nm}", [128, D], F32) for nm in ("w1", "w2", "bb", "b2")}
            ffn_t = [sbC(f"ffn{jt}", [128, D], F32) for jt in range(2)]
            xtC = [sbC(f"xtC{jt}", [128, D], F32) for jt in range(2)]
            xr2C = [sbC(f"xr2C{jt}", [128, D], F32) for jt in range(2)]
            mbC = [sbC(f"mbC{jt}", [128, D], F32) for jt in range(2)]
            oC = [sbC(f"oC{jt}", [128, D], F32) for jt in range(2)]
            t1C = sbC("t1C", [128, D], F32)
            sqdC = stC[:, 0:1024]
            lnC = {nm: sbC(f"lnC_{nm}", [128, 4], F32)
                   for nm in ("s1", "mu", "s2", "ex2", "mu2", "var", "av", "la", "rr")}

            lgC = {"cpk_ta": [0, 0], "h2_ta": 0,
                   "xnsl_ta": 0, "w1_ta": 0, "out_sp": [0, 0],
                   "slot_a": [0] * 4, "tile_v": [0, 0], "stage_v": 0}

            if "C" in PHASES:
                for i in range(8):
                    pl.wait("sync", va, lgC["stage_v"])
                    t1 = pl.dma("sync", sA, stC[:, 0:512],
                                w_rail1[i * 4:(i + 1) * 4].transpose([1, 0, 2]))
                    pl.wait("vector", sA, t1)
                    lgC["stage_v"] = pl.op("vector", lambda e, i=i: e.tensor_copy(
                        out=rails1[:, i * 512:(i + 1) * 512], in_=stC[:, 0:512]), va)
                for i in range(32):
                    pl.wait("sync", va, lgC["stage_v"])
                    t1 = pl.dma("sync", sA, stC[:, 0:512],
                                w_rail2[i * 4:(i + 1) * 4].transpose([1, 0, 2]))
                    pl.wait("vector", sA, t1)
                    lgC["stage_v"] = pl.op("vector", lambda e, i=i: e.tensor_copy(
                        out=rails2[:, i * 512:(i + 1) * 512], in_=stC[:, 0:512]), va)
                pl.dma("sync", sWB, wbc["w1"][:], ln1_w_row[:].broadcast_to([128, D]))
                pl.dma("sync", sWB, wbc["w2"][:], ln2_w_row[:].broadcast_to([128, D]))
                pl.dma("sync", sWB, wbc["bb"][:], ln1_b_row[:].broadcast_to([128, D]))
                t1 = pl.dma("sync", sWB, wbc["b2"][:], ln2_b_row[:].broadcast_to([128, D]))
                pl.wait("vector", sWB, t1)
                pl.op("vector", lambda e: e.tensor_add(
                    out=wbc["bb"][:], in0=wbc["bb"][:], in1=wbc["b2"][:]), va)
                rails_v = va.n

                for blk in range(TSL // 256):
                    c0 = blk * 256
                    pl.wait("sync", c1, 1)
                    pl.wait("sync", ta, lgC["w1_ta"])
                    xs_t = 0
                    for dc in range(8):
                        xs_t = pl.dma("sync", sXS, xnsl[dc][:],
                                      xnpl_rx[dc * 128:(dc + 1) * 128, c0:c0 + 256])
                    pl.wait("tensor", sXS, xs_t)
                    if blk == 0:
                        pl.wait("tensor", va, rails_v)
                    for h in range(4):
                        for dc in range(8):
                            bi = 4 + (h * 8 + dc) % 4
                            pl.wait("tensor", aa, bank_evac[bi])
                            pl.op("tensor", lambda e, h=h, dc=dc, bi=bi: e.matmul(
                                bank_ap[bi][:, 0:256],
                                rails1[:, (h * 8 + dc) * 128:(h * 8 + dc + 1) * 128],
                                xnsl[dc][:, 0:256], start=True, stop=True), ta)
                            bank_stop[bi] = ta.n
                            pl.wait("scalar", ta, bank_stop[bi])
                            pl.op("scalar", lambda e, h=h, dc=dc, bi=bi: e.copy(
                                out=y1[:, (h * 8 + dc) * 256:(h * 8 + dc + 1) * 256],
                                in_=bank_ap[bi][:, 0:256]), aa)
                            bank_evac[bi] = aa.n
                    lgC["w1_ta"] = ta.n
                    a_y1 = aa.n
                    pl.wait("tensor", aa, a_y1)
                    cp_lds = [0, 0]
                    for half in range(2):
                        kbase = half * 64
                        pl.wait("scalar", ta, lgC["h2_ta"])
                        for ki in range(64):
                            kidx = kbase + ki
                            h, mt = kidx // 32, kidx % 32
                            dc = mt // 4
                            bi = 4 + ki % 4
                            pl.wait("tensor", aa, bank_evac[bi])
                            pl.op("tensor", lambda e, kidx=kidx, h=h, dc=dc, bi=bi: e.matmul(
                                bank_ap[bi][:, 0:256],
                                rails2[:, kidx * 128:(kidx + 1) * 128],
                                y1[:, (h * 8 + dc) * 256:(h * 8 + dc + 1) * 256],
                                start=True, stop=True), ta)
                            bank_stop[bi] = ta.n
                            pl.wait("scalar", ta, bank_stop[bi])
                            a_h2 = pl.op("scalar", lambda e, bi=bi, ki=ki: e.activation(
                                out=h2a[:, ki * 256:(ki + 1) * 256],
                                in_=bank_ap[bi][:, 0:256], func=AF.Gelu), aa)
                            bank_evac[bi] = a_h2
                        a_h2_all = aa.n
                        pl.wait("tensor", aa, a_h2_all)
                        for ki in range(64):
                            kidx = kbase + ki
                            kb = kidx // 4
                            kp = kb % 2
                            ko = kidx % 4
                            if ko == 0:
                                if blk == 0 and kb < 2:
                                    pl.wait("sync", pCPK)
                                pl.wait("sync", ta, lgC["cpk_ta"][kp])
                                cp_lds[kp] = pl.dma(
                                    "sync", sCP[kp], cpk[kp][:],
                                    cprojbf[kb * 512:(kb + 1) * 512, :].rearrange(
                                        "(f p) c -> p f c", p=128))
                            if ko == 0:
                                pl.wait("tensor", sCP[kp], cp_lds[kp])
                            if kidx == 0:
                                pl.wait("tensor", aa, bank_evac[0])
                            for m2 in range(2):
                                for fh in range(2):
                                    pl.op("tensor", lambda e, ko=ko, ki=ki, kidx=kidx, m2=m2, fh=fh, kp=kp: e.matmul(
                                        psA[:, (m2 * 2 + fh) * 512:(m2 * 2 + fh + 1) * 512],
                                        h2a[:, ki * 256 + m2 * 128:ki * 256 + (m2 + 1) * 128],
                                        cpk[kp][:, ko * D + fh * 512:ko * D + fh * 512 + 512],
                                        start=(kidx == 0), stop=(kidx == 127)),
                                        ta if (m2 == 1 and fh == 1) else None)
                            if ko == 3:
                                lgC["cpk_ta"][kp] = ta.n
                        lgC["h2_ta"] = ta.n
                    lgC["xnsl_ta"] = ta.n
                    stop_cp = ta.n
                    pl.wait("scalar", ta, stop_cp)
                    for jt in range(2):
                        pl.op("scalar", lambda e, jt=jt: e.copy(
                            out=ffn_t[jt][:], in_=psA[:, jt * 1024:(jt + 1) * 1024]), aa)
                    a_ffn = aa.n
                    for i in range(4):
                        bank_evac[i] = a_ffn
                        bank_stop[i] = stop_cp
                    for jt in range(2):
                        j = blk * 2 + jt
                        slt = j % 4
                        tok = j * 128
                        pl.wait("gpsimd", va, lgC["tile_v"][jt])
                        pl.dma("gpsimd", sF[jt], xtC[jt][:], x_sl[tok:tok + 128, :])
                        pl.wait("gpsimd", c2, 1)
                        pl.dma("gpsimd", sF[jt], xr2C[jt][:], xr2_rx[tok:tok + 128, :])
                        pl.wait("gpsimd", c3, 1)
                        tl = pl.dma("gpsimd", sF[jt], mbC[jt][:], mbuf[tok:tok + 128, :])
                        pl.wait("scalar", sF[jt], tl)
                        pl.wait("scalar", va, lgC["tile_v"][jt])
                        pl.op("scalar", lambda e, jt=jt, slt=slt: e.activation(
                            out=sqdC, in_=mbC[jt][:], func=AF.Square,
                            accum_out=lnC["s2"][:, slt:slt + 1]), aa)
                        a_s2 = aa.n
                        pl.wait("vector", sF[jt], tl)
                        pl.wait("vector", aa, lgC["slot_a"][slt])
                        pl.op("vector", lambda e, jt=jt, slt=slt: e.reduce_sum(
                            out=lnC["s1"][:, slt:slt + 1], in_=mbC[jt][:], axis=AXX), va)
                        pl.wait("vector", va)
                        pl.op("vector", lambda e, slt=slt: e.tensor_scalar(
                            out=lnC["mu"][:, slt:slt + 1], in0=lnC["s1"][:, slt:slt + 1],
                            scalar1=1.0 / D, scalar2=None, op0=ALU.mult), None)
                        pl.wait("vector", aa, a_s2)
                        pl.op("vector", lambda e, slt=slt: e.tensor_scalar(
                            out=lnC["ex2"][:, slt:slt + 1], in0=lnC["s2"][:, slt:slt + 1],
                            scalar1=1.0 / D, scalar2=None, op0=ALU.mult), None)
                        pl.op("vector", lambda e, slt=slt: e.tensor_mul(
                            out=lnC["mu2"][:, slt:slt + 1], in0=lnC["mu"][:, slt:slt + 1],
                            in1=lnC["mu"][:, slt:slt + 1]), va)
                        pl.wait("vector", va)
                        pl.op("vector", lambda e, slt=slt: e.tensor_sub(
                            out=lnC["var"][:, slt:slt + 1], in0=lnC["ex2"][:, slt:slt + 1],
                            in1=lnC["mu2"][:, slt:slt + 1]), va)
                        pl.wait("vector", va)
                        pl.op("vector", lambda e, slt=slt: e.tensor_scalar(
                            out=lnC["av"][:, slt:slt + 1], in0=lnC["var"][:, slt:slt + 1],
                            scalar1=EPS, scalar2=None, op0=ALU.add), va)
                        v_av = va.n
                        pl.wait("scalar", va, v_av)
                        pl.op("scalar", lambda e, slt=slt: e.activation(
                            out=lnC["la"][:, slt:slt + 1], in_=lnC["av"][:, slt:slt + 1],
                            func=AF.Ln), aa)
                        pl.wait("scalar", aa)
                        a_r = pl.op("scalar", lambda e, slt=slt: e.activation(
                            out=lnC["rr"][:, slt:slt + 1], in_=lnC["la"][:, slt:slt + 1],
                            func=AF.Exp, scale=-0.5), aa)
                        lgC["slot_a"][slt] = a_r
                        pl.wait("vector", aa, a_r)
                        pl.wait("vector", pOUT[jt], lgC["out_sp"][jt])
                        pl.op("vector", lambda e, jt=jt, slt=slt: e.tensor_scalar(
                            out=mbC[jt][:], in0=mbC[jt][:],
                            scalar1=lnC["mu"][:, slt:slt + 1],
                            scalar2=lnC["rr"][:, slt:slt + 1],
                            op0=ALU.subtract, op1=ALU.mult), None)
                        pl.op("vector", lambda e, jt=jt: e.tensor_mul(
                            out=mbC[jt][:], in0=mbC[jt][:], in1=wbc["w2"][:]), None)
                        pl.op("vector", lambda e, jt=jt: e.tensor_mul(
                            out=t1C[:], in0=xr2C[jt][:], in1=wbc["w1"][:]), None)
                        pl.op("vector", lambda e, jt=jt: e.tensor_add(
                            out=oC[jt][:], in0=mbC[jt][:], in1=t1C[:]), None)
                        pl.op("vector", lambda e, jt=jt: e.tensor_add(
                            out=oC[jt][:], in0=oC[jt][:], in1=wbc["bb"][:]), None)
                        pl.op("vector", lambda e, jt=jt: e.tensor_add(
                            out=oC[jt][:], in0=oC[jt][:], in1=xtC[jt][:]), None)
                        pl.wait("vector", aa, a_ffn)
                        t_o = pl.op("vector", lambda e, jt=jt: e.tensor_add(
                            out=oC[jt][:], in0=oC[jt][:], in1=ffn_t[jt][:]), va)
                        lgC["tile_v"][jt] = t_o
                        pl.wait("scalar", va, t_o)
                        lgC["out_sp"][jt] = pl.dma(
                            "scalar", pOUT[jt], out_sl[tok:tok + 128, :], oC[jt][:])
                pl.wait("scalar", pOUT[0])
                pl.wait("scalar", pOUT[1])
                pl.op("scalar", lambda e: e.copy(out=bscr[:, 3:4], in_=bscr[:, 3:4]), aa)
                pl.wait("sync", aa)

        with nc.Block() as block:
            @block.sync
            def _(e):
                for fn in pl.q["sync"]:
                    fn(e)

            @block.scalar
            def _(e):
                for fn in pl.q["scalar"]:
                    fn(e)

            @block.vector
            def _(e):
                for fn in pl.q["vector"]:
                    fn(e)

            @block.tensor
            def _(e):
                for fn in pl.q["tensor"]:
                    fn(e)

            @block.gpsimd
            def _(e):
                for fn in pl.q["gpsimd"]:
                    fn(e)

    return nc


_NC = None


def _get_nc():
    global _NC
    if _NC is None:
        _NC = build_nc()
    return _NC


def _in_map(inputs, b, q):
    g = lambda k: np.asarray(inputs[k])
    x = g("L0_attn_x")[b]
    lo, hi = q * EQ, (q + 1) * EQ
    perm = np.concatenate([np.arange(lo, hi), np.arange(0, lo), np.arange(hi, E)])
    m = {}
    m["x"] = np.ascontiguousarray(x)
    m["x_sl"] = np.ascontiguousarray(x[q * TSL:(q + 1) * TSL])
    for d, pre in enumerate(("L0_attn_m1_", "L0_attn_m2_")):
        W_in = g(pre + "W_in")
        m[f"w_inT{d}"] = np.ascontiguousarray(np.concatenate(
            [W_in[:E][perm].T, W_in[E + lo:E + hi].T], axis=1))
        m[f"conv_w{d}"] = np.ascontiguousarray(g(pre + "conv_w")[perm].reshape(16, 128, KC))
        m[f"conv_b{d}"] = np.ascontiguousarray(g(pre + "conv_b")[perm].reshape(16, 128))
        m[f"w_xpT{d}"] = np.ascontiguousarray(g(pre + "W_xp")[:, perm].T)
        m[f"w_dtT{d}"] = np.ascontiguousarray(g(pre + "W_dt")[lo:hi].T)
        m[f"dt_b{d}"] = np.ascontiguousarray(g(pre + "dt_b")[lo:hi].reshape(NEC, 128))
        m[f"a_log{d}"] = np.ascontiguousarray(g(pre + "A_log")[lo:hi].reshape(NEC, 128, N))
        m[f"d_col{d}"] = np.ascontiguousarray(g(pre + "D")[lo:hi].reshape(NEC, 128))
        m[f"w_outT{d}"] = np.ascontiguousarray(g(pre + "W_out")[:, lo:hi].T)
    m["ln1_w8"] = np.ascontiguousarray(g("L0_attn_ln1_w").reshape(8, 128))
    m["ln1_b8"] = np.ascontiguousarray(g("L0_attn_ln1_b").reshape(8, 128))
    m["ln1_w_row"] = np.ascontiguousarray(g("L0_attn_ln1_w").reshape(1, D))
    m["ln1_b_row"] = np.ascontiguousarray(g("L0_attn_ln1_b").reshape(1, D))
    m["ln2_w_row"] = np.ascontiguousarray(g("L0_attn_ln2_w").reshape(1, D))
    m["ln2_b_row"] = np.ascontiguousarray(g("L0_attn_ln2_b").reshape(1, D))
    rm = g("L0_ffn_rail_maker")
    rup = g("L0_ffn_rail_up_proj")
    W1 = np.zeros((4, 8, 128, 128), np.float32)
    for r in range(64):
        dcc, ri = r // 8, r % 8
        W1[:, dcc, ri * 16:(ri + 1) * 16, ri * 16:(ri + 1) * 16] = rm[:, r]
    m["w_rail1"] = np.ascontiguousarray(W1.reshape(32, 128, 128))
    W2 = np.zeros((4, 32, 128, 128), np.float32)
    for mt in range(32):
        for r2 in range(2):
            r = mt * 2 + r2
            ri = r % 8
            W2[:, mt, ri * 16:(ri + 1) * 16, r2 * 64:(r2 + 1) * 64] = \
                rup[:, r].transpose(0, 2, 1)
    m["w_rail2"] = np.ascontiguousarray(W2.reshape(128, 128, 128))
    m["cprojT"] = np.ascontiguousarray(g("L0_ffn_w_cproj").T)
    m["ident"] = np.eye(128, dtype=np.float32)
    return m


def kernel(**inputs):
    nc = _get_nc()
    in_maps = [_in_map(inputs, c // 4, c % 4) for c in range(8)]
    res = run_bass_kernel_spmd(nc, in_maps, core_ids=list(range(8)))
    out = np.empty((2, T, D), np.float32)
    for c in range(8):
        b, q = c // 4, c % 4
        out[b, q * TSL:(q + 1) * TSL] = res.results[c]["out_sl"]
    return out
